# version 14
# baseline (speedup 1.0000x reference)
import math
import numpy as np
import concourse.bass as bass
import concourse.mybir as mybir
from concourse import library_config

F32 = mybir.dt.float32
BF16 = mybir.dt.bfloat16
I16 = mybir.dt.int16
I32 = mybir.dt.int32
U32 = mybir.dt.uint32
AF = mybir.ActivationFunctionType
ALU = mybir.AluOpType

D = 2048
S = 2048
T = 1024
NH = 32
NKV = 8
DH = 64
NE = 32
DI = 768
CC = 512
NSLOT = NE * CC
NSLOT_PAD = NSLOT + 16
CAP = 4096
K_TOP = 8

TWO_PI_F32 = float(np.float32(2.0 * np.pi))
C1_2PI = float(np.floor(2.0 * np.pi * 4096.0) / 4096.0)
C2_2PI = float(np.float32(2.0 * np.pi - C1_2PI))
PI_F32 = float(np.float32(np.pi))
HALF_PI_F32 = float(np.float32(np.pi / 2))
LN1E4_D32 = math.log(10000.0) / 32.0
C1_FREQ = float(np.float32(-LN1E4_D32))
C2_FREQ = float(np.float32(-LN1E4_D32 - C1_FREQ))
INV2PI = float(np.float32(1.0 / TWO_PI_F32))
PI_DOWN = float(np.nextafter(np.float32(np.pi), np.float32(0.0)))


def qhead_of(tile, j):
    return 16 * (tile // 4) + 4 * j + (tile % 4)


def perm_q_rows():
    perm = np.zeros(D, dtype=np.int64)
    for tile in range(8):
        for j in range(4):
            qh = qhead_of(tile, j)
            for i in range(32):
                perm[tile * 128 + j * 32 + i] = qh * 64 + 2 * i
                perm[1024 + tile * 128 + j * 32 + i] = qh * 64 + 2 * i + 1
    return perm


def perm_k_rows():
    perm = np.zeros(512, dtype=np.int64)
    for k in range(8):
        tile = k // 4
        base = 32 * (k % 4)
        for i in range(32):
            perm[tile * 128 + base + i] = k * 64 + 2 * i
            perm[256 + tile * 128 + base + i] = k * 64 + 2 * i + 1
    return perm


def perm_attn_rows():
    perm = np.zeros(D, dtype=np.int64)
    for tau in range(8):
        for pair in range(2):
            blk = tau * 2 + pair
            for half in range(2):
                qh = qhead_of(tau, 2 * pair + half)
                for dd in range(64):
                    perm[blk * 128 + half * 64 + dd] = qh * 64 + dd
    return perm


class Sem:
    __slots__ = ("h", "n", "name")

    def __init__(self, h, name):
        self.h = h
        self.n = 0
        self.name = name


class Builder:
    ENGS = ("sync", "tensor", "vector", "scalar", "gpsimd")

    def __init__(self, nc, es):
        self.nc = nc
        self.es = es
        self.q = {e: [] for e in self.ENGS}
        self.sems = {}
        self.track = {}

    def sem(self, name):
        if name not in self.sems:
            self.sems[name] = Sem(self.es.enter_context(self.nc.semaphore(name)), name)
        return self.sems[name]

    def I(self, eng, make, waits=(), inc=None, inck=1):
        wcap = []
        if eng in ("vector", "scalar") and eng in self.track:
            ts, tn = self.track[eng]
            wcap.append((ts.h, tn))
        for w in waits:
            if isinstance(w, tuple):
                s, v = w
                wcap.append((s.h, s.n if v is None else v))
            else:
                wcap.append((w.h, w.n))
        icap = None
        if inc is not None:
            inc.n += inck
            icap = (inc.h, inck)
            if eng in ("vector", "scalar"):
                self.track[eng] = (inc, inc.n)

        def f(e, wcap=tuple(wcap), icap=icap, make=make):
            for h, v in wcap:
                if v > 0:
                    e.wait_ge(h, v)
            try:
                ins = make(e)
            except Exception:
                import traceback
                traceback.print_exc()
                raise
            if icap is not None:
                ins.then_inc(icap[0], icap[1])
        self.q[eng].append(f)

    def raw(self, eng, f):
        self.q[eng].append(f)

    def G(self, make_prep, sem_done, waits=(), inck=16, queue_num=0):
        prep = self.sem("swdge_prep")
        wcap = []
        for w in waits:
            if isinstance(w, tuple):
                s, v = w
                wcap.append((s.h, s.n if v is None else v))
            else:
                wcap.append((w.h, w.n))
        sem_done.n += inck
        prep.n += 1
        pn = prep.n

        def f(e, wcap=tuple(wcap), make_prep=make_prep, h=sem_done.h, ph=prep.h, pn=pn, qn=queue_num):
            for hh, v in wcap:
                if v > 0:
                    e.wait_ge(hh, v)
            make_prep(e, h).then_inc(ph, 1)
            e.wait_ge(ph, pn)
            e.trigger_dma(count=1, queue_num=qn)
        self.q["gpsimd"].append(f)

    def flush_block(self):
        nc = self.nc
        q, self.q = self.q, {e: [] for e in self.ENGS}
        with nc.Block() as block:
            for name in self.ENGS:
                if q[name]:
                    def body(e, fns=tuple(q[name])):
                        for f in fns:
                            f(e)
                    getattr(block, name)(body)


def build(debug=(), cc_shared=True):
    from contextlib import ExitStack
    nc = bass.Bass("TRN2", target_bir_lowering=False, debug=False, num_swdge_queues=2)
    es = ExitStack()
    B = Builder(nc, es)
    I, sem = B.I, B.sem

    xT = nc.dram_tensor("xT", [D, S], F32, kind="ExternalInput").ap()
    xqT = nc.dram_tensor("xqT", [D, T], F32, kind="ExternalInput").ap()
    wq = nc.dram_tensor("wq", [D, D], F32, kind="ExternalInput").ap()
    wkv = nc.dram_tensor("wkv", [D, 1024], F32, kind="ExternalInput").ap()
    wo = nc.dram_tensor("wo", [D, D], F32, kind="ExternalInput").ap()
    bqd = nc.dram_tensor("bq", [128, 16], F32, kind="ExternalInput").ap()
    bkvd = nc.dram_tensor("bkv", [128, 8], F32, kind="ExternalInput").ap()
    bod = nc.dram_tensor("bo", [128, 16], F32, kind="ExternalInput").ap()
    gwd = nc.dram_tensor("gw", [D, NE], F32, kind="ExternalInput").ap()
    wgd = nc.dram_tensor("wg", [NE, 128, 16 * DI], F32, kind="ExternalInput").ap()
    wud = nc.dram_tensor("wu", [NE, 128, 16 * DI], F32, kind="ExternalInput").ap()
    wdd = nc.dram_tensor("wd", [NE, 128, 6 * D], F32, kind="ExternalInput").ap()
    pos0d = nc.dram_tensor("pos0", [1, 1], F32, kind="ExternalInput").ap()
    outd = nc.dram_tensor("out", [T, D], F32, kind="ExternalOutput").ap()

    hbm_xn = nc.dram_tensor("hbm_xn", [T, D], BF16, kind="Internal").ap()
    hbm_ps = nc.dram_tensor("hbm_ps", [1, T * K_TOP], F32, kind="Internal").ap()
    hbm_yi = nc.dram_tensor("hbm_yi", [1, T * K_TOP], F32, kind="Internal").ap()
    hbm_ab = nc.dram_tensor("hbm_ab", [2, T], F32, kind="Internal").ap()
    hbm_roster = nc.dram_tensor("hbm_roster", [NSLOT_PAD, 64], F32, kind="Internal").ap()
    hbm_y = nc.dram_tensor("hbm_y", [NSLOT_PAD, D], BF16, kind="Internal").ap()

    dbg = {}
    if "qkv" in debug:
        dbg["d_q"] = nc.dram_tensor("d_q", [16, 128, T], F32, kind="ExternalOutput").ap()
        dbg["d_k"] = nc.dram_tensor("d_k", [4, 128, S], F32, kind="ExternalOutput").ap()
        dbg["d_v"] = nc.dram_tensor("d_v", [16, 128, 520], F32, kind="ExternalOutput").ap()

    def sb(name, shape, dtype, stack=None):
        return (stack or es).enter_context(nc.sbuf_tensor(name, shape, dtype))
    es_attn = ExitStack()
    es_a = ExitStack()
    es_b = ExitStack()
    psa = lambda name, shape, dtype=F32: es.enter_context(nc.psum_tensor(name, shape, dtype))

    ident = sb("ident", [128, 128], BF16)
    triu = sb("triu", [128, 128], BF16)
    onescol_b = sb("onescol_b", [128, 1], BF16)
    onescol_f = sb("onescol_f", [128, 1], F32)
    onesrow_b = sb("onesrow_b", [1, 128], BF16)
    onesrow_f = sb("onesrow_f", [1, 128], F32)
    bq_t = sb("bq_t", [128, 16], F32)
    bkv_t = sb("bkv_t", [128, 8], F32)
    bo_t = sb("bo_t", [128, 16], F32)
    gwb = sb("gwb", [128, 16, NE], BF16)
    scr = sb("scr", [128, 8], F32)

    p03 = psa("p03", [128, 4, 512])
    p47 = psa("p47", [128, 4, 512])
    p0 = p03[:, 0, :]
    p1 = p03[:, 1, :]
    p2 = p03[:, 2, :]
    p3 = p03[:, 3, :]
    p4 = p47[:, 0, :]
    p5 = p47[:, 1, :]
    p6 = p47[:, 2, :]
    p7 = p47[:, 3, :]
    pbank = [p0, p1, p2, p3, p4, p5, p6, p7]

    attnb = sb("attnb", [128, 16, T], BF16)
    qbf = sb("qbf", [128, 16, T], BF16, es_attn)
    kbf = sb("kbf", [128, 4, S], BF16, es_attn)
    vkm = sb("vkm", [128, 16, 520], BF16, es_attn)
    cosK = sb("cosK", [128, S], F32, es_b)
    sinK = sb("sinK", [128, S], F32, es_b)
    cosQ = sb("cosQ", [128, T], F32, es_b)
    sinQ = sb("sinQ", [128, T], F32, es_b)

    s_ld0 = sem("ld0")
    swd_x = [sem("swdx0"), sem("swdx1")]
    swd_w = [sem("swdw0"), sem("swdw1"), sem("swdw2")]
    swd_m = sem("swdm")
    swd_y = [sem("swdy0"), sem("swdy1")]
    tg = {}
    s_gw = swd_m
    s_v = sem("sv")
    s_sc = sem("ssc")
    s_g = sem("sg")
    s_t = sem("st_t")

    def phase_A():
        I("sync", lambda e: e.dma_start(out=bq_t[:], in_=bqd[:]), inc=s_ld0, inck=16)
        I("sync", lambda e: e.dma_start(out=bkv_t[:], in_=bkvd[:]), inc=s_ld0, inck=16)
        I("sync", lambda e: e.dma_start(out=bo_t[:], in_=bod[:]), inc=s_ld0, inck=16)
        I("sync", lambda e: e.dma_start(out=scr[:, 0:1], in_=bass.AP(pos0d.tensor, 0, [[0, 128], [1, 1]])), inc=s_ld0, inck=16)

        I("gpsimd", lambda e: e.dma_start(out=gwb[:], in_=bass.AP(gwd.tensor, 0, [[NE, 128], [128 * NE, 16], [1, NE]])), inc=swd_m, inck=16)
        tg["gw"] = (swd_m, swd_m.n)

        io_r = sb("io_r", [128, 128], F32, es_a)
        io_c = sb("io_c", [128, 128], F32, es_a)
        tbl = sb("tbl", [128, S], F32, es_a)
        tA = sb("tA", [128, S], F32, es_a)
        tB = sb("tB", [128, S], F32, es_a)
        tC = sb("tC", [128, S], F32, es_a)
        iT = sb("iT", [128, S], I32, es_a)
        freqc = sb("freqc", [128, 1], F32, es_a)
        f1 = sb("f1", [128, 1], F32, es_a)
        pmod = sb("pmod", [128, 1], F32, es_a)
        thK = sb("thK", [128, S], F32, es_a)
        thQ = sb("thQ", [128, T], F32, es_a)

        I("gpsimd", lambda e: e.iota(scr[:, 1:2], [[0, 1]], base=0, channel_multiplier=1, allow_small_or_imprecise_dtypes=True), inc=s_g)
        I("gpsimd", lambda e: e.iota(io_r[:], [[0, 128]], base=0, channel_multiplier=1, allow_small_or_imprecise_dtypes=True), inc=s_g)
        I("gpsimd", lambda e: e.iota(io_c[:], [[1, 128]], base=0, channel_multiplier=0, allow_small_or_imprecise_dtypes=True), inc=s_g)
        I("gpsimd", lambda e: e.iota(tbl[:], [[1, S]], base=0, channel_multiplier=0, allow_small_or_imprecise_dtypes=True), inc=s_g)

        for tile, val in ((onescol_b, 1.0), (onescol_f, 1.0), (onesrow_b, 1.0), (onesrow_f, 1.0)):
            I("vector", lambda e, t=tile, v=val: e.memset(t[:], v), inc=s_v)
        I("vector", lambda e: e.tensor_tensor(ident[:], io_r[:], io_c[:], ALU.is_equal), waits=[(s_g, 3), s_v], inc=s_v)
        I("vector", lambda e: e.tensor_tensor(triu[:], io_r[:], io_c[:], ALU.is_lt), waits=[s_v], inc=s_v)
        I("vector", lambda e: e.tensor_scalar(f1[:], scr[:, 1:2], 1.0 / 32.0, None, ALU.mult), waits=[(s_g, 1), s_v], inc=s_v)
        I("vector", lambda e: e.tensor_copy(iT[:, 0:1], f1[:]), waits=[s_v], inc=s_v)
        I("vector", lambda e: e.tensor_copy(pmod[:], iT[:, 0:1]), waits=[s_v], inc=s_v)
        I("vector", lambda e: e.tensor_tensor(freqc[:], pmod[:], f1[:], ALU.is_gt), waits=[s_v], inc=s_v)
        I("vector", lambda e: e.tensor_tensor(f1[:], freqc[:], pmod[:], ALU.subtract), waits=[s_v], inc=s_v)
        I("vector", lambda e: e.scalar_tensor_tensor(pmod[:], f1[:], 32.0, scr[:, 1:2], ALU.mult, ALU.add), waits=[s_v], inc=s_v)
        I("scalar", lambda e: e.activation(f1[:], pmod[:], AF.Exp, scale=C1_FREQ), waits=[s_v], inc=s_sc)
        I("scalar", lambda e: e.activation(pmod[:], pmod[:], AF.Exp, scale=C2_FREQ), waits=[s_sc], inc=s_sc)
        I("vector", lambda e: e.tensor_tensor(freqc[:], f1[:], pmod[:], ALU.mult), waits=[s_sc, s_v], inc=s_v)
        I("vector", lambda e: e.tensor_scalar(thK[:], tbl[:], freqc[:], None, ALU.mult), waits=[(s_g, 4), s_v], inc=s_v)
        I("vector", lambda e: e.tensor_scalar(thQ[:], tbl[:, 0:T], scr[:, 0:1], None, ALU.add), waits=[(s_ld0, 64), s_v], inc=s_v)
        I("vector", lambda e: e.tensor_scalar(thQ[:], thQ[:], freqc[:], None, ALU.mult), waits=[s_v], inc=s_v)

        def table(theta, shift, ncols, dest):
            a = tA[:, 0:ncols]
            b = tB[:, 0:ncols]
            c = tC[:, 0:ncols]
            it = iT[:, 0:ncols]
            I("vector", lambda e: e.tensor_scalar(a, theta, INV2PI, shift * INV2PI + 0.5, ALU.mult, ALU.add), waits=[s_v, s_sc], inc=s_v)
            I("vector", lambda e: e.tensor_copy(it, a), waits=[s_v], inc=s_v)
            I("vector", lambda e: e.tensor_copy(b, it), waits=[s_v], inc=s_v)
            I("vector", lambda e: e.tensor_tensor(c, b, a, ALU.is_gt), waits=[s_v], inc=s_v)
            I("vector", lambda e: e.tensor_tensor(a, c, b, ALU.subtract), waits=[s_v], inc=s_v)
            I("vector", lambda e: e.scalar_tensor_tensor(b, a, C1_2PI, theta, ALU.mult, ALU.add), waits=[s_v], inc=s_v)
            I("vector", lambda e: e.scalar_tensor_tensor(c, a, C2_2PI, b, ALU.mult, ALU.add), waits=[s_v], inc=s_v)
            I("vector", lambda e: e.tensor_scalar(a, c, shift, PI_DOWN, ALU.add, ALU.min), waits=[s_v], inc=s_v)
            I("vector", lambda e: e.tensor_scalar(a, a, -PI_DOWN, None, ALU.max), waits=[s_v], inc=s_v)
            I("scalar", lambda e: e.activation(dest, a, AF.Sin), waits=[s_v, s_sc], inc=s_sc)

        table(thK[:], 0.0, S, sinK[:])
        table(thK[:], HALF_PI_F32, S, cosK[:])
        table(thQ[:], 0.0, T, sinQ[:])
        table(thQ[:], HALF_PI_F32, T, cosQ[:])

    phase_A()
    B.flush_block()
    es_a.close()

    xbf = [sb("xbf0", [128, 16, 512], BF16, es_b), sb("xbf1", [128, 16, 512], BF16, es_b)]
    wkvb = sb("wkvb", [128, 16, 512], BF16, es_b)
    wqs = [sb("wqs%d" % i, [128, 16, 128], BF16, es_b) for i in range(3)]
    kf32 = sb("kf32", [128, 4, 512], F32, es_b)
    qf32 = kf32
    vfe = sb("vfe", [128, 512], BF16, es_b)
    rta = sb("rta", [128, 512], F32, es_b)
    rtb = sb("rtb", [128, 512], F32, es_b)


    s_kvmm = sem("kvmm")
    s_kevac = sem("kevac")
    s_rope = sem("rope")
    s_tr = sem("tr")
    s_trv = sem("trv")
    s_qmm = sem("qmm")
    s_qevac = sem("qevac")

    def phase_B():
        for qtc in range(2):
            I("gpsimd", lambda e, qtc=qtc: e.dma_start(
                out=xbf[qtc][:],
                in_=bass.AP(xqT.tensor, qtc * 512, [[T, 128], [128 * T, 16], [1, 512]])),
                inc=swd_x[qtc], inck=16)
            tg["xq%d" % qtc] = (swd_x[qtc], swd_x[qtc].n)
        for i2 in range(16):
            pr, half = i2 // 2, i2 % 2
            oc = pr + 8 * half
            w = []
            if i2 >= 3:
                I("gpsimd", lambda e, i2=i2: e.nop(), waits=[(s_qmm, 4 * ((i2 - 3) // 2 + 1))])
                w = [tg["wq%d" % (i2 - 3)]]
            I("gpsimd", lambda e, oc=oc, b=i2 % 3: e.dma_start(
                out=wqs[b][:],
                in_=bass.AP(wq.tensor, oc * 128, [[D, 128], [128 * D, 16], [1, 128]])),
                waits=w, inc=swd_w[i2 % 3], inck=16)
            tg["wq%d" % i2] = (swd_w[i2 % 3], swd_w[i2 % 3].n)
        I("gpsimd", lambda e: e.dma_start(out=wkvb[:], in_=bass.AP(wkv.tensor, 0, [[1024, 128], [128 * 1024, 16], [1, 512]])), waits=[tg["gw"]], inc=swd_m, inck=16)
        tg["wkv"] = (swd_m, swd_m.n)
        for tc in range(4):
            if tc >= 2:
                I("gpsimd", lambda e: e.nop(), waits=[(s_kvmm, (tc - 1) * 4)])
            else:
                I("gpsimd", lambda e: e.nop(), waits=[(s_qmm, 32)])
            I("gpsimd", lambda e, tc=tc: e.dma_start(
                out=xbf[tc % 2][:],
                in_=bass.AP(xT.tensor, tc * 512, [[S, 128], [128 * S, 16], [1, 512]])),
                waits=[tg["xq%d" % (tc % 2)] if tc < 2 else tg["xc%d" % (tc - 2)]],
                inc=swd_x[tc % 2], inck=16)
            tg["xc%d" % tc] = (swd_x[tc % 2], swd_x[tc % 2].n)
        I("gpsimd", lambda e: e.nop(), waits=[(s_kvmm, 16)])
        I("gpsimd", lambda e: e.dma_start(out=wkvb[:], in_=bass.AP(wkv.tensor, 512, [[1024, 128], [128 * 1024, 16], [1, 512]])), waits=[tg["wkv"]], inc=swd_m, inck=16)
        tg["wkv2"] = (swd_m, swd_m.n)
        for tc in range(4):
            if tc >= 2:
                I("gpsimd", lambda e: e.nop(), waits=[(s_kvmm, 16 + (tc - 1) * 4)])
            I("gpsimd", lambda e, tc=tc: e.dma_start(
                out=xbf[tc % 2][:],
                in_=bass.AP(xT.tensor, tc * 512, [[S, 128], [128 * S, 16], [1, 512]])),
                waits=[tg["xc%d" % (tc + 2)] if tc < 2 else tg["xc2%d" % (tc - 2)]],
                inc=swd_x[tc % 2], inck=16)
            tg["xc2%d" % tc] = (swd_x[tc % 2], swd_x[tc % 2].n)

        qps = [p0, p1, p2, p3]
        for pr in range(8):
            for half in range(2):
                for qtc in range(2):
                    pt = qps[half * 2 + qtc]
                    for hc in range(16):
                        w = []
                        if hc == 0:
                            i2 = 2 * pr + half
                            w = [tg["xq%d" % qtc], tg["wq%d" % i2]]
                            if pr > 0:
                                w.append((s_qevac, (pr - 1) * 4 + half * 2 + qtc + 1))
                        I("tensor", lambda e, pr=pr, half=half, qtc=qtc, hc=hc, pt=pt: e.matmul(
                            pt[:], wqs[(2 * pr + half) % 3][:, hc, :], xbf[qtc][:, hc, :],
                            start=(hc == 0), stop=(hc == 15)),
                            waits=w, inc=(s_qmm if hc == 15 else None))
            for half in range(2):
                for qtc in range(2):
                    pt = qps[half * 2 + qtc]
                    w = [(s_qmm, pr * 4 + half * 2 + qtc + 1)]
                    if pr > 0:
                        w.append((s_rope, pr * 2))
                    I("scalar", lambda e, pr=pr, half=half, qtc=qtc, pt=pt: e.activation(
                        qf32[:, half * 2 + qtc, :], pt[:], AF.Identity,
                        bias=bq_t[:, pr + 8 * half:pr + 8 * half + 1]),
                        waits=w + [s_sc, (s_ld0, 64)], inc=s_qevac)
            for qtc in range(2):
                sl = slice(qtc * 512, (qtc + 1) * 512)
                x1 = qf32[:, qtc, :]
                x2 = qf32[:, 2 + qtc, :]
                w = [(s_qevac, pr * 4 + 4)]
                I("vector", lambda e, x1=x1, sl=sl: e.tensor_tensor(rta[:], x1, cosQ[:, sl], ALU.mult), waits=w + [s_v], inc=s_v)
                I("vector", lambda e, x2=x2, sl=sl: e.tensor_tensor(rtb[:], x2, sinQ[:, sl], ALU.mult), waits=[s_v], inc=s_v)
                I("vector", lambda e, pr=pr, sl=sl: e.tensor_tensor(qbf[:, pr, sl], rta[:], rtb[:], ALU.subtract), waits=[s_v], inc=s_v)
                I("vector", lambda e, x1=x1, sl=sl: e.tensor_tensor(rta[:], x1, sinQ[:, sl], ALU.mult), waits=[s_v], inc=s_v)
                I("vector", lambda e, x2=x2, sl=sl: e.tensor_tensor(rtb[:], x2, cosQ[:, sl], ALU.mult), waits=[s_v], inc=s_v)
                I("vector", lambda e, pr=pr, sl=sl: e.tensor_tensor(qbf[:, pr + 8, sl], rta[:], rtb[:], ALU.add), waits=[s_v], inc=s_rope)

        for tc in range(4):
            sl = slice(tc * 512, (tc + 1) * 512)
            for oc in range(4):
                for hc in range(16):
                    w = []
                    if hc == 0:
                        w = [tg["xc%d" % tc], tg["wkv"]]
                        if tc > 0:
                            w.append((s_kevac, tc * 4))
                        else:
                            w.append((s_qevac, 32))
                    I("tensor", lambda e, oc=oc, hc=hc, tc=tc: e.matmul(
                        pbank[oc], wkvb[:, hc, oc * 128:(oc + 1) * 128], xbf[tc % 2][:, hc, :],
                        start=(hc == 0), stop=(hc == 15)),
                        waits=w, inc=(s_kvmm if hc == 15 else None))
            for oc in range(4):
                w = [(s_kvmm, tc * 4 + oc + 1), (s_ld0, 64), (s_rope, 16 + tc * 2)]
                I("scalar", lambda e, oc=oc: e.activation(
                    kf32[:, oc, :], pbank[oc], AF.Identity, bias=bkv_t[:, oc:oc + 1]),
                    waits=w + [s_sc], inc=s_kevac)
            for a, b2 in ((0, 2), (1, 3)):
                w = [(s_kevac, tc * 4 + 4)]
                I("vector", lambda e, a=a, sl=sl: e.tensor_tensor(rta[:], kf32[:, a, :], cosK[:, sl], ALU.mult), waits=w + [s_v], inc=s_v)
                I("vector", lambda e, b2=b2, sl=sl: e.tensor_tensor(rtb[:], kf32[:, b2, :], sinK[:, sl], ALU.mult), waits=[s_v], inc=s_v)
                I("vector", lambda e, a=a, sl=sl: e.tensor_tensor(kbf[:, a, sl], rta[:], rtb[:], ALU.subtract), waits=[s_v], inc=s_v)
                I("vector", lambda e, a=a, sl=sl: e.tensor_tensor(rta[:], kf32[:, a, :], sinK[:, sl], ALU.mult), waits=[s_v], inc=s_v)
                I("vector", lambda e, b2=b2, sl=sl: e.tensor_tensor(rtb[:], kf32[:, b2, :], cosK[:, sl], ALU.mult), waits=[s_v], inc=s_v)
                I("vector", lambda e, b2=b2, sl=sl: e.tensor_tensor(kbf[:, b2, sl], rta[:], rtb[:], ALU.add), waits=[s_v], inc=s_rope)
        for tc in range(4):
            for ocv in range(4):
                for hc in range(16):
                    w = []
                    if hc == 0:
                        w = [tg["xc2%d" % tc], tg["wkv2"]]
                        if tc == 0:
                            w.append((s_kevac, 16))
                        else:
                            w += [(s_kevac, 16 + tc * 4), (s_trv, tc * 16)]
                    I("tensor", lambda e, ocv=ocv, hc=hc, tc=tc: e.matmul(
                        pbank[ocv], wkvb[:, hc, ocv * 128:(ocv + 1) * 128], xbf[tc % 2][:, hc, :],
                        start=(hc == 0), stop=(hc == 15)),
                        waits=w, inc=(s_kvmm if hc == 15 else None))
                w = [(s_kvmm, 16 + tc * 4 + ocv + 1), (s_ld0, 64), (s_tr, tc * 16 + ocv * 4)]
                I("scalar", lambda e, ocv=ocv: e.activation(
                    vfe[:], pbank[ocv], AF.Identity, bias=bkv_t[:, 4 + ocv:5 + ocv]),
                    waits=w + [s_sc], inc=s_kevac)
                for kts in range(4):
                    kt = tc * 4 + kts
                    nt = tc * 16 + ocv * 4 + kts
                    w = [(s_kevac, 16 + tc * 4 + ocv + 1)]
                    if nt > 0:
                        w.append((s_trv, nt))
                    I("tensor", lambda e, kts=kts: e.transpose(
                        p0[:, 0:64].bitcast(BF16), vfe[:, kts * 128:(kts + 1) * 128], ident[:]),
                        waits=w, inc=s_tr)
                    h0 = 2 * ocv
                    I("vector", lambda e, kt=kt, h0=h0: e.tensor_copy(
                        vkm[:, kt, h0 * 65:h0 * 65 + 64], p0[:, 0:32].bitcast(BF16)),
                        waits=[(s_tr, tc * 16 + ocv * 4 + kts + 1), s_v], inc=s_v)
                    I("vector", lambda e, kt=kt, h0=h0: e.tensor_copy(
                        vkm[:, kt, (h0 + 1) * 65:(h0 + 1) * 65 + 64], p0[:, 32:64].bitcast(BF16)),
                        waits=[s_v], inc=s_trv)
        for h in range(8):
            I("vector", lambda e, h=h: e.memset(vkm[:, :, h * 65 + 64], 1.0), waits=[s_v], inc=s_v)

    phase_B()
    B.flush_block()
    es_b.close()

    if "qkv" in debug:
        s_dq = sem("dq")
        for i in range(16):
            I("gpsimd", lambda e, i=i: e.dma_start(out=dbg["d_q"][i], in_=qbf[:, i, :]), waits=[(s_rope, s_rope.n)], inc=s_dq, inck=16)
        for i in range(4):
            I("gpsimd", lambda e, i=i: e.dma_start(out=dbg["d_k"][i], in_=kbf[:, i, :]), inc=s_dq, inck=16)
        for i in range(16):
            I("gpsimd", lambda e, i=i: e.dma_start(out=dbg["d_v"][i], in_=vkm[:, i, :]), waits=[(s_trv, s_trv.n), (s_v, s_v.n)], inc=s_dq, inck=16)
        I("gpsimd", lambda e: e.nop(), waits=[s_dq])

    es_c = ExitStack()
    pbuf = sb("pbuf", [128, 2, 2048], BF16, es_c)
    seb = [sb("seb0", [128, 512], F32, es_c), sb("seb1", [128, 512], F32, es_c)]
    serow = sb("serow", [128, 512], F32, es_c)
    ones4 = sb("ones4", [128, 64], F32, es_c)
    sacc = sb("sacc", [128, 2048], F32, es_c)
    kbf2 = sb("kbf2", [128, 4, S], BF16, es_c)
    qbf2 = sb("qbf2", [128, 16, T], BF16, es_c)
    io_c2 = sb("io_c2", [128, 128], F32, es_c)
    selm = [sb("selm%d" % i, [128, 128], BF16, es_c) for i in range(4)]
    self32 = sb("self32", [128, 128], F32, es_c)
    rcol = sb("rcol", [128, 6], F32, es_c)
    icol = sb("icol", [128, 1], I32, es_c)

    s_smm = sem("smm")
    s_exp = sem("exp")
    s_acc = sem("acp")
    s_sum = sem("sum")
    s_pv = sem("pv")
    s_rec = sem("rec")
    s_sebc = sem("sebc")
    s_bc = sem("bc")
    s_aev = sem("aev")
    psum_bc = p7

    s_kq2 = sem("kq2")
    s_km = sem("km")

    def phase_C():
        I("vector", lambda e: e.memset(ones4[:], 1.0), inc=s_v)
        I("vector", lambda e: e.memset(p6[:], 0.0), waits=[s_v], inc=s_v)
        I("gpsimd", lambda e: e.iota(io_c2[:], [[1, 128]], base=0, channel_multiplier=0, allow_small_or_imprecise_dtypes=True), inc=s_g)
        for si, (h, jp) in enumerate(((0, 0), (0, 1), (1, 0), (1, 1))):
            c = [rcol[:, k:k + 1] for k in range(6)]
            I("vector", lambda e, jp=jp: e.tensor_scalar(c[0], scr[:, 1:2], float(64 - 64 * jp), 1.0 / 32.0, ALU.add, ALU.mult), waits=[(s_g, s_g.n), s_v], inc=s_v)
            I("vector", lambda e: e.tensor_copy(icol[:], c[0]), waits=[s_v], inc=s_v)
            I("vector", lambda e: e.tensor_copy(c[1], icol[:]), waits=[s_v], inc=s_v)
            I("vector", lambda e: e.tensor_tensor(c[2], c[1], c[0], ALU.is_gt), waits=[s_v], inc=s_v)
            I("vector", lambda e: e.tensor_tensor(c[1], c[1], c[2], ALU.subtract), waits=[s_v], inc=s_v)
            I("vector", lambda e: e.tensor_scalar(c[2], c[1], 1.5, None, ALU.is_gt), waits=[s_v], inc=s_v)
            I("vector", lambda e: e.tensor_scalar(c[3], c[1], 3.5, None, ALU.is_lt), waits=[s_v], inc=s_v)
            I("vector", lambda e: e.tensor_tensor(c[2], c[2], c[3], ALU.mult), waits=[s_v], inc=s_v)
            I("vector", lambda e, jp=jp, h=h: e.tensor_scalar(c[4], scr[:, 1:2], float(64 - 64 * jp + 32 * h - 128), None, ALU.add), waits=[s_v], inc=s_v)
            I("vector", lambda e: e.scalar_tensor_tensor(c[4], c[1], 32.0, c[4], ALU.mult, ALU.add), waits=[s_v], inc=s_v)
            I("vector", lambda e: e.tensor_tensor(c[4], c[4], c[2], ALU.mult), waits=[s_v], inc=s_v)
            I("vector", lambda e: e.tensor_tensor(c[4], c[4], c[2], ALU.add), waits=[s_v], inc=s_v)
            I("vector", lambda e: e.tensor_scalar(c[4], c[4], -1.0, None, ALU.add), waits=[s_v], inc=s_v)
            I("vector", lambda e: e.tensor_scalar(self32[:], io_c2[:], c[4], None, ALU.is_equal), waits=[s_v], inc=s_v)
            I("vector", lambda e, si=si: e.tensor_copy(selm[si][:], self32[:]), waits=[s_v], inc=s_v)
        nk2 = 0
        for g in range(2):
            for jp in range(2):
                for ns in range(4):
                    ksl2 = slice(ns * 512, (ns + 1) * 512)
                    pbk = pbank[nk2 % 4]
                    w = [(s_v, B.sems["sv"].n)]
                    if nk2 >= 4:
                        w.append((s_kq2, nk2 - 3))
                    I("tensor", lambda e, g=g, jp=jp, ksl2=ksl2, pbk=pbk: e.matmul(
                        pbk, selm[jp][:], kbf[:, g, ksl2], start=True, stop=False), waits=w)
                    I("tensor", lambda e, g=g, jp=jp, ksl2=ksl2, pbk=pbk: e.matmul(
                        pbk, selm[2 + jp][:], kbf[:, 2 + g, ksl2], start=False, stop=True), inc=s_km)
                    I("scalar", lambda e, g=g, jp=jp, ksl2=ksl2, pbk=pbk, nk2=nk2: e.activation(
                        kbf2[:, g * 2 + jp, ksl2], pbk, AF.Identity),
                        waits=[(s_km, nk2 + 1), s_sc], inc=s_kq2)
                    nk2 += 1
        for tau8 in range(8):
            for jp in range(2):
                for ns in range(2):
                    qsl2 = slice(ns * 512, (ns + 1) * 512)
                    pbk = pbank[nk2 % 4]
                    w = []
                    if nk2 >= 4:
                        w = [(s_kq2, nk2 - 3)]
                    I("tensor", lambda e, tau8=tau8, jp=jp, qsl2=qsl2, pbk=pbk: e.matmul(
                        pbk, selm[jp][:], qbf[:, tau8, qsl2], start=True, stop=False), waits=w)
                    I("tensor", lambda e, tau8=tau8, jp=jp, qsl2=qsl2, pbk=pbk: e.matmul(
                        pbk, selm[2 + jp][:], qbf[:, 8 + tau8, qsl2], start=False, stop=True), inc=s_km)
                    I("scalar", lambda e, tau8=tau8, jp=jp, qsl2=qsl2, pbk=pbk, nk2=nk2: e.activation(
                        qbf2[:, tau8 * 2 + jp, qsl2], pbk, AF.Identity),
                        waits=[(s_km, nk2 + 1), s_sc], inc=s_kq2)
                    nk2 += 1
        NKQ2 = nk2
        for u in range(16):
            tau, qc = u // 2, u % 2
            qsl = slice(qc * 512, (qc + 1) * 512)
            for kc in range(17):
                if kc < 16:
                    ksl = slice(kc * 128, (kc + 1) * 128)
                    for j in range(4):
                        jm, jp = j % 2, j // 2
                        w = []
                        if j in (0, 2):
                            if kc > 0:
                                w = [(s_exp, u * 32 + (kc - 1) * 2 + j // 2 + 1)]
                            elif j == 0 and u > 0:
                                w = [(s_exp, u * 32)]
                            elif j == 0:
                                w = [(s_kq2, NKQ2)]
                        I("tensor", lambda e, tau=tau, j=j, jm=jm, jp=jp, ksl=ksl, qsl=qsl: e.matmul(
                            pbank[j], kbf2[64 * jm:64 * jm + 64, (tau // 4) * 2 + jp, ksl],
                            qbf2[64 * jm:64 * jm + 64, tau * 2 + jp, qsl],
                            start=True, stop=True, tile_position=(64 * jm, 0)), waits=w, inc=s_smm)
                    for P in range(2):
                        w = [(s_smm, u * 64 + kc * 4 + 2 * (P + 1))]
                        if kc >= 1:
                            w.append((s_acc, u * 32 + (kc - 1) * 2))
                        elif u > 0:
                            w.append((s_acc, u * 32))
                        I("scalar", lambda e, P=P, kc=kc: e.activation(
                            pbuf[:, kc % 2, P * 1024:(P + 1) * 1024],
                            p03[:, 2 * P:2 * P + 2, :], AF.Exp, scale=0.125),
                            waits=w, inc=s_exp)
                    for P in range(2):
                        w = [(s_exp, u * 32 + kc * 2 + P + 1), s_v]
                        if kc == 0 and P == 0:
                            w.append((s_sum, u * 4))
                        if kc == 0:
                            I("vector", lambda e, P=P: e.tensor_copy(
                                sacc[:, P * 1024:(P + 1) * 1024], pbuf[:, 0, P * 1024:(P + 1) * 1024]),
                                waits=w, inc=s_acc)
                        else:
                            I("vector", lambda e, P=P, kc=kc: e.tensor_tensor(
                                sacc[:, P * 1024:(P + 1) * 1024], sacc[:, P * 1024:(P + 1) * 1024],
                                pbuf[:, kc % 2, P * 1024:(P + 1) * 1024], ALU.add),
                                waits=w, inc=s_acc)
                if kc >= 1:
                    kp = kc - 1
                    for pair in range(2):
                        for pi in range(2):
                            j = pair * 2 + pi
                            kv = 4 * (tau // 4) + j
                            w = []
                            if kp == 15 and pi == 0:
                                w.append((s_exp, u * 32 + kp * 2 + pair + 1))
                            if kp == 0 and pi == 0 and pair == 0:
                                w.append((s_aev, max(0, u * 2)))
                            I("tensor", lambda e, pair=pair, pi=pi, j=j, kv=kv, kp=kp: e.matmul(
                                pbank[4 + pair][pi * 64:(pi + 1) * 64, :],
                                vkm[:, kp, kv * 65:kv * 65 + 64],
                                pbuf[:, kp % 2, j * 512:(j + 1) * 512],
                                start=(kp == 0), stop=(kp == 15), tile_position=(0, pi * 64),
                                skip_group_check=True),
                                waits=w,
                                inc=(s_pv if kp == 15 else None))
            for j in range(4):
                w = [(s_acc, (u + 1) * 32)]
                if j == 0:
                    w.append((s_rec, u))
                I("tensor", lambda e, j=j: e.matmul(
                    p6[32 * j:32 * j + 1, :],
                    onescol_f[:], sacc[:, j * 512:(j + 1) * 512],
                    start=True, stop=True, tile_position=(0, 32 * j),
                    skip_group_check=True),
                    waits=w, inc=s_sum)
            w = [(s_sum, (u + 1) * 4), s_v]
            if u > 0:
                w.append((s_bc, u * 2))
            I("vector", lambda e: e.tensor_copy(serow[:], p6[:]), waits=w, inc=s_rec)
            for pair in range(2):
                for pi in range(2):
                    j = pair * 2 + pi
                    w = [(s_rec, u + 1)]
                    if pi == 0 and pair == 0:
                        w.append((s_sebc, u * 2))
                    elif pi == 0:
                        w.append((s_sebc, u * 2 + 1))
                    I("tensor", lambda e, j=j, pi=pi: e.matmul(
                        psum_bc[pi * 64:(pi + 1) * 64, :],
                        ones4[32 * j:32 * j + 1, 0:64],
                        serow[32 * j:32 * j + 1, :],
                        start=True, stop=True, tile_position=(32 * j, pi * 64),
                        skip_group_check=True),
                        waits=w, inc=(s_bc if pi == 1 else None))
                w = [(s_bc, u * 2 + pair + 1)]
                if u > 0:
                    w.append((s_aev, (u - 1) * 2 + pair + 1))
                I("vector", lambda e, pair=pair: e.reciprocal_approx_fast(seb[pair][:], psum_bc[:]),
                    waits=w + [s_v], inc=s_sebc)
                w = [(s_pv, u * 4 + pair * 2 + 2)]
                I("vector", lambda e, tau=tau, pair=pair, qsl=qsl: e.tensor_tensor(
                    attnb[:, tau * 2 + pair, qsl], pbank[4 + pair], seb[pair][:], ALU.mult),
                    waits=w + [s_v], inc=s_aev)

    phase_C()
    B.flush_block()

    if "attn" in debug:
        d_at = nc.dram_tensor("d_at", [16, 128, T], F32, kind="ExternalOutput").ap()
        dbg["d_at"] = d_at
        s_da = sem("da")
        for i in range(16):
            I("gpsimd", lambda e, i=i: e.dma_start(out=d_at[i], in_=attnb[:, i, :]), waits=[(s_aev, 32)], inc=s_da, inck=16)
        I("gpsimd", lambda e: e.nop(), waits=[s_da])
        B.flush_block()
    es_c.close()
    es_attn.close()

    es_d = ExitStack()
    wos = [sb("wos%d" % i, [128, 16, 128], BF16, es_d) for i in range(3)]
    r_t = sb("r_t", [128, 16, T], F32, es_d)
    xr = [sb("xr0", [128, T], F32, es_d), sb("xr1", [128, T], F32, es_d)]
    rsq = [sb("rsq0", [128, 512], F32, es_d), sb("rsq1", [128, 512], F32, es_d)]
    lnrow = sb("lnrow", [1, 4 * T], F32, es_d)
    abB = sb("abB", [128, 2 * T], F32, es_d)
    xnb = attnb
    epsr = sb("epsr", [1, 1], F32, es_d)

    s_xrl = [sem("xrl%d" % i) for i in range(2)]
    s_omm = sem("omm")
    s_resid = sem("resid")
    s_sq = sem("sq")
    s_sqm = sem("sqm")
    s_lnm = sem("lnm")
    s_ab = sem("ab")
    s_xn = sem("xn")

    def phase_D1():
        for oc in range(16):
            w = [tg["wq%d" % (13 + oc % 3)]] if oc < 3 else [tg["wo%d" % (oc - 3)]]
            if oc >= 3:
                w.append((s_omm, (oc - 2) * 2))
            I("gpsimd", lambda e, oc=oc: e.dma_start(
                out=wos[oc % 3][:],
                in_=bass.AP(wo.tensor, oc * 128, [[D, 128], [128 * D, 16], [1, 128]])),
                waits=w, inc=swd_w[oc % 3], inck=16)
            tg["wo%d" % oc] = (swd_w[oc % 3], swd_w[oc % 3].n)
            I("sync", lambda e, oc=oc: e.dma_start(
                out=xr[oc % 2][:],
                in_=bass.AP(xqT.tensor, oc * 128 * T, [[T, 128], [1, T]])),
                waits=([(s_resid, (oc - 1) * 2)] if oc >= 2 else []),
                inc=s_xrl[oc % 2], inck=16)
        for oc in range(16):
            for tc in range(2):
                tsl = slice(tc * 512, (tc + 1) * 512)
                pq = (p0, p1)[tc]
                for fc in range(16):
                    w = []
                    if fc == 0:
                        w = [tg["wo%d" % oc]]
                        if oc > 0:
                            w.append((s_resid, (oc - 1) * 2 + tc + 1))
                    I("tensor", lambda e, oc=oc, fc=fc, tsl=tsl, pq=pq: e.matmul(
                        pq[:], wos[oc % 3][:, fc, :], attnb[:, fc, tsl],
                        start=(fc == 0), stop=(fc == 15)),
                        waits=w, inc=(s_omm if fc == 15 else None))
                I("vector", lambda e, oc=oc, tc=tc, tsl=tsl, pq=pq: e.scalar_tensor_tensor(
                    r_t[:, oc, tsl], pq[:], bo_t[:, oc:oc + 1], xr[oc % 2][:, tsl],
                    ALU.add, ALU.add),
                    waits=[(s_omm, oc * 2 + tc + 1), (s_xrl[oc % 2], 16 * (oc // 2 + 1)), s_v],
                    inc=s_resid)
                I("scalar", lambda e, oc=oc, tc=tc, tsl=tsl: e.activation(
                    rsq[tc][:], r_t[:, oc, tsl], AF.Square),
                    waits=[(s_resid, oc * 2 + tc + 1), (s_sqm, max(0, (oc - 1) * 2 + tc + 1)), s_sc],
                    inc=s_sq)
                pm = (p2, p3)[tc]
                psq = (p4, p5)[tc]
                I("tensor", lambda e, oc=oc, tsl=tsl, pm=pm: e.matmul(
                    pm[0:1, :], onescol_f[:], r_t[:, oc, tsl],
                    start=(oc == 0), stop=(oc == 15)),
                    waits=[(s_resid, oc * 2 + tc + 1)],
                    inc=(s_lnm if oc == 15 else None))
                I("tensor", lambda e, oc=oc, tc=tc, psq=psq: e.matmul(
                    psq[0:1, :], onescol_f[:], rsq[tc][:],
                    start=(oc == 0), stop=(oc == 15)),
                    waits=[(s_sq, oc * 2 + tc + 1)],
                    inc=s_sqm)
        mrow = lnrow[0:1, 0:T]
        sqrow = lnrow[0:1, T:2 * T]
        arow = lnrow[0:1, 2 * T:3 * T]
        brow = lnrow[0:1, 3 * T:4 * T]
        I("vector", lambda e: e.memset(epsr[:], 1e-5), waits=[s_v], inc=s_v)
        I("vector", lambda e: e.tensor_copy(lnrow[0:1, 0:512], p2[0:1, :]), waits=[(s_lnm, 2), (s_sqm, 32), s_v], inc=s_v)
        I("vector", lambda e: e.tensor_copy(lnrow[0:1, 512:1024], p3[0:1, :]), waits=[s_v], inc=s_v)
        I("vector", lambda e: e.tensor_copy(lnrow[0:1, T:T + 512], p4[0:1, :]), waits=[s_v], inc=s_v)
        I("vector", lambda e: e.tensor_copy(lnrow[0:1, T + 512:2 * T], p5[0:1, :]), waits=[s_v], inc=s_v)
        I("vector", lambda e: e.tensor_scalar(mrow, mrow, 1.0 / D, None, ALU.mult), waits=[s_v], inc=s_v)
        I("vector", lambda e: e.tensor_scalar(sqrow, sqrow, 1.0 / D, None, ALU.mult), waits=[s_v], inc=s_v)
        I("vector", lambda e: e.scalar_tensor_tensor(arow, mrow, 1.0, mrow, ALU.mult, ALU.mult), waits=[s_v], inc=s_v)
        I("vector", lambda e: e.tensor_tensor(sqrow, sqrow, arow, ALU.subtract), waits=[s_v], inc=s_v)
        I("scalar", lambda e: e.activation(arow, sqrow, AF.Sqrt, bias=epsr[0:1, 0:1]), waits=[s_v, s_sc], inc=s_sc)
        I("vector", lambda e: e.reciprocal_approx_fast(arow, arow), waits=[s_sc, s_v], inc=s_v)
        I("vector", lambda e: e.scalar_tensor_tensor(brow, mrow, -1.0, arow, ALU.mult, ALU.mult), waits=[s_v], inc=s_v)
        I("vector", lambda e: e.nop(), waits=[s_v], inc=s_ab)
        I("sync", lambda e: e.dma_start(out=hbm_ab[0], in_=arow), waits=[s_ab], inc=s_ab, inck=16)
        I("sync", lambda e: e.dma_start(out=hbm_ab[1], in_=brow), inc=s_ab, inck=16)
        I("sync", lambda e: e.nop(), waits=[s_ab])
        I("sync", lambda e: e.dma_start(out=abB[:], in_=bass.AP(hbm_ab.tensor, 0, [[0, 128], [1, 2 * T]])), inc=s_ab, inck=16)
        for oc in range(16):
            I("vector", lambda e, oc=oc: e.tensor_tensor(xr[0][:], r_t[:, oc, :], abB[:, 0:T], ALU.mult),
                waits=[(s_ab, 49), s_v], inc=s_v)
            I("vector", lambda e, oc=oc: e.tensor_tensor(xnb[:, oc, :], xr[0][:], abB[:, T:2 * T], ALU.add),
                waits=[s_v], inc=s_xn)

    phase_D1()
    B.flush_block()
    if "xn" in debug:
        dbg["d_xn"] = nc.dram_tensor("d_xn", [16, 128, T], F32, kind="ExternalOutput").ap()
        s_dxn = sem("dxn")
        for oc in range(16):
            I("gpsimd", lambda e, oc=oc: e.dma_start(out=dbg["d_xn"][oc], in_=xnb[:, oc, :]), inc=s_dxn, inck=16)
        I("gpsimd", lambda e: e.nop(), waits=[s_dxn])
        B.flush_block()

    es_d.close()
    es_moe = ExitStack()
    gidx = sb("gidx", [128, 32 * NE], I16, es_moe)
    gidxf = sb("gidxf", [128, 32 * NE], F32, es_moe)
    cwsel = sb("cwsel", [128, 8, K_TOP], F32, es_moe)
    epsF = sb("epsF", [128, 1], F32, es_moe)
    yidxw = sb("yidxw", [128, 512], I16, es_moe)
    es_w = ExitStack()
    wring = [sb("wring%d" % i, [128, 12288], BF16, es_w) for i in range(3)]
    es_d2 = ExitStack()
    xnt = [sb("xnt0", [128, D], BF16, es_d2), sb("xnt1", [128, D], BF16, es_d2)]
    zf = sb("zf", [128, 8, NE], F32, es_d2)
    iotaE = sb("iotaE", [128, NE], F32, es_d2)
    ones32 = sb("ones32", [128, NE], F32, es_d2)
    onesT = sb("onesT", [128, 128], BF16, es_d2)
    indb = sb("indb", [128, 8, NE], BF16, es_d2)
    rkm = sb("rkm", [128, 8, NE], F32, es_d2)
    cwu = sb("cwu", [128, 8, NE], F32, es_d2)
    wsumr = sb("wsumr", [128, 8], F32, es_d2)
    pposF = sb("pposF", [128, 8, NE], F32, es_d2)
    ra = sb("ra", [128, 64], F32, es_d2)
    rb = sb("rb", [128, NE], F32, es_d2)
    rc = sb("rc", [128, NE], F32, es_d2)
    identF = sb("identF", [128, 128], F32, es_d2)
    slotT = sb("slotT", [8, 1024], F32, es_d2)
    flb = sb("flb", [128, 16], F32, es_d2)
    cntrow = sb("cntrow", [1, NE], F32, es_d2)
    ccs = sb("ccs", [8, NE], F32, es_d2)
    pidf = sb("pidf", [8, 1], F32, es_d2)
    iota8 = sb("iota8", [8, 1], F32, es_d2)
    mask8 = sb("mask8", [8, 1], F32, es_d2)
    cprow = sb("cprow", [1, NE], F32, es_d2)
    cpB = sb("cpB", [128, NE], F32, es_d2)
    stage = sb("stage", [128, 64, 64], F32, es_d2)
    idxw = sb("idxw", [128, 512], I16, es_d2)
    ztile = sb("ztile", [128, 2050], F32, es_d2)
    pid_u = sb("pid_u", [8, 1], U32, es_d2)
    hbm_cp = nc.dram_tensor("hbm_cp", [1, NE], F32, kind="Internal").ap()
    cc_in = nc.dram_tensor("cc_in", [1, NE], F32, kind="Internal").ap()
    cc_out = nc.dram_tensor("cc_out", [8, NE], F32, kind="Internal", addr_space=("Shared" if cc_shared else "Local")).ap()

    s_z = sem("sz")
    s_zev = sem("zev")
    s_pmm = sem("pmm")
    s_ppe = sem("ppe")
    s_cnt = sem("cnt")
    s_xtr = sem("xtr")
    s_xtv = sem("xtv")
    s_xnw = sem("xnw")
    s_psw = sem("psw")
    s_idr = sem("idr")
    s_cc = sem("scc")
    s_ccd = sem("ccd")
    s_cpd = sem("cpd")
    s_keep = sem("keep")
    s_stg = sem("stg")
    s_sca = sem("sca")
    s_rz = sem("srz")
    s_gx = sem("gxr")
    s_stv = sem("stv")
    s_iev = sem("iev")

    regCC = nc.gpsimd.alloc_register()
    regT = nc.gpsimd.alloc_register()
    reg1k = nc.gpsimd.alloc_register()

    def phase_D2():
        I("gpsimd", lambda e: e.reg_mov(regCC, CC))
        I("gpsimd", lambda e: e.reg_mov(regT, T))
        I("gpsimd", lambda e: e.reg_mov(reg1k, 1024))
        for t, hd in ((0, wgd), (1, wud)):
            I("gpsimd", lambda ee, t=t, hd=hd: ee.dma_start(
                out=wring[t][:], in_=bass.AP(hd.tensor, 0, [[16 * DI, 128], [1, 16 * DI]])),
                waits=[tg["wo%d" % (13 + t)]], inc=swd_w[t], inck=16)
            tg["we0_%d" % t] = (swd_w[t], swd_w[t].n)
        for tt in range(8):
            for oc in range(16):
                w = [(s_xn, oc + 1)]
                nt = tt * 16 + oc
                if nt > 1:
                    w.append((s_xtv, nt - 1))
                pb = (p2, p3)[nt % 2]
                I("tensor", lambda e, tt=tt, oc=oc, pb=pb: e.transpose(
                    pb[:, 0:64].bitcast(BF16), xnb[:, oc, tt * 128:(tt + 1) * 128], ident[:]),
                    waits=w, inc=s_xtr)
                w2 = [(s_xtr, tt * 16 + oc + 1), s_v]
                if tt >= 2 and oc == 0:
                    w2.append((s_xnw, 16 * (tt - 1)))
                I("vector", lambda e, tt=tt, oc=oc, pb=pb: e.tensor_copy(
                    xnt[tt % 2][:, oc * 128:(oc + 1) * 128], pb[:, 0:64].bitcast(BF16)),
                    waits=w2, inc=s_xtv)
            I("sync", lambda e, tt=tt: e.dma_start(
                out=bass.AP(hbm_xn.tensor, tt * 128 * D, [[D, 128], [1, D]]),
                in_=xnt[tt % 2][:]),
                waits=[(s_xtv, (tt + 1) * 16), (s_xnw, 16 * tt)], inc=s_xnw, inck=16)

        I("gpsimd", lambda e: e.iota(iotaE[:], [[1, NE]], base=0, channel_multiplier=0, allow_small_or_imprecise_dtypes=True), inc=s_g)
        I("gpsimd", lambda e: e.iota(iota8[:], [[0, 1]], base=0, channel_multiplier=1, allow_small_or_imprecise_dtypes=True), inc=s_g)
        I("vector", lambda e: e.memset(ones32[:], 1.0), waits=[s_v], inc=s_v)
        I("vector", lambda e: e.memset(onesT[:], 1.0), waits=[s_v], inc=s_v)
        I("vector", lambda e: e.memset(stage[:], 0.0), waits=[s_v], inc=s_stg)
        I("vector", lambda e: e.memset(ztile[:], 0.0), waits=[s_v], inc=s_stg)
        I("vector", lambda e: e.memset(epsF[:], 1e-5), waits=[s_v], inc=s_v)
        I("vector", lambda e: e.tensor_copy(identF[:], ident[:]), waits=[s_v], inc=s_v)

        for tau in range(8):
            tsl = slice(tau * 128, (tau + 1) * 128)
            for hc in range(16):
                w = [(s_xn, hc + 1), (s_gw, 16)]
                if hc == 0 and tau > 0:
                    w.append((s_zev, tau))
                I("tensor", lambda e, hc=hc, tsl=tsl: e.matmul(
                    p6[:, 0:NE], xnb[:, hc, tsl], gwb[:, hc, :],
                    start=(hc == 0), stop=(hc == 15)),
                    waits=w, inc=(s_z if hc == 15 else None))
            I("vector", lambda e, tau=tau: e.tensor_copy(zf[:, tau, :], p6[:, 0:NE]),
                waits=[(s_z, tau + 1), s_v], inc=s_zev)
            I("vector", lambda e, tau=tau: e.tensor_reduce(ra[:, 0:1], zf[:, tau, :], mybir.AxisListType.X, ALU.max), waits=[s_v], inc=s_v)
            I("vector", lambda e: e.tensor_scalar(ra[:, 1:2], ra[:, 0:1], -1.0, None, ALU.mult), waits=[s_v], inc=s_v)
            I("scalar", lambda e, tau=tau: e.activation(rb[:], zf[:, tau, :], AF.Exp, bias=ra[:, 1:2], accum_out=ra[:, 2:3]), waits=[s_v, s_sc], inc=s_sc)
            I("vector", lambda e: e.reciprocal(ra[:, 3:4], ra[:, 2:3]), waits=[s_sc, s_v], inc=s_v)
            I("vector", lambda e: e.tensor_scalar(rb[:], rb[:], ra[:, 3:4], None, ALU.mult), waits=[s_v], inc=s_v)
            I("vector", lambda e: e.max(out=ra[:, 8:16], in_=rb[:]), waits=[s_v], inc=s_v)
            I("vector", lambda e: e.match_replace(out=rc[:], in_to_replace=ra[:, 8:16], in_values=rb[:], imm_value=0.0), waits=[s_v], inc=s_v)
            I("vector", lambda e, tau=tau: e.tensor_tensor(cwu[:, tau, :], rb[:], rc[:], ALU.subtract), waits=[s_v], inc=s_v)
            I("vector", lambda e, tau=tau: e.tensor_reduce(wsumr[:, tau:tau + 1], ra[:, 8:16], mybir.AxisListType.X, ALU.add), waits=[s_v], inc=s_v)
            I("vector", lambda e, tau=tau: e.tensor_scalar(ra[:, 16:48], cwu[:, tau, :], 0.0, None, ALU.is_gt), waits=[s_v], inc=s_v)
            I("vector", lambda e, tau=tau: e.tensor_copy(indb[:, tau, :], ra[:, 16:48]), waits=[s_v], inc=s_v)
            I("vector", lambda e: e.tensor_tensor_scan(rc[:], ones32[:], ra[:, 16:48], 0.0, ALU.mult, ALU.add), waits=[s_v], inc=s_v)
            I("vector", lambda e, tau=tau: e.tensor_tensor(rc[:], rc[:], ra[:, 16:48], ALU.mult), waits=[s_v], inc=s_v)
            I("vector", lambda e, tau=tau: e.tensor_scalar(rkm[:, tau, :], rc[:], -1.0, None, ALU.add), waits=[s_v], inc=s_v)
            for tp in range(tau + 1):
                w = [(s_v, B.sems["sv"].n)]
                if tp == 0 and tau > 0:
                    w.append((s_ppe, tau))
                I("tensor", lambda e, tau=tau, tp=tp: e.matmul(
                    p7[:, 0:NE],
                    (triu if tp == tau else onesT)[:],
                    indb[:, tp, :],
                    start=(tp == 0), stop=(tp == tau)),
                    waits=w, inc=(s_pmm if tp == tau else None))
            I("vector", lambda e, tau=tau: e.tensor_copy(pposF[:, tau, :], p7[:, 0:NE]),
                waits=[(s_pmm, tau + 1), s_v], inc=s_ppe)
            I("tensor", lambda e, tau=tau: e.matmul(
                p5[0:1, 0:NE], onescol_b[:], indb[:, tau, :],
                start=(tau == 0), stop=(tau == 7)),
                waits=[(s_v, B.sems["sv"].n)], inc=(s_cnt if tau == 7 else None))
            I("vector", lambda e, tau=tau: e.scalar_tensor_tensor(ra[:, 16:48], iotaE[:], float(CC), pposF[:, tau, :], ALU.mult, ALU.add), waits=[(s_ppe, tau + 1), (s_g, 6), s_v], inc=s_v)
            I("vector", lambda e, tau=tau: e.tensor_scalar(rc[:], pposF[:, tau, :], float(CC), None, ALU.is_lt), waits=[s_v], inc=s_v)
            I("vector", lambda e: e.tensor_scalar(ra[:, 16:48], ra[:, 16:48], float(-NSLOT), None, ALU.add), waits=[s_v], inc=s_v)
            I("vector", lambda e: e.tensor_tensor(ra[:, 16:48], ra[:, 16:48], rc[:], ALU.mult), waits=[s_v], inc=s_v)
            I("vector", lambda e: e.tensor_scalar(ra[:, 16:48], ra[:, 16:48], float(NSLOT), None, ALU.add), waits=[s_v], inc=s_v)
            for k in range(K_TOP):
                w = [s_v]
                if k == 0:
                    w.append((s_stv, tau))
                I("vector", lambda e, tau=tau, k=k: e.tensor_scalar(rc[:], rkm[:, tau, :], float(k), None, ALU.is_equal), waits=w, inc=s_v)
                I("vector", lambda e: e.tensor_tensor(rb[:], ra[:, 16:48], rc[:], ALU.mult), waits=[s_v], inc=s_v)
                I("vector", lambda e, tau=tau, k=k: e.tensor_reduce(ra[:, 48 + k:49 + k], rb[:], mybir.AxisListType.X, ALU.add), waits=[s_v], inc=s_v)
            I("tensor", lambda e, tau=tau: e.transpose(p4[0:8, 0:128], ra[:, 48:56], identF[:]),
                waits=[(s_v, B.sems["sv"].n)], inc=s_stv)
            I("vector", lambda e, tau=tau: e.tensor_copy(slotT[:, tau * 128:(tau + 1) * 128], p4[0:8, 0:128]),
                waits=[(s_stv, tau + 1), s_v], inc=s_v)
        I("sync", lambda e: e.dma_start(out=bass.AP(hbm_ps.tensor, 0, [[1024, 8], [1, 1024]]), in_=slotT[:]),
            waits=[(s_v, B.sems["sv"].n)], inc=s_psw, inck=16)
        I("sync", lambda e: e.dma_start(out=bass.AP(hbm_yi.tensor, 0, [[128, 8], [1024, 8], [1, 128]]), in_=slotT[:]),
            waits=[(s_v, B.sems["sv"].n)], inc=s_psw, inck=16)
        I("vector", lambda e: e.memset(idxw[:], 0), waits=[s_v], inc=s_iev)
        I("vector", lambda e: e.memset(yidxw[:], 0), waits=[s_v], inc=s_iev)
        for gi, (srcT, dstT) in enumerate(((hbm_ps, idxw), (hbm_yi, yidxw))):
            for blk in range(4):
                w = [(s_psw, 32), (s_stv, s_stv.n)]
                if gi == 0 and blk == 0:
                    w.append((s_iev, 2))
                I("sync", lambda e, srcT=srcT, blk=blk: e.dma_start(
                    out=flb[:], in_=bass.AP(srcT.tensor, blk * 2048, [[16, 128], [1, 16]])),
                    waits=w, inc=s_idr, inck=16)
                I("tensor", lambda e: e.transpose(p4[0:16, 0:128], flb[:], identF[:]),
                    waits=[(s_idr, s_idr.n), (s_iev, s_iev.n)], inc=s_stv)
                I("vector", lambda e, dstT=dstT, blk=blk: e.tensor_copy(
                    dstT[0:16, blk * 128:(blk + 1) * 128], p4[0:16, 0:128]),
                    waits=[(s_stv, s_stv.n), s_v], inc=s_iev)
        I("sync", lambda e: e.dma_start(out=idxw[16:32, :], in_=idxw[0:16, :]),
            waits=[(s_iev, 6), (s_idr, 128)], inc=s_idr, inck=16)
        I("sync", lambda e: e.dma_start(out=yidxw[16:32, :], in_=yidxw[0:16, :]),
            waits=[(s_iev, 10), (s_idr, 144)], inc=s_idr, inck=16)
        I("sync", lambda e: e.dma_start(out=yidxw[32:64, :], in_=yidxw[0:32, :]),
            waits=[(s_idr, 160)], inc=s_idr, inck=16)

        I("vector", lambda e: e.tensor_copy(cntrow[:], p5[0:1, 0:NE]), waits=[(s_cnt, 1), s_v], inc=s_v)
        I("sync", lambda e: e.dma_start(out=cc_in[:], in_=cntrow[:]), waits=[(s_v, B.sems["sv"].n)], inc=s_ccd, inck=16)
        I("gpsimd", lambda e: e.collective_compute(
            "AllGather", ALU.bypass, replica_groups=[list(range(8))],
            ins=[cc_in], outs=[cc_out]), waits=[(s_ccd, 16)], inc=s_cc)
        I("sync", lambda e: e.dma_start(out=ccs[:], in_=cc_out[:]), waits=[(s_cc, 1)], inc=s_ccd, inck=16)
        I("sync", lambda e: e.dma_start(out=pid_u[:], in_=bass.AP(nc.partition_id_tensor, 0, [[0, 8], [1, 1]])), inc=s_cpd, inck=16)
        I("vector", lambda e: e.tensor_copy(pidf[:], pid_u[:]), waits=[(s_cpd, 16), s_v], inc=s_v)
        I("vector", lambda e: e.tensor_tensor(mask8[:], iota8[:], pidf[:], ALU.is_lt), waits=[(s_g, s_g.n), s_v], inc=s_v)
        I("tensor", lambda e: e.matmul(p6[0:1, 0:NE], mask8[:], ccs[:], start=True, stop=True),
            waits=[(s_v, B.sems["sv"].n), (s_ccd, 32), (s_zev, 8)], inc=s_z)
        I("vector", lambda e: e.tensor_copy(cprow[:], p6[0:1, 0:NE]), waits=[(s_z, 9), s_v], inc=s_v)
        I("sync", lambda e: e.dma_start(out=hbm_cp[:], in_=cprow[:]), waits=[(s_v, B.sems["sv"].n)], inc=s_cpd, inck=16)
        I("sync", lambda e: e.dma_start(out=cpB[:], in_=bass.AP(hbm_cp.tensor, 0, [[0, 128], [1, NE]])), waits=[(s_cpd, 32)], inc=s_cpd, inck=16)

        for tau in range(8):
            I("vector", lambda e, tau=tau: e.tensor_tensor(ra[:, 16:48], pposF[:, tau, :], cpB[:], ALU.add), waits=[(s_cpd, 48), s_v], inc=s_v)
            I("vector", lambda e: e.tensor_scalar(ra[:, 16:48], ra[:, 16:48], float(CAP), None, ALU.is_lt), waits=[s_v], inc=s_v)
            I("vector", lambda e, tau=tau: e.tensor_scalar(rc[:], pposF[:, tau, :], float(CC), None, ALU.is_lt), waits=[s_v], inc=s_v)
            I("vector", lambda e: e.tensor_tensor(ra[:, 16:48], ra[:, 16:48], rc[:], ALU.mult), waits=[s_v], inc=s_v)
            I("vector", lambda e, tau=tau: e.tensor_tensor(ra[:, 16:48], ra[:, 16:48], cwu[:, tau, :], ALU.mult), waits=[s_v], inc=s_v)
            I("vector", lambda e, tau=tau: e.reciprocal(ra[:, 0:1], wsumr[:, tau:tau + 1]), waits=[s_v], inc=s_v)
            I("vector", lambda e: e.tensor_scalar(ra[:, 16:48], ra[:, 16:48], ra[:, 0:1], None, ALU.mult), waits=[s_v], inc=s_v)
            for k in range(K_TOP):
                I("vector", lambda e, tau=tau, k=k: e.tensor_scalar(rc[:], rkm[:, tau, :], float(k), None, ALU.is_equal), waits=[s_v], inc=s_v)
                I("vector", lambda e: e.tensor_tensor(rb[:], ra[:, 16:48], rc[:], ALU.mult), waits=[s_v], inc=s_v)
                I("vector", lambda e, tau=tau, k=k: e.tensor_reduce(cwsel[:, tau, k:k + 1], rb[:], mybir.AxisListType.X, ALU.add), waits=[s_v], inc=(s_keep if k == 7 else s_v))

        for tau in range(8):
            I("gpsimd", lambda e, tau=tau: e.iota(
                bass.AP(stage, tau * 64, [[4096, 128], [512, 8], [1, 1]]),
                [[0, 8], [0, 1]], base=tau * 128, channel_multiplier=1,
                allow_small_or_imprecise_dtypes=True),
                waits=[(s_stg, 2)], inc=s_g)
        for zi in range(4):
            I("sync", lambda e, zi=zi: e.dma_start(
                out=bass.AP(hbm_roster.tensor, zi * 128 * 2050, [[2050, 128], [1, 2050]]),
                in_=ztile[:]),
                waits=[(s_stg, 2)], inc=s_rz, inck=16)
        I("gpsimd", lambda e: e.load_library(library_config.mlp), waits=[(s_g, s_g.n)])
        for ch in range(8):
            w = [(s_rz, 64), (s_idr, 144), (s_iev, 6), (s_g, s_g.n)]
            w.append(tg["wkv2"] if ch == 0 else tg["sca%d" % (ch - 1)])
            B.G(lambda e, h, ch=ch: e.dma_scatter_add(
                hbm_roster[:], stage[:, 8 * ch:8 * (ch + 1), :], idxw[:, 64 * ch:64 * (ch + 1)],
                1024, reg1k, 64, prepare_only=True, sem=h),
                swd_m, waits=w)
            tg["sca%d" % ch] = (swd_m, swd_m.n)
        I("vector", lambda e: e.memset(gidxf[:], 0.0), waits=[s_v], inc=s_gx)
        I("vector", lambda e: e.memset(gidx[:], 0), waits=[s_v], inc=s_gx)

    phase_D2()
    B.flush_block()
    es_d2.close()
    if "route" in debug:
        dbg["d_roster"] = nc.dram_tensor("d_roster", [NSLOT_PAD, 2], F32, kind="ExternalOutput").ap()
        s_dr = sem("dr")
        I("gpsimd", lambda e: e.dma_start(out=dbg["d_roster"][0:8200], in_=bass.AP(hbm_roster.tensor, 0, [[64, 8200], [1, 2]])), waits=[tg["sca7"]], inc=s_dr, inck=16)
        I("gpsimd", lambda e: e.dma_start(out=dbg["d_roster"][8200:16400], in_=bass.AP(hbm_roster.tensor, 8200 * 64, [[64, 8200], [1, 2]])), waits=[(s_dr, 16)], inc=s_dr, inck=16)
        I("gpsimd", lambda e: e.nop(), waits=[s_dr])
        B.flush_block()

    es_e = ExitStack()
    xg = [sb("xg0", [128, 16, CC], BF16, es_e), sb("xg1", [128, 16, CC], BF16, es_e)]
    sg = sb("sg", [128, 6, CC], BF16, es_e)
    au = sb("au", [128, 6, CC], BF16, es_e)
    ystage = [bass.AP(attnb, 0, [[16384, 128], [2048, 4], [1, 2048]]),
              bass.AP(attnb, 8192, [[16384, 128], [2048, 4], [1, 2048]])]


    s_gxc = sem("gxc")
    s_gu = sem("sgu")
    s_si = sem("ssi")
    s_am = sem("sam")
    s_dm = sem("sdm")
    s_ye = sem("sye")
    s_yw = sem("syw")

    def phase_E():
        I("sync", lambda ee: ee.dma_start(
            out=gidxf[0:16, :],
            in_=bass.AP(hbm_roster.tensor, 0, [[64, 16], [1024, 32 * NE]]),
            allow_slow_non_contiguous=True),
            waits=[tg["sca7"], (s_gx, 2)], inc=s_gx, inck=16)
        I("sync", lambda ee: ee.dma_start(
            out=gidxf[16:32, :],
            in_=bass.AP(hbm_roster.tensor, 0, [[64, 16], [1024, 32 * NE]]),
            allow_slow_non_contiguous=True),
            inc=s_gx, inck=16)
        I("vector", lambda ee: ee.tensor_copy(gidx[0:32, :], gidxf[0:32, :]),
            waits=[(s_gx, 2 + 32), s_v], inc=s_gxc)
        for e in range(NE):
            par = e % 2
            w = [(s_gxc, 1), (s_xnw, 128)]
            if e >= 2:
                w.append((s_gu, (e - 1) * 12))
                w.append(tg["xg%d" % (e - 2)])
            else:
                w.append(tg["xc2%d" % (2 + par)])
            B.G(lambda ee, h, e=e, par=par: ee.dma_gather(
                xg[par][:], hbm_xn[:], gidx[:, 32 * e:32 * (e + 1)], CC, regCC, D,
                transpose=True, prepare_only=True, sem=h),
                swd_x[par], waits=w)
            tg["xg%d" % e] = (swd_x[par], swd_x[par].n)
            for t, hd in ((0, wgd), (1, wud)):
                if e == 0:
                    continue
                for ic in range(6):
                    w = [(s_gu, (e - 1) * 12 + 2 * ic + t + 1)]
                    w.append(tg["we%d_%d" % (e - 1, t)] if ic == 0 else tg["we%d_%d_%d" % (e, t, ic - 1)])
                    I("gpsimd", lambda ee, t=t, hd=hd, e=e, ic=ic: ee.dma_start(
                        out=wring[t][:, ic * 2048:(ic + 1) * 2048],
                        in_=bass.AP(hd.tensor, e * D * DI + ic * 2048, [[16 * DI, 128], [1, 2048]])),
                        waits=w, inc=swd_w[t], inck=16)
                    tg["we%d_%d_%d" % (e, t, ic)] = (swd_w[t], swd_w[t].n)
                tg["we%d_%d" % (e, t)] = tg["we%d_%d_5" % (e, t)]
            w = []
            if e >= 1:
                w = [(s_dm, e * 16), tg["we%d_2" % (e - 1)]]
            else:
                w.append(tg["wo15"])
            I("gpsimd", lambda ee, e=e: ee.dma_start(
                out=wring[2][:], in_=bass.AP(wdd.tensor, e * DI * D, [[6 * D, 128], [1, 6 * D]])),
                waits=w, inc=swd_w[2], inck=16)
            tg["we%d_2" % e] = (swd_w[2], swd_w[2].n)
            wgu = [tg["we%d_0" % e], tg["we%d_1" % e], tg["we%d_2" % e]]
            for ic in range(6):
                pg = (p0, p1)[ic % 2]
                pu = (p2, p3)[ic % 2]
                for hc in range(16):
                    w = []
                    if hc == 0:
                        w = [tg.get("we%d_0_%d" % (e, ic), wgu[0]), tg["xg%d" % e]]
                        w.append((s_si, e * 6 + ic - 1) if ic >= 2 else (s_si, e * 6))
                    I("tensor", lambda ee, e=e, ic=ic, hc=hc, pg=pg, par=par: ee.matmul(
                        pg[:, 0:CC], wring[0][:, ic * 2048 + hc * 128: ic * 2048 + hc * 128 + 128],
                        xg[par][:, hc, :], start=(hc == 0), stop=(hc == 15)),
                        waits=w, inc=(s_gu if hc == 15 else None))
                for hc in range(16):
                    w = []
                    if hc == 0:
                        w = [tg.get("we%d_1_%d" % (e, ic), wgu[1])]
                        w.append((s_am, e * 6 + ic - 1) if ic >= 2 else (s_am, e * 6))
                    I("tensor", lambda ee, e=e, ic=ic, hc=hc, pu=pu, par=par: ee.matmul(
                        pu[:, 0:CC], wring[1][:, ic * 2048 + hc * 128: ic * 2048 + hc * 128 + 128],
                        xg[par][:, hc, :], start=(hc == 0), stop=(hc == 15)),
                        waits=w, inc=(s_gu if hc == 15 else None))
                I("scalar", lambda ee, e=e, ic=ic, pg=pg: ee.activation(
                    sg[:, ic, :], pg[:, 0:CC], AF.Silu),
                    waits=[(s_gu, e * 12 + 2 * ic + 1), (s_am, max(0, (e - 1) * 6 + ic + 1)), s_sc], inc=s_si)
                I("vector", lambda ee, e=e, ic=ic, pu=pu: ee.tensor_tensor(
                    au[:, ic, :], sg[:, ic, :], pu[:, 0:CC], ALU.mult),
                    waits=[(s_si, e * 6 + ic + 1), (s_gu, e * 12 + 2 * ic + 2), (s_dm, e * 16), s_v], inc=s_am)
            for sc in range(4):
                for icc in range(6):
                    for fc in range(4):
                        w = []
                        if icc == 0 and fc == 0:
                            w = [wgu[2], (s_ye, e * 16 + sc * 4)]
                        if fc == 0:
                            w.append((s_am, e * 6 + icc + 1))
                        pq = (p4, p5, p6, p7)[fc]
                        I("tensor", lambda ee, e=e, sc=sc, icc=icc, fc=fc, pq=pq: ee.matmul(
                            pq[:], au[:, icc, sc * 128:(sc + 1) * 128],
                            wring[2][:, icc * D + fc * 512: icc * D + fc * 512 + 512],
                            start=(icc == 0), stop=(icc == 5)),
                            waits=w, inc=(s_dm if icc == 5 else None))
                for fc in range(4):
                    pq = (p4, p5, p6, p7)[fc]
                    w = [(s_dm, e * 16 + sc * 4 + fc + 1)]
                    if sc == 0 and fc == 0:
                        w.append((s_yw, 16 * max(0, e - 1)))
                    I("scalar", lambda ee, e=e, sc=sc, fc=fc, pq=pq: ee.activation(
                        ystage[e % 2][:, sc, fc * 512:(fc + 1) * 512], pq[:], AF.Identity),
                        waits=w + [s_sc], inc=s_ye)
            I("sync", lambda ee, e=e: ee.dma_start(
                out=bass.AP(hbm_y.tensor, e * CC * D, [[D, 128], [128 * D, 4], [1, D]]),
                in_=ystage[e % 2]),
                waits=[(s_ye, (e + 1) * 16), (s_yw, 16 * e)], inc=s_yw, inck=16)

    phase_E()
    B.flush_block()
    es_e.close()
    es_w.close()

    es_f = ExitStack()
    gb = [sb("gb%d" % i, [128, K_TOP, D], BF16, es_f) for i in range(3)]
    accA = sb("accA", [128, D], F32, es_f)
    accB = sb("accB", [128, D], F32, es_f)
    outb = [sb("outb0", [128, D], F32, es_f), sb("outb1", [128, D], F32, es_f)]
    xna = [sb("xna%d" % i, [128, D], BF16, es_f) for i in range(3)]

    zsm = sb("zsm", [128, 256], BF16, es_f)
    fcol = sb("fcol", [128, 8], F32, es_f)

    s_fz = sem("fz")
    s_yir = sem("yir")
    s_xnl = [sem("xnl%d" % i) for i in range(3)]
    s_acc = sem("sacc")
    s_out = sem("sout")

    def phase_F():
        sq_mark = {}
        I("vector", lambda e: e.memset(zsm[:], 0.0), waits=[s_v], inc=s_v)
        I("sync", lambda e: e.dma_start(
            out=bass.AP(hbm_y.tensor, NSLOT * D, [[256, 128], [1, 256]]), in_=zsm[:]),
            waits=[(s_v, B.sems["sv"].n)], inc=s_fz, inck=16)
        for tau in range(8):
            par = tau % 3
            w = [(s_yw, 16 * NE), (s_fz, 16), (s_idr, 176)]
            if tau >= 2:
                w.append(tg["yg%d" % (tau - 2)])
            if tau >= 3:
                w.append((s_acc, tau - 2))
            if tau < 2:
                w.append(tg["xg30"])
                w.append(tg["xg31"])
            B.G(lambda e, h, tau=tau, par=par: e.dma_gather(
                gb[par][:], hbm_y[:], yidxw[:, tau * 64:(tau + 1) * 64], T, regT, D,
                transpose=False, prepare_only=True, sem=h, queue_num=1),
                swd_y[tau % 2], waits=w, queue_num=1)
            tg["yg%d" % tau] = (swd_y[tau % 2], swd_y[tau % 2].n)
            w2 = [(s_xnw, 128)]
            if tau >= 3:
                w2.append((s_acc, tau - 2))
            I("sync", lambda e, tau=tau, par=par: e.dma_start(
                out=xna[par][:],
                in_=bass.AP(hbm_xn.tensor, tau * 128 * D, [[D, 128], [1, D]])),
                waits=w2, inc=s_xnl[par], inck=16)
            for j in range(K_TOP):
                src = accA if j % 2 == 0 else accB
                dst = accB if j % 2 == 0 else accA
                w3 = [s_v]
                if j == 0:
                    w3 += [tg["yg%d" % tau], (s_xnl[par], 16 * (tau // 3 + 1)), (s_keep, 8)]
                    if tau >= 1:
                        w3.append((s_sc, sq_mark[tau - 1]))
                I("vector", lambda e, tau=tau, j=j, par=par, src=src, dst=dst: e.scalar_tensor_tensor(
                    dst[:], gb[par][:, j, :], cwsel[:, tau, j:j + 1],
                    (xna[par][:] if j == 0 else src[:]),
                    ALU.mult, ALU.add),
                    waits=w3, inc=(s_acc if j == 7 else s_v))
            I("vector", lambda e, tau=tau: e.tensor_reduce(fcol[:, 0:1], accA[:], mybir.AxisListType.X, ALU.add), waits=[s_v, (s_acc, tau + 1)], inc=s_v)
            I("vector", lambda e: e.tensor_scalar(fcol[:, 0:1], fcol[:, 0:1], 1.0 / D, None, ALU.mult), waits=[s_v], inc=s_v)
            I("scalar", lambda e: e.activation(accB[:], accA[:], AF.Square, accum_out=fcol[:, 1:2]), waits=[(s_v, B.sems["sv"].n), s_sc], inc=s_sc)
            sq_mark[tau] = B.sems["ssc"].n
            I("vector", lambda e: e.tensor_scalar(fcol[:, 1:2], fcol[:, 1:2], 1.0 / D, None, ALU.mult), waits=[s_sc, s_v], inc=s_v)
            I("vector", lambda e: e.scalar_tensor_tensor(fcol[:, 2:3], fcol[:, 0:1], 1.0, fcol[:, 0:1], ALU.mult, ALU.mult), waits=[s_v], inc=s_v)
            I("vector", lambda e: e.tensor_tensor(fcol[:, 1:2], fcol[:, 1:2], fcol[:, 2:3], ALU.subtract), waits=[s_v], inc=s_v)
            I("scalar", lambda e: e.activation(fcol[:, 3:4], fcol[:, 1:2], AF.Sqrt, bias=epsF[:]), waits=[s_v, s_sc], inc=s_sc)
            I("vector", lambda e: e.reciprocal(fcol[:, 4:5], fcol[:, 3:4]), waits=[s_sc, s_v], inc=s_v)
            ob = outb[tau % 2]
            w4 = [s_v]
            if tau >= 2:
                w4.append((s_out, 16 * (tau - 1)))
            I("vector", lambda e, tau=tau, ob=ob: e.tensor_scalar(ob[:], accA[:], fcol[:, 0:1], None, ALU.subtract), waits=w4, inc=s_v)
            I("vector", lambda e, ob=ob: e.tensor_scalar(ob[:], ob[:], fcol[:, 4:5], None, ALU.mult), waits=[s_v], inc=s_v)
            I("sync", lambda e, tau=tau, ob=ob: e.dma_start(
                out=bass.AP(outd.tensor, tau * 128 * D, [[D, 128], [1, D]]),
                in_=ob[:]),
                waits=[(s_v, B.sems["sv"].n), (s_out, 16 * tau)], inc=s_out, inck=16)
        I("sync", lambda e: e.nop(), waits=[(s_out, 128)])

    phase_F()

    if "tbl" in debug:
        d_tbl = nc.dram_tensor("d_tbl", [4, 128, S], F32, kind="ExternalOutput").ap()
        dbg["d_tbl"] = d_tbl
        s_dbg = sem("dbg")
        I("sync", lambda e: e.dma_start(out=d_tbl[0], in_=cosK[:]), waits=[B.sems["ssc"], B.sems["sv"]], inc=s_dbg, inck=16)
        I("sync", lambda e: e.dma_start(out=d_tbl[1], in_=sinK[:]), inc=s_dbg, inck=16)
        I("sync", lambda e: e.dma_start(out=bass.AP(d_tbl.tensor, 2 * 128 * S, [[S, 128], [1, T]]), in_=cosQ[:]), inc=s_dbg, inck=16)
        I("sync", lambda e: e.dma_start(out=bass.AP(d_tbl.tensor, 3 * 128 * S, [[S, 128], [1, T]]), in_=sinQ[:]), inc=s_dbg, inck=16)
        I("sync", lambda e: e.nop(), waits=[s_dbg])
        I("vector", lambda e: e.tensor_copy(scr[:, 2:3], ident[:, 0:1]), waits=[B.sems["sv"]], inc=B.sems["sv"])

    B.flush_block()
    es_f.close()
    es_moe.close()
    return nc, es, dbg


from concourse.bass_utils import run_bass_kernel_spmd
from concourse.library_overlay import lower_extended_insts

_PQ = perm_q_rows()
_PK = perm_k_rows()
_PA = perm_attn_rows()


def kernel(L0_attn_x, L0_attn_W_q, L0_attn_b_q, L0_attn_W_kv, L0_attn_b_kv,
           L0_attn_W_out, L0_attn_b_out, L0_ffn_gate_w, L0_ffn_w_gate,
           L0_ffn_w_up, L0_ffn_w_down):
    x = np.ascontiguousarray(np.asarray(L0_attn_x, dtype=np.float32))
    Wq = np.asarray(L0_attn_W_q, dtype=np.float32)[_PQ]
    bq = np.asarray(L0_attn_b_q, dtype=np.float32)[_PQ]
    Wkv = np.asarray(L0_attn_W_kv, dtype=np.float32).copy()
    Wkv[:512] = np.asarray(L0_attn_W_kv, dtype=np.float32)[:512][_PK]
    bkv = np.asarray(L0_attn_b_kv, dtype=np.float32).copy()
    bkv[:512] = np.asarray(L0_attn_b_kv, dtype=np.float32)[:512][_PK]
    woT = np.ascontiguousarray(np.asarray(L0_attn_W_out, dtype=np.float32).T[_PA])
    wqT = np.ascontiguousarray(Wq.T)
    wkvT = np.ascontiguousarray(Wkv.T)
    bq_t = np.ascontiguousarray(bq.reshape(16, 128).T)
    bkv_t = np.ascontiguousarray(bkv.reshape(8, 128).T)
    bo_t = np.ascontiguousarray(np.asarray(L0_attn_b_out, dtype=np.float32).reshape(16, 128).T)
    gw = np.ascontiguousarray(np.asarray(L0_ffn_gate_w, dtype=np.float32))
    wg = np.ascontiguousarray(np.asarray(L0_ffn_w_gate, dtype=np.float32)
                              .reshape(32, 16, 128, 6, 128).transpose(0, 2, 3, 1, 4).reshape(32, 128, 16 * 768))
    wu = np.ascontiguousarray(np.asarray(L0_ffn_w_up, dtype=np.float32)
                              .reshape(32, 16, 128, 6, 128).transpose(0, 2, 3, 1, 4).reshape(32, 128, 16 * 768))
    wd = np.ascontiguousarray(np.asarray(L0_ffn_w_down, dtype=np.float32)
                              .reshape(32, 6, 128, 2048).transpose(0, 2, 1, 3).reshape(32, 128, 6 * 2048))

    xTs = [np.ascontiguousarray(x[b].T) for b in range(4)]
    in_maps = []
    for c in range(8):
        b, h = c // 2, c % 2
        in_maps.append({
            "xT": xTs[b],
            "xqT": np.ascontiguousarray(xTs[b][:, h * T:(h + 1) * T]),
            "wq": wqT, "wkv": wkvT, "wo": woT,
            "bq": bq_t, "bkv": bkv_t, "bo": bo_t,
            "gw": gw, "wg": wg, "wu": wu, "wd": wd,
            "pos0": np.array([[float(h * T)]], np.float32),
        })

    nc, es, dbg = build()
    with es:
        lower_extended_insts(nc)
        res = run_bass_kernel_spmd(nc, in_maps, core_ids=list(range(8)))
    out = np.empty((4, S, D), np.float32)
    for c in range(8):
        b, h = c // 2, c % 2
        out[b, h * T:(h + 1) * T, :] = res.results[c]["out"]
    return out


# version 16 (final)
# speedup vs baseline: 1.0388x; 1.0388x over previous
import math
import numpy as np
import concourse.bass as bass
import concourse.mybir as mybir
from concourse import library_config

F32 = mybir.dt.float32
BF16 = mybir.dt.bfloat16
I16 = mybir.dt.int16
I32 = mybir.dt.int32
U32 = mybir.dt.uint32
AF = mybir.ActivationFunctionType
ALU = mybir.AluOpType

D = 2048
S = 2048
T = 1024
NH = 32
NKV = 8
DH = 64
NE = 32
DI = 768
CC = 512
NSLOT = NE * CC
NSLOT_PAD = NSLOT + 16
CAP = 4096
K_TOP = 8

TWO_PI_F32 = float(np.float32(2.0 * np.pi))
C1_2PI = float(np.floor(2.0 * np.pi * 4096.0) / 4096.0)
C2_2PI = float(np.float32(2.0 * np.pi - C1_2PI))
PI_F32 = float(np.float32(np.pi))
HALF_PI_F32 = float(np.float32(np.pi / 2))
LN1E4_D32 = math.log(10000.0) / 32.0
C1_FREQ = float(np.float32(-LN1E4_D32))
C2_FREQ = float(np.float32(-LN1E4_D32 - C1_FREQ))
INV2PI = float(np.float32(1.0 / TWO_PI_F32))
PI_DOWN = float(np.nextafter(np.float32(np.pi), np.float32(0.0)))


def qhead_of(tile, j):
    return 16 * (tile // 4) + 4 * j + (tile % 4)


def perm_q_rows():
    perm = np.zeros(D, dtype=np.int64)
    for tile in range(8):
        for j in range(4):
            qh = qhead_of(tile, j)
            for i in range(32):
                perm[tile * 128 + j * 32 + i] = qh * 64 + 2 * i
                perm[1024 + tile * 128 + j * 32 + i] = qh * 64 + 2 * i + 1
    return perm


def perm_k_rows():
    perm = np.zeros(512, dtype=np.int64)
    for k in range(8):
        tile = k // 4
        base = 32 * (k % 4)
        for i in range(32):
            perm[tile * 128 + base + i] = k * 64 + 2 * i
            perm[256 + tile * 128 + base + i] = k * 64 + 2 * i + 1
    return perm


def perm_attn_rows():
    perm = np.zeros(D, dtype=np.int64)
    for tau in range(8):
        for pair in range(2):
            blk = tau * 2 + pair
            for half in range(2):
                qh = qhead_of(tau, 2 * pair + half)
                for dd in range(64):
                    perm[blk * 128 + half * 64 + dd] = qh * 64 + dd
    return perm


class Sem:
    __slots__ = ("h", "n", "name")

    def __init__(self, h, name):
        self.h = h
        self.n = 0
        self.name = name


class Builder:
    ENGS = ("sync", "tensor", "vector", "scalar", "gpsimd")

    def __init__(self, nc, es):
        self.nc = nc
        self.es = es
        self.q = {e: [] for e in self.ENGS}
        self.sems = {}
        self.track = {}

    def sem(self, name):
        if name not in self.sems:
            self.sems[name] = Sem(self.es.enter_context(self.nc.semaphore(name)), name)
        return self.sems[name]

    def I(self, eng, make, waits=(), inc=None, inck=1):
        wcap = []
        if eng in ("vector", "scalar") and eng in self.track:
            ts, tn = self.track[eng]
            wcap.append((ts.h, tn))
        for w in waits:
            if isinstance(w, tuple):
                s, v = w
                wcap.append((s.h, s.n if v is None else v))
            else:
                wcap.append((w.h, w.n))
        icap = None
        if inc is not None:
            inc.n += inck
            icap = (inc.h, inck)
            if eng in ("vector", "scalar"):
                self.track[eng] = (inc, inc.n)

        def f(e, wcap=tuple(wcap), icap=icap, make=make):
            for h, v in wcap:
                if v > 0:
                    e.wait_ge(h, v)
            try:
                ins = make(e)
            except Exception:
                import traceback
                traceback.print_exc()
                raise
            if icap is not None:
                ins.then_inc(icap[0], icap[1])
        self.q[eng].append(f)

    def raw(self, eng, f):
        self.q[eng].append(f)

    def G(self, make_prep, sem_done, waits=(), inck=16, queue_num=0):
        prep = self.sem("swdge_prep")
        wcap = []
        for w in waits:
            if isinstance(w, tuple):
                s, v = w
                wcap.append((s.h, s.n if v is None else v))
            else:
                wcap.append((w.h, w.n))
        sem_done.n += inck
        prep.n += 1
        pn = prep.n

        def f(e, wcap=tuple(wcap), make_prep=make_prep, h=sem_done.h, ph=prep.h, pn=pn, qn=queue_num):
            for hh, v in wcap:
                if v > 0:
                    e.wait_ge(hh, v)
            make_prep(e, h).then_inc(ph, 1)
            e.wait_ge(ph, pn)
            e.trigger_dma(count=1, queue_num=qn)
        self.q["gpsimd"].append(f)

    def flush_block(self):
        nc = self.nc
        q, self.q = self.q, {e: [] for e in self.ENGS}
        with nc.Block() as block:
            for name in self.ENGS:
                if q[name]:
                    def body(e, fns=tuple(q[name])):
                        for f in fns:
                            f(e)
                    getattr(block, name)(body)


def build(debug=(), cc_shared=True):
    from contextlib import ExitStack
    nc = bass.Bass("TRN2", target_bir_lowering=False, debug=False, num_swdge_queues=2)
    es = ExitStack()
    B = Builder(nc, es)
    I, sem = B.I, B.sem

    xT = nc.dram_tensor("xT", [D, S], F32, kind="ExternalInput").ap()
    xqT = nc.dram_tensor("xqT", [D, T], F32, kind="ExternalInput").ap()
    wq = nc.dram_tensor("wq", [D, D], F32, kind="ExternalInput").ap()
    wkv = nc.dram_tensor("wkv", [D, 1024], F32, kind="ExternalInput").ap()
    wo = nc.dram_tensor("wo", [D, D], F32, kind="ExternalInput").ap()
    bqd = nc.dram_tensor("bq", [128, 16], F32, kind="ExternalInput").ap()
    bkvd = nc.dram_tensor("bkv", [128, 8], F32, kind="ExternalInput").ap()
    bod = nc.dram_tensor("bo", [128, 16], F32, kind="ExternalInput").ap()
    gwd = nc.dram_tensor("gw", [D, NE], F32, kind="ExternalInput").ap()
    wgd = nc.dram_tensor("wg", [NE, 128, 16 * DI], F32, kind="ExternalInput").ap()
    wud = nc.dram_tensor("wu", [NE, 128, 16 * DI], F32, kind="ExternalInput").ap()
    wdd = nc.dram_tensor("wd", [NE, 128, 6 * D], F32, kind="ExternalInput").ap()
    pos0d = nc.dram_tensor("pos0", [1, 1], F32, kind="ExternalInput").ap()
    outd = nc.dram_tensor("out", [T, D], F32, kind="ExternalOutput").ap()

    hbm_xn = nc.dram_tensor("hbm_xn", [T, D], BF16, kind="Internal").ap()
    hbm_ps = nc.dram_tensor("hbm_ps", [1, T * K_TOP], F32, kind="Internal").ap()
    hbm_yi = nc.dram_tensor("hbm_yi", [1, T * K_TOP], F32, kind="Internal").ap()
    hbm_ab = nc.dram_tensor("hbm_ab", [2, T], F32, kind="Internal").ap()
    hbm_roster = nc.dram_tensor("hbm_roster", [NSLOT_PAD, 64], F32, kind="Internal").ap()
    hbm_y = nc.dram_tensor("hbm_y", [NSLOT_PAD, D], BF16, kind="Internal").ap()

    dbg = {}
    if "qkv" in debug:
        dbg["d_q"] = nc.dram_tensor("d_q", [16, 128, T], F32, kind="ExternalOutput").ap()
        dbg["d_k"] = nc.dram_tensor("d_k", [4, 128, S], F32, kind="ExternalOutput").ap()
        dbg["d_v"] = nc.dram_tensor("d_v", [16, 128, 520], F32, kind="ExternalOutput").ap()

    def sb(name, shape, dtype, stack=None):
        return (stack or es).enter_context(nc.sbuf_tensor(name, shape, dtype))
    es_attn = ExitStack()
    es_a = ExitStack()
    es_b = ExitStack()
    psa = lambda name, shape, dtype=F32: es.enter_context(nc.psum_tensor(name, shape, dtype))

    ident = sb("ident", [128, 128], BF16)
    triu = sb("triu", [128, 128], BF16)
    onescol_b = sb("onescol_b", [128, 1], BF16)
    onescol_f = sb("onescol_f", [128, 1], F32)
    onesrow_b = sb("onesrow_b", [1, 128], BF16)
    onesrow_f = sb("onesrow_f", [1, 128], F32)
    bq_t = sb("bq_t", [128, 16], F32)
    bkv_t = sb("bkv_t", [128, 8], F32)
    bo_t = sb("bo_t", [128, 16], F32)
    gwb = sb("gwb", [128, 16, NE], BF16)
    scr = sb("scr", [128, 8], F32)

    p03 = psa("p03", [128, 4, 512])
    p47 = psa("p47", [128, 4, 512])
    p0 = p03[:, 0, :]
    p1 = p03[:, 1, :]
    p2 = p03[:, 2, :]
    p3 = p03[:, 3, :]
    p4 = p47[:, 0, :]
    p5 = p47[:, 1, :]
    p6 = p47[:, 2, :]
    p7 = p47[:, 3, :]
    pbank = [p0, p1, p2, p3, p4, p5, p6, p7]

    attnb = sb("attnb", [128, 16, T], BF16)
    qbf = sb("qbf", [128, 16, T], BF16, es_attn)
    kbf = sb("kbf", [128, 4, S], BF16, es_attn)
    vkm = sb("vkm", [128, 16, 520], BF16, es_attn)
    cosK = sb("cosK", [128, S], F32, es_b)
    sinK = sb("sinK", [128, S], F32, es_b)
    cosQ = sb("cosQ", [128, T], F32, es_b)
    sinQ = sb("sinQ", [128, T], F32, es_b)

    s_ld0 = sem("ld0")
    swd_x = [sem("swdx0"), sem("swdx1")]
    swd_w = [sem("swdw0"), sem("swdw1"), sem("swdw2")]
    swd_m = sem("swdm")
    swd_y = [sem("swdy0"), sem("swdy1")]
    tg = {}
    s_gw = swd_m
    s_v = sem("sv")
    s_sc = sem("ssc")
    s_g = sem("sg")
    s_t = sem("st_t")

    def phase_A():
        I("sync", lambda e: e.dma_start(out=bq_t[:], in_=bqd[:]), inc=s_ld0, inck=16)
        I("sync", lambda e: e.dma_start(out=bkv_t[:], in_=bkvd[:]), inc=s_ld0, inck=16)
        I("sync", lambda e: e.dma_start(out=bo_t[:], in_=bod[:]), inc=s_ld0, inck=16)
        I("sync", lambda e: e.dma_start(out=scr[:, 0:1], in_=bass.AP(pos0d.tensor, 0, [[0, 128], [1, 1]])), inc=s_ld0, inck=16)

        I("gpsimd", lambda e: e.dma_start(out=gwb[:], in_=bass.AP(gwd.tensor, 0, [[NE, 128], [128 * NE, 16], [1, NE]])), inc=swd_m, inck=16)
        tg["gw"] = (swd_m, swd_m.n)

        io_r = sb("io_r", [128, 128], F32, es_a)
        io_c = sb("io_c", [128, 128], F32, es_a)
        tbl = sb("tbl", [128, S], F32, es_a)
        tA = sb("tA", [128, S], F32, es_a)
        tB = sb("tB", [128, S], F32, es_a)
        tC = sb("tC", [128, S], F32, es_a)
        iT = sb("iT", [128, S], I32, es_a)
        freqc = sb("freqc", [128, 1], F32, es_a)
        f1 = sb("f1", [128, 1], F32, es_a)
        pmod = sb("pmod", [128, 1], F32, es_a)
        thK = sb("thK", [128, S], F32, es_a)
        thQ = sb("thQ", [128, T], F32, es_a)

        I("gpsimd", lambda e: e.iota(scr[:, 1:2], [[0, 1]], base=0, channel_multiplier=1, allow_small_or_imprecise_dtypes=True), inc=s_g)
        I("gpsimd", lambda e: e.iota(io_r[:], [[0, 128]], base=0, channel_multiplier=1, allow_small_or_imprecise_dtypes=True), inc=s_g)
        I("gpsimd", lambda e: e.iota(io_c[:], [[1, 128]], base=0, channel_multiplier=0, allow_small_or_imprecise_dtypes=True), inc=s_g)
        I("gpsimd", lambda e: e.iota(tbl[:], [[1, S]], base=0, channel_multiplier=0, allow_small_or_imprecise_dtypes=True), inc=s_g)

        for tile, val in ((onescol_b, 1.0), (onescol_f, 1.0), (onesrow_b, 1.0), (onesrow_f, 1.0)):
            I("vector", lambda e, t=tile, v=val: e.memset(t[:], v), inc=s_v)
        I("vector", lambda e: e.tensor_tensor(ident[:], io_r[:], io_c[:], ALU.is_equal), waits=[(s_g, 3), s_v], inc=s_v)
        I("vector", lambda e: e.tensor_tensor(triu[:], io_r[:], io_c[:], ALU.is_lt), waits=[s_v], inc=s_v)
        I("vector", lambda e: e.tensor_scalar(f1[:], scr[:, 1:2], 1.0 / 32.0, None, ALU.mult), waits=[(s_g, 1), s_v], inc=s_v)
        I("vector", lambda e: e.tensor_copy(iT[:, 0:1], f1[:]), waits=[s_v], inc=s_v)
        I("vector", lambda e: e.tensor_copy(pmod[:], iT[:, 0:1]), waits=[s_v], inc=s_v)
        I("vector", lambda e: e.tensor_tensor(freqc[:], pmod[:], f1[:], ALU.is_gt), waits=[s_v], inc=s_v)
        I("vector", lambda e: e.tensor_tensor(f1[:], freqc[:], pmod[:], ALU.subtract), waits=[s_v], inc=s_v)
        I("vector", lambda e: e.scalar_tensor_tensor(pmod[:], f1[:], 32.0, scr[:, 1:2], ALU.mult, ALU.add), waits=[s_v], inc=s_v)
        I("scalar", lambda e: e.activation(f1[:], pmod[:], AF.Exp, scale=C1_FREQ), waits=[s_v], inc=s_sc)
        I("scalar", lambda e: e.activation(pmod[:], pmod[:], AF.Exp, scale=C2_FREQ), waits=[s_sc], inc=s_sc)
        I("vector", lambda e: e.tensor_tensor(freqc[:], f1[:], pmod[:], ALU.mult), waits=[s_sc, s_v], inc=s_v)
        I("vector", lambda e: e.tensor_scalar(thK[:], tbl[:], freqc[:], None, ALU.mult), waits=[(s_g, 4), s_v], inc=s_v)
        I("vector", lambda e: e.tensor_scalar(thQ[:], tbl[:, 0:T], scr[:, 0:1], None, ALU.add), waits=[(s_ld0, 64), s_v], inc=s_v)
        I("vector", lambda e: e.tensor_scalar(thQ[:], thQ[:], freqc[:], None, ALU.mult), waits=[s_v], inc=s_v)

        def table(theta, shift, ncols, dest):
            a = tA[:, 0:ncols]
            b = tB[:, 0:ncols]
            c = tC[:, 0:ncols]
            it = iT[:, 0:ncols]
            I("vector", lambda e: e.tensor_scalar(a, theta, INV2PI, shift * INV2PI + 0.5, ALU.mult, ALU.add), waits=[s_v, s_sc], inc=s_v)
            I("vector", lambda e: e.tensor_copy(it, a), waits=[s_v], inc=s_v)
            I("vector", lambda e: e.tensor_copy(b, it), waits=[s_v], inc=s_v)
            I("vector", lambda e: e.tensor_tensor(c, b, a, ALU.is_gt), waits=[s_v], inc=s_v)
            I("vector", lambda e: e.tensor_tensor(a, c, b, ALU.subtract), waits=[s_v], inc=s_v)
            I("vector", lambda e: e.scalar_tensor_tensor(b, a, C1_2PI, theta, ALU.mult, ALU.add), waits=[s_v], inc=s_v)
            I("vector", lambda e: e.scalar_tensor_tensor(c, a, C2_2PI, b, ALU.mult, ALU.add), waits=[s_v], inc=s_v)
            I("vector", lambda e: e.tensor_scalar(a, c, shift, PI_DOWN, ALU.add, ALU.min), waits=[s_v], inc=s_v)
            I("vector", lambda e: e.tensor_scalar(a, a, -PI_DOWN, None, ALU.max), waits=[s_v], inc=s_v)
            I("scalar", lambda e: e.activation(dest, a, AF.Sin), waits=[s_v, s_sc], inc=s_sc)

        table(thK[:], 0.0, S, sinK[:])
        table(thK[:], HALF_PI_F32, S, cosK[:])
        table(thQ[:], 0.0, T, sinQ[:])
        table(thQ[:], HALF_PI_F32, T, cosQ[:])

    phase_A()
    B.flush_block()
    es_a.close()

    xbf = [sb("xbf0", [128, 16, 512], BF16, es_b), sb("xbf1", [128, 16, 512], BF16, es_b)]
    wkvb = sb("wkvb", [128, 16, 512], BF16, es_b)
    wqs = [sb("wqs%d" % i, [128, 16, 128], BF16, es_b) for i in range(3)]
    kf32 = sb("kf32", [128, 4, 512], F32, es_b)
    qf32 = kf32
    vfe = sb("vfe", [128, 512], BF16, es_b)
    rta = sb("rta", [128, 512], F32, es_b)
    rtb = sb("rtb", [128, 512], F32, es_b)


    s_kvmm = sem("kvmm")
    s_kevac = sem("kevac")
    s_rope = sem("rope")
    s_tr = sem("tr")
    s_trv = sem("trv")
    s_qmm = sem("qmm")
    s_qevac = sem("qevac")

    def phase_B():
        for qtc in range(2):
            I("gpsimd", lambda e, qtc=qtc: e.dma_start(
                out=xbf[qtc][:],
                in_=bass.AP(xqT.tensor, qtc * 512, [[T, 128], [128 * T, 16], [1, 512]])),
                inc=swd_x[qtc], inck=16)
            tg["xq%d" % qtc] = (swd_x[qtc], swd_x[qtc].n)
        for i2 in range(16):
            pr, half = i2 // 2, i2 % 2
            oc = pr + 8 * half
            w = []
            if i2 >= 3:
                I("gpsimd", lambda e, i2=i2: e.nop(), waits=[(s_qmm, 4 * ((i2 - 3) // 2 + 1))])
                w = [tg["wq%d" % (i2 - 3)]]
            I("gpsimd", lambda e, oc=oc, b=i2 % 3: e.dma_start(
                out=wqs[b][:],
                in_=bass.AP(wq.tensor, oc * 128, [[D, 128], [128 * D, 16], [1, 128]])),
                waits=w, inc=swd_w[i2 % 3], inck=16)
            tg["wq%d" % i2] = (swd_w[i2 % 3], swd_w[i2 % 3].n)
        I("gpsimd", lambda e: e.dma_start(out=wkvb[:], in_=bass.AP(wkv.tensor, 0, [[1024, 128], [128 * 1024, 16], [1, 512]])), waits=[tg["gw"]], inc=swd_m, inck=16)
        tg["wkv"] = (swd_m, swd_m.n)
        for tc in range(4):
            if tc >= 2:
                I("gpsimd", lambda e: e.nop(), waits=[(s_kvmm, (tc - 1) * 4)])
            else:
                I("gpsimd", lambda e: e.nop(), waits=[(s_qmm, 32)])
            I("gpsimd", lambda e, tc=tc: e.dma_start(
                out=xbf[tc % 2][:],
                in_=bass.AP(xT.tensor, tc * 512, [[S, 128], [128 * S, 16], [1, 512]])),
                waits=[tg["xq%d" % (tc % 2)] if tc < 2 else tg["xc%d" % (tc - 2)]],
                inc=swd_x[tc % 2], inck=16)
            tg["xc%d" % tc] = (swd_x[tc % 2], swd_x[tc % 2].n)
        I("gpsimd", lambda e: e.nop(), waits=[(s_kvmm, 16)])
        I("gpsimd", lambda e: e.dma_start(out=wkvb[:], in_=bass.AP(wkv.tensor, 512, [[1024, 128], [128 * 1024, 16], [1, 512]])), waits=[tg["wkv"]], inc=swd_m, inck=16)
        tg["wkv2"] = (swd_m, swd_m.n)
        for tc in range(4):
            if tc >= 2:
                I("gpsimd", lambda e: e.nop(), waits=[(s_kvmm, 16 + (tc - 1) * 4)])
            I("gpsimd", lambda e, tc=tc: e.dma_start(
                out=xbf[tc % 2][:],
                in_=bass.AP(xT.tensor, tc * 512, [[S, 128], [128 * S, 16], [1, 512]])),
                waits=[tg["xc%d" % (tc + 2)] if tc < 2 else tg["xc2%d" % (tc - 2)]],
                inc=swd_x[tc % 2], inck=16)
            tg["xc2%d" % tc] = (swd_x[tc % 2], swd_x[tc % 2].n)

        qps = [p0, p1, p2, p3]
        for pr in range(8):
            for half in range(2):
                for qtc in range(2):
                    pt = qps[half * 2 + qtc]
                    for hc in range(16):
                        w = []
                        if hc == 0:
                            i2 = 2 * pr + half
                            w = [tg["xq%d" % qtc], tg["wq%d" % i2]]
                            if pr > 0:
                                w.append((s_qevac, (pr - 1) * 4 + half * 2 + qtc + 1))
                        I("tensor", lambda e, pr=pr, half=half, qtc=qtc, hc=hc, pt=pt: e.matmul(
                            pt[:], wqs[(2 * pr + half) % 3][:, hc, :], xbf[qtc][:, hc, :],
                            start=(hc == 0), stop=(hc == 15)),
                            waits=w, inc=(s_qmm if hc == 15 else None))
            for half in range(2):
                for qtc in range(2):
                    pt = qps[half * 2 + qtc]
                    w = [(s_qmm, pr * 4 + half * 2 + qtc + 1)]
                    if pr > 0:
                        w.append((s_rope, pr * 2))
                    I("scalar", lambda e, pr=pr, half=half, qtc=qtc, pt=pt: e.activation(
                        qf32[:, half * 2 + qtc, :], pt[:], AF.Identity,
                        bias=bq_t[:, pr + 8 * half:pr + 8 * half + 1]),
                        waits=w + [s_sc, (s_ld0, 64)], inc=s_qevac)
            for qtc in range(2):
                sl = slice(qtc * 512, (qtc + 1) * 512)
                x1 = qf32[:, qtc, :]
                x2 = qf32[:, 2 + qtc, :]
                w = [(s_qevac, pr * 4 + 4)]
                I("vector", lambda e, x1=x1, sl=sl: e.tensor_tensor(rta[:], x1, cosQ[:, sl], ALU.mult), waits=w + [s_v], inc=s_v)
                I("vector", lambda e, x2=x2, sl=sl: e.tensor_tensor(rtb[:], x2, sinQ[:, sl], ALU.mult), waits=[s_v], inc=s_v)
                I("vector", lambda e, pr=pr, sl=sl: e.tensor_tensor(qbf[:, pr, sl], rta[:], rtb[:], ALU.subtract), waits=[s_v], inc=s_v)
                I("vector", lambda e, x1=x1, sl=sl: e.tensor_tensor(rta[:], x1, sinQ[:, sl], ALU.mult), waits=[s_v], inc=s_v)
                I("vector", lambda e, x2=x2, sl=sl: e.tensor_tensor(rtb[:], x2, cosQ[:, sl], ALU.mult), waits=[s_v], inc=s_v)
                I("vector", lambda e, pr=pr, sl=sl: e.tensor_tensor(qbf[:, pr + 8, sl], rta[:], rtb[:], ALU.add), waits=[s_v], inc=s_rope)

        for tc in range(4):
            sl = slice(tc * 512, (tc + 1) * 512)
            for oc in range(4):
                for hc in range(16):
                    w = []
                    if hc == 0:
                        w = [tg["xc%d" % tc], tg["wkv"]]
                        if tc > 0:
                            w.append((s_kevac, tc * 4))
                        else:
                            w.append((s_qevac, 32))
                    I("tensor", lambda e, oc=oc, hc=hc, tc=tc: e.matmul(
                        pbank[oc], wkvb[:, hc, oc * 128:(oc + 1) * 128], xbf[tc % 2][:, hc, :],
                        start=(hc == 0), stop=(hc == 15)),
                        waits=w, inc=(s_kvmm if hc == 15 else None))
            for oc in range(4):
                w = [(s_kvmm, tc * 4 + oc + 1), (s_ld0, 64), (s_rope, 16 + tc * 2)]
                I("scalar", lambda e, oc=oc: e.activation(
                    kf32[:, oc, :], pbank[oc], AF.Identity, bias=bkv_t[:, oc:oc + 1]),
                    waits=w + [s_sc], inc=s_kevac)
            for a, b2 in ((0, 2), (1, 3)):
                w = [(s_kevac, tc * 4 + 4)]
                I("vector", lambda e, a=a, sl=sl: e.tensor_tensor(rta[:], kf32[:, a, :], cosK[:, sl], ALU.mult), waits=w + [s_v], inc=s_v)
                I("vector", lambda e, b2=b2, sl=sl: e.tensor_tensor(rtb[:], kf32[:, b2, :], sinK[:, sl], ALU.mult), waits=[s_v], inc=s_v)
                I("vector", lambda e, a=a, sl=sl: e.tensor_tensor(kbf[:, a, sl], rta[:], rtb[:], ALU.subtract), waits=[s_v], inc=s_v)
                I("vector", lambda e, a=a, sl=sl: e.tensor_tensor(rta[:], kf32[:, a, :], sinK[:, sl], ALU.mult), waits=[s_v], inc=s_v)
                I("vector", lambda e, b2=b2, sl=sl: e.tensor_tensor(rtb[:], kf32[:, b2, :], cosK[:, sl], ALU.mult), waits=[s_v], inc=s_v)
                I("vector", lambda e, b2=b2, sl=sl: e.tensor_tensor(kbf[:, b2, sl], rta[:], rtb[:], ALU.add), waits=[s_v], inc=s_rope)
        for tc in range(4):
            for ocv in range(4):
                for hc in range(16):
                    w = []
                    if hc == 0:
                        w = [tg["xc2%d" % tc], tg["wkv2"]]
                        if tc == 0:
                            w.append((s_kevac, 16))
                        else:
                            w += [(s_kevac, 16 + tc * 4), (s_trv, tc * 16)]
                    I("tensor", lambda e, ocv=ocv, hc=hc, tc=tc: e.matmul(
                        pbank[ocv], wkvb[:, hc, ocv * 128:(ocv + 1) * 128], xbf[tc % 2][:, hc, :],
                        start=(hc == 0), stop=(hc == 15)),
                        waits=w, inc=(s_kvmm if hc == 15 else None))
                w = [(s_kvmm, 16 + tc * 4 + ocv + 1), (s_ld0, 64), (s_tr, tc * 16 + ocv * 4)]
                I("scalar", lambda e, ocv=ocv: e.activation(
                    vfe[:], pbank[ocv], AF.Identity, bias=bkv_t[:, 4 + ocv:5 + ocv]),
                    waits=w + [s_sc], inc=s_kevac)
                for kts in range(4):
                    kt = tc * 4 + kts
                    nt = tc * 16 + ocv * 4 + kts
                    w = [(s_kevac, 16 + tc * 4 + ocv + 1)]
                    if nt > 0:
                        w.append((s_trv, nt))
                    I("tensor", lambda e, kts=kts: e.transpose(
                        p0[:, 0:64].bitcast(BF16), vfe[:, kts * 128:(kts + 1) * 128], ident[:]),
                        waits=w, inc=s_tr)
                    h0 = 2 * ocv
                    I("vector", lambda e, kt=kt, h0=h0: e.tensor_copy(
                        vkm[:, kt, h0 * 65:h0 * 65 + 64], p0[:, 0:32].bitcast(BF16)),
                        waits=[(s_tr, tc * 16 + ocv * 4 + kts + 1), s_v], inc=s_v)
                    I("vector", lambda e, kt=kt, h0=h0: e.tensor_copy(
                        vkm[:, kt, (h0 + 1) * 65:(h0 + 1) * 65 + 64], p0[:, 32:64].bitcast(BF16)),
                        waits=[s_v], inc=s_trv)
        for h in range(8):
            I("vector", lambda e, h=h: e.memset(vkm[:, :, h * 65 + 64], 1.0), waits=[s_v], inc=s_v)

    phase_B()
    B.flush_block()
    es_b.close()

    if "qkv" in debug:
        s_dq = sem("dq")
        for i in range(16):
            I("gpsimd", lambda e, i=i: e.dma_start(out=dbg["d_q"][i], in_=qbf[:, i, :]), waits=[(s_rope, s_rope.n)], inc=s_dq, inck=16)
        for i in range(4):
            I("gpsimd", lambda e, i=i: e.dma_start(out=dbg["d_k"][i], in_=kbf[:, i, :]), inc=s_dq, inck=16)
        for i in range(16):
            I("gpsimd", lambda e, i=i: e.dma_start(out=dbg["d_v"][i], in_=vkm[:, i, :]), waits=[(s_trv, s_trv.n), (s_v, s_v.n)], inc=s_dq, inck=16)
        I("gpsimd", lambda e: e.nop(), waits=[s_dq])

    es_c = ExitStack()
    pbuf = sb("pbuf", [128, 2, 2048], BF16, es_c)
    seb = [sb("seb0", [128, 512], F32, es_c), sb("seb1", [128, 512], F32, es_c)]
    serow = sb("serow", [128, 512], F32, es_c)
    ones4 = sb("ones4", [128, 64], F32, es_c)
    sacc = sb("sacc", [128, 2048], F32, es_c)
    kbf2 = sb("kbf2", [128, 4, S], BF16, es_c)
    qbf2 = sb("qbf2", [128, 16, T], BF16, es_c)
    io_c2 = sb("io_c2", [128, 128], F32, es_c)
    selm = [sb("selm%d" % i, [128, 128], BF16, es_c) for i in range(4)]
    self32 = sb("self32", [128, 128], F32, es_c)
    rcol = sb("rcol", [128, 6], F32, es_c)
    icol = sb("icol", [128, 1], I32, es_c)

    s_smm = sem("smm")
    s_exp = sem("exp")
    s_acc = sem("acp")
    s_sum = sem("sum")
    s_pv = sem("pv")
    s_rec = sem("rec")
    s_sebc = sem("sebc")
    s_bc = sem("bc")
    s_aev = sem("aev")
    psum_bc = p7

    s_kq2 = sem("kq2")
    s_km = sem("km")

    def phase_C():
        I("vector", lambda e: e.memset(ones4[:], 1.0), inc=s_v)
        I("vector", lambda e: e.memset(p6[:], 0.0), waits=[s_v], inc=s_v)
        I("gpsimd", lambda e: e.iota(io_c2[:], [[1, 128]], base=0, channel_multiplier=0, allow_small_or_imprecise_dtypes=True), inc=s_g)
        for si, (h, jp) in enumerate(((0, 0), (0, 1), (1, 0), (1, 1))):
            c = [rcol[:, k:k + 1] for k in range(6)]
            I("vector", lambda e, jp=jp: e.tensor_scalar(c[0], scr[:, 1:2], float(64 - 64 * jp), 1.0 / 32.0, ALU.add, ALU.mult), waits=[(s_g, s_g.n), s_v], inc=s_v)
            I("vector", lambda e: e.tensor_copy(icol[:], c[0]), waits=[s_v], inc=s_v)
            I("vector", lambda e: e.tensor_copy(c[1], icol[:]), waits=[s_v], inc=s_v)
            I("vector", lambda e: e.tensor_tensor(c[2], c[1], c[0], ALU.is_gt), waits=[s_v], inc=s_v)
            I("vector", lambda e: e.tensor_tensor(c[1], c[1], c[2], ALU.subtract), waits=[s_v], inc=s_v)
            I("vector", lambda e: e.tensor_scalar(c[2], c[1], 1.5, None, ALU.is_gt), waits=[s_v], inc=s_v)
            I("vector", lambda e: e.tensor_scalar(c[3], c[1], 3.5, None, ALU.is_lt), waits=[s_v], inc=s_v)
            I("vector", lambda e: e.tensor_tensor(c[2], c[2], c[3], ALU.mult), waits=[s_v], inc=s_v)
            I("vector", lambda e, jp=jp, h=h: e.tensor_scalar(c[4], scr[:, 1:2], float(64 - 64 * jp + 32 * h - 128), None, ALU.add), waits=[s_v], inc=s_v)
            I("vector", lambda e: e.scalar_tensor_tensor(c[4], c[1], 32.0, c[4], ALU.mult, ALU.add), waits=[s_v], inc=s_v)
            I("vector", lambda e: e.tensor_tensor(c[4], c[4], c[2], ALU.mult), waits=[s_v], inc=s_v)
            I("vector", lambda e: e.tensor_tensor(c[4], c[4], c[2], ALU.add), waits=[s_v], inc=s_v)
            I("vector", lambda e: e.tensor_scalar(c[4], c[4], -1.0, None, ALU.add), waits=[s_v], inc=s_v)
            I("vector", lambda e: e.tensor_scalar(self32[:], io_c2[:], c[4], None, ALU.is_equal), waits=[s_v], inc=s_v)
            I("vector", lambda e, si=si: e.tensor_copy(selm[si][:], self32[:]), waits=[s_v], inc=s_v)
        nk2 = 0
        for g in range(2):
            for jp in range(2):
                for ns in range(4):
                    ksl2 = slice(ns * 512, (ns + 1) * 512)
                    pbk = pbank[nk2 % 4]
                    w = [(s_v, B.sems["sv"].n)]
                    if nk2 >= 4:
                        w.append((s_kq2, nk2 - 3))
                    I("tensor", lambda e, g=g, jp=jp, ksl2=ksl2, pbk=pbk: e.matmul(
                        pbk, selm[jp][:], kbf[:, g, ksl2], start=True, stop=False), waits=w)
                    I("tensor", lambda e, g=g, jp=jp, ksl2=ksl2, pbk=pbk: e.matmul(
                        pbk, selm[2 + jp][:], kbf[:, 2 + g, ksl2], start=False, stop=True), inc=s_km)
                    I("scalar", lambda e, g=g, jp=jp, ksl2=ksl2, pbk=pbk, nk2=nk2: e.activation(
                        kbf2[:, g * 2 + jp, ksl2], pbk, AF.Identity),
                        waits=[(s_km, nk2 + 1), s_sc], inc=s_kq2)
                    nk2 += 1
        for tau8 in range(8):
            for jp in range(2):
                for ns in range(2):
                    qsl2 = slice(ns * 512, (ns + 1) * 512)
                    pbk = pbank[nk2 % 4]
                    w = []
                    if nk2 >= 4:
                        w = [(s_kq2, nk2 - 3)]
                    I("tensor", lambda e, tau8=tau8, jp=jp, qsl2=qsl2, pbk=pbk: e.matmul(
                        pbk, selm[jp][:], qbf[:, tau8, qsl2], start=True, stop=False), waits=w)
                    I("tensor", lambda e, tau8=tau8, jp=jp, qsl2=qsl2, pbk=pbk: e.matmul(
                        pbk, selm[2 + jp][:], qbf[:, 8 + tau8, qsl2], start=False, stop=True), inc=s_km)
                    I("scalar", lambda e, tau8=tau8, jp=jp, qsl2=qsl2, pbk=pbk, nk2=nk2: e.activation(
                        qbf2[:, tau8 * 2 + jp, qsl2], pbk, AF.Identity),
                        waits=[(s_km, nk2 + 1), s_sc], inc=s_kq2)
                    nk2 += 1
        NKQ2 = nk2
        for u in range(16):
            tau, qc = u // 2, u % 2
            qsl = slice(qc * 512, (qc + 1) * 512)
            for kc in range(17):
                if kc < 16:
                    ksl = slice(kc * 128, (kc + 1) * 128)
                    for j in range(4):
                        jm, jp = j % 2, j // 2
                        w = []
                        if j in (0, 2):
                            if kc > 0:
                                w = [(s_exp, u * 32 + (kc - 1) * 2 + j // 2 + 1)]
                            elif j == 0 and u > 0:
                                w = [(s_exp, u * 32)]
                            elif j == 0:
                                w = [(s_kq2, NKQ2)]
                        I("tensor", lambda e, tau=tau, j=j, jm=jm, jp=jp, ksl=ksl, qsl=qsl: e.matmul(
                            pbank[j], kbf2[64 * jm:64 * jm + 64, (tau // 4) * 2 + jp, ksl],
                            qbf2[64 * jm:64 * jm + 64, tau * 2 + jp, qsl],
                            start=True, stop=True, tile_position=(64 * jm, 0)), waits=w, inc=s_smm)
                    for P in range(2):
                        w = [(s_smm, u * 64 + kc * 4 + 2 * (P + 1))]
                        if kc >= 1:
                            w.append((s_acc, u * 32 + (kc - 1) * 2))
                        elif u > 0:
                            w.append((s_acc, u * 32))
                        I("scalar", lambda e, P=P, kc=kc: e.activation(
                            pbuf[:, kc % 2, P * 1024:(P + 1) * 1024],
                            p03[:, 2 * P:2 * P + 2, :], AF.Exp, scale=0.125),
                            waits=w, inc=s_exp)
                    for P in range(2):
                        w = [(s_exp, u * 32 + kc * 2 + P + 1), s_v]
                        if kc == 0 and P == 0:
                            w.append((s_sum, u * 4))
                        if kc == 0:
                            I("vector", lambda e, P=P: e.tensor_copy(
                                sacc[:, P * 1024:(P + 1) * 1024], pbuf[:, 0, P * 1024:(P + 1) * 1024]),
                                waits=w, inc=s_acc)
                        else:
                            I("vector", lambda e, P=P, kc=kc: e.tensor_tensor(
                                sacc[:, P * 1024:(P + 1) * 1024], sacc[:, P * 1024:(P + 1) * 1024],
                                pbuf[:, kc % 2, P * 1024:(P + 1) * 1024], ALU.add),
                                waits=w, inc=s_acc)
                if kc >= 1:
                    kp = kc - 1
                    for pair in range(2):
                        for pi in range(2):
                            j = pair * 2 + pi
                            kv = 4 * (tau // 4) + j
                            w = []
                            if kp == 15 and pi == 0:
                                w.append((s_exp, u * 32 + kp * 2 + pair + 1))
                            if kp == 0 and pi == 0 and pair == 0:
                                w.append((s_aev, max(0, u * 2)))
                            I("tensor", lambda e, pair=pair, pi=pi, j=j, kv=kv, kp=kp: e.matmul(
                                pbank[4 + pair][pi * 64:(pi + 1) * 64, :],
                                vkm[:, kp, kv * 65:kv * 65 + 64],
                                pbuf[:, kp % 2, j * 512:(j + 1) * 512],
                                start=(kp == 0), stop=(kp == 15), tile_position=(0, pi * 64),
                                skip_group_check=True),
                                waits=w,
                                inc=(s_pv if kp == 15 else None))
            for j in range(4):
                w = [(s_acc, (u + 1) * 32)]
                if j == 0:
                    w.append((s_rec, u))
                I("tensor", lambda e, j=j: e.matmul(
                    p6[32 * j:32 * j + 1, :],
                    onescol_f[:], sacc[:, j * 512:(j + 1) * 512],
                    start=True, stop=True, tile_position=(0, 32 * j),
                    skip_group_check=True),
                    waits=w, inc=s_sum)
            w = [(s_sum, (u + 1) * 4), s_v]
            if u > 0:
                w.append((s_bc, u * 2))
            I("vector", lambda e: e.tensor_copy(serow[:], p6[:]), waits=w, inc=s_rec)
            for pair in range(2):
                for pi in range(2):
                    j = pair * 2 + pi
                    w = [(s_rec, u + 1)]
                    if pi == 0 and pair == 0:
                        w.append((s_sebc, u * 2))
                    elif pi == 0:
                        w.append((s_sebc, u * 2 + 1))
                    I("tensor", lambda e, j=j, pi=pi: e.matmul(
                        psum_bc[pi * 64:(pi + 1) * 64, :],
                        ones4[32 * j:32 * j + 1, 0:64],
                        serow[32 * j:32 * j + 1, :],
                        start=True, stop=True, tile_position=(32 * j, pi * 64),
                        skip_group_check=True),
                        waits=w, inc=(s_bc if pi == 1 else None))
                w = [(s_bc, u * 2 + pair + 1)]
                if u > 0:
                    w.append((s_aev, (u - 1) * 2 + pair + 1))
                I("vector", lambda e, pair=pair: e.reciprocal_approx_fast(seb[pair][:], psum_bc[:]),
                    waits=w + [s_v], inc=s_sebc)
                w = [(s_pv, u * 4 + pair * 2 + 2)]
                I("vector", lambda e, tau=tau, pair=pair, qsl=qsl: e.tensor_tensor(
                    attnb[:, tau * 2 + pair, qsl], pbank[4 + pair], seb[pair][:], ALU.mult),
                    waits=w + [s_v], inc=s_aev)

    phase_C()
    B.flush_block()

    if "attn" in debug:
        d_at = nc.dram_tensor("d_at", [16, 128, T], F32, kind="ExternalOutput").ap()
        dbg["d_at"] = d_at
        s_da = sem("da")
        for i in range(16):
            I("gpsimd", lambda e, i=i: e.dma_start(out=d_at[i], in_=attnb[:, i, :]), waits=[(s_aev, 32)], inc=s_da, inck=16)
        I("gpsimd", lambda e: e.nop(), waits=[s_da])
        B.flush_block()
    es_c.close()
    es_attn.close()

    es_d = ExitStack()
    wos = [sb("wos%d" % i, [128, 16, 128], BF16, es_d) for i in range(3)]
    r_t = sb("r_t", [128, 16, T], F32, es_d)
    xr = [sb("xr0", [128, T], F32, es_d), sb("xr1", [128, T], F32, es_d)]
    rsq = [sb("rsq0", [128, 512], F32, es_d), sb("rsq1", [128, 512], F32, es_d)]
    lnrow = sb("lnrow", [1, 4 * T], F32, es_d)
    abB = sb("abB", [128, 2 * T], F32, es_d)
    xnb = attnb
    epsr = sb("epsr", [1, 1], F32, es_d)

    s_xrl = [sem("xrl%d" % i) for i in range(2)]
    s_omm = sem("omm")
    s_resid = sem("resid")
    s_sq = sem("sq")
    s_sqm = sem("sqm")
    s_lnm = sem("lnm")
    s_ab = sem("ab")
    s_xn = sem("xn")

    def phase_D1():
        for oc in range(16):
            w = [tg["wq%d" % (13 + oc % 3)]] if oc < 3 else [tg["wo%d" % (oc - 3)]]
            if oc >= 3:
                w.append((s_omm, (oc - 2) * 2))
            I("gpsimd", lambda e, oc=oc: e.dma_start(
                out=wos[oc % 3][:],
                in_=bass.AP(wo.tensor, oc * 128, [[D, 128], [128 * D, 16], [1, 128]])),
                waits=w, inc=swd_w[oc % 3], inck=16)
            tg["wo%d" % oc] = (swd_w[oc % 3], swd_w[oc % 3].n)
            I("sync", lambda e, oc=oc: e.dma_start(
                out=xr[oc % 2][:],
                in_=bass.AP(xqT.tensor, oc * 128 * T, [[T, 128], [1, T]])),
                waits=([(s_resid, (oc - 1) * 2)] if oc >= 2 else []),
                inc=s_xrl[oc % 2], inck=16)
        for oc in range(16):
            for tc in range(2):
                tsl = slice(tc * 512, (tc + 1) * 512)
                pq = (p0, p1)[tc]
                for fc in range(16):
                    w = []
                    if fc == 0:
                        w = [tg["wo%d" % oc]]
                        if oc > 0:
                            w.append((s_resid, (oc - 1) * 2 + tc + 1))
                    I("tensor", lambda e, oc=oc, fc=fc, tsl=tsl, pq=pq: e.matmul(
                        pq[:], wos[oc % 3][:, fc, :], attnb[:, fc, tsl],
                        start=(fc == 0), stop=(fc == 15)),
                        waits=w, inc=(s_omm if fc == 15 else None))
                I("vector", lambda e, oc=oc, tc=tc, tsl=tsl, pq=pq: e.scalar_tensor_tensor(
                    r_t[:, oc, tsl], pq[:], bo_t[:, oc:oc + 1], xr[oc % 2][:, tsl],
                    ALU.add, ALU.add),
                    waits=[(s_omm, oc * 2 + tc + 1), (s_xrl[oc % 2], 16 * (oc // 2 + 1)), s_v],
                    inc=s_resid)
                I("scalar", lambda e, oc=oc, tc=tc, tsl=tsl: e.activation(
                    rsq[tc][:], r_t[:, oc, tsl], AF.Square),
                    waits=[(s_resid, oc * 2 + tc + 1), (s_sqm, max(0, (oc - 1) * 2 + tc + 1)), s_sc],
                    inc=s_sq)
                pm = (p2, p3)[tc]
                psq = (p4, p5)[tc]
                I("tensor", lambda e, oc=oc, tsl=tsl, pm=pm: e.matmul(
                    pm[0:1, :], onescol_f[:], r_t[:, oc, tsl],
                    start=(oc == 0), stop=(oc == 15)),
                    waits=[(s_resid, oc * 2 + tc + 1)],
                    inc=(s_lnm if oc == 15 else None))
                I("tensor", lambda e, oc=oc, tc=tc, psq=psq: e.matmul(
                    psq[0:1, :], onescol_f[:], rsq[tc][:],
                    start=(oc == 0), stop=(oc == 15)),
                    waits=[(s_sq, oc * 2 + tc + 1)],
                    inc=s_sqm)
        mrow = lnrow[0:1, 0:T]
        sqrow = lnrow[0:1, T:2 * T]
        arow = lnrow[0:1, 2 * T:3 * T]
        brow = lnrow[0:1, 3 * T:4 * T]
        I("vector", lambda e: e.memset(epsr[:], 1e-5), waits=[s_v], inc=s_v)
        I("vector", lambda e: e.tensor_copy(lnrow[0:1, 0:512], p2[0:1, :]), waits=[(s_lnm, 2), (s_sqm, 32), s_v], inc=s_v)
        I("vector", lambda e: e.tensor_copy(lnrow[0:1, 512:1024], p3[0:1, :]), waits=[s_v], inc=s_v)
        I("vector", lambda e: e.tensor_copy(lnrow[0:1, T:T + 512], p4[0:1, :]), waits=[s_v], inc=s_v)
        I("vector", lambda e: e.tensor_copy(lnrow[0:1, T + 512:2 * T], p5[0:1, :]), waits=[s_v], inc=s_v)
        I("vector", lambda e: e.tensor_scalar(mrow, mrow, 1.0 / D, None, ALU.mult), waits=[s_v], inc=s_v)
        I("vector", lambda e: e.tensor_scalar(sqrow, sqrow, 1.0 / D, None, ALU.mult), waits=[s_v], inc=s_v)
        I("vector", lambda e: e.scalar_tensor_tensor(arow, mrow, 1.0, mrow, ALU.mult, ALU.mult), waits=[s_v], inc=s_v)
        I("vector", lambda e: e.tensor_tensor(sqrow, sqrow, arow, ALU.subtract), waits=[s_v], inc=s_v)
        I("scalar", lambda e: e.activation(arow, sqrow, AF.Sqrt, bias=epsr[0:1, 0:1]), waits=[s_v, s_sc], inc=s_sc)
        I("vector", lambda e: e.reciprocal_approx_fast(arow, arow), waits=[s_sc, s_v], inc=s_v)
        I("vector", lambda e: e.scalar_tensor_tensor(brow, mrow, -1.0, arow, ALU.mult, ALU.mult), waits=[s_v], inc=s_v)
        I("vector", lambda e: e.nop(), waits=[s_v], inc=s_ab)
        I("sync", lambda e: e.dma_start(out=hbm_ab[0], in_=arow), waits=[s_ab], inc=s_ab, inck=16)
        I("sync", lambda e: e.dma_start(out=hbm_ab[1], in_=brow), inc=s_ab, inck=16)
        I("sync", lambda e: e.nop(), waits=[s_ab])
        I("sync", lambda e: e.dma_start(out=abB[:], in_=bass.AP(hbm_ab.tensor, 0, [[0, 128], [1, 2 * T]])), inc=s_ab, inck=16)
        for oc in range(16):
            I("vector", lambda e, oc=oc: e.tensor_tensor(xr[0][:], r_t[:, oc, :], abB[:, 0:T], ALU.mult),
                waits=[(s_ab, 49), s_v], inc=s_v)
            I("vector", lambda e, oc=oc: e.tensor_tensor(xnb[:, oc, :], xr[0][:], abB[:, T:2 * T], ALU.add),
                waits=[s_v], inc=s_xn)

    phase_D1()
    B.flush_block()
    if "xn" in debug:
        dbg["d_xn"] = nc.dram_tensor("d_xn", [16, 128, T], F32, kind="ExternalOutput").ap()
        s_dxn = sem("dxn")
        for oc in range(16):
            I("gpsimd", lambda e, oc=oc: e.dma_start(out=dbg["d_xn"][oc], in_=xnb[:, oc, :]), inc=s_dxn, inck=16)
        I("gpsimd", lambda e: e.nop(), waits=[s_dxn])
        B.flush_block()

    es_d.close()
    es_moe = ExitStack()
    gidx = sb("gidx", [128, 32 * NE], I16, es_moe)
    gidxf = sb("gidxf", [128, 32 * NE], F32, es_moe)
    cwsel = sb("cwsel", [128, 8, K_TOP], F32, es_moe)
    epsF = sb("epsF", [128, 1], F32, es_moe)
    yidxw = sb("yidxw", [128, 512], I16, es_moe)
    es_w = ExitStack()
    wring = [sb("wring%d" % i, [128, 12288], BF16, es_w) for i in range(3)]
    es_d2 = ExitStack()
    xnt = [sb("xnt0", [128, D], BF16, es_d2), sb("xnt1", [128, D], BF16, es_d2)]
    zf = sb("zf", [128, 8, NE], F32, es_d2)
    iotaE = sb("iotaE", [128, NE], F32, es_d2)
    ones32 = sb("ones32", [128, NE], F32, es_d2)
    onesT = sb("onesT", [128, 128], BF16, es_d2)
    indb = sb("indb", [128, 8, NE], BF16, es_d2)
    rkm = sb("rkm", [128, 8, NE], F32, es_d2)
    cwu = sb("cwu", [128, 8, NE], F32, es_d2)
    wsumr = sb("wsumr", [128, 8], F32, es_d2)
    pposF = sb("pposF", [128, 8, NE], F32, es_d2)
    ra = sb("ra", [128, 64], F32, es_d2)
    rb = sb("rb", [128, NE], F32, es_d2)
    rc = sb("rc", [128, NE], F32, es_d2)
    identF = sb("identF", [128, 128], F32, es_d2)
    slotT = sb("slotT", [8, 1024], F32, es_d2)
    flb = sb("flb", [128, 16], F32, es_d2)
    cntrow = sb("cntrow", [1, NE], F32, es_d2)
    ccs = sb("ccs", [8, NE], F32, es_d2)
    pidf = sb("pidf", [8, 1], F32, es_d2)
    iota8 = sb("iota8", [8, 1], F32, es_d2)
    mask8 = sb("mask8", [8, 1], F32, es_d2)
    cprow = sb("cprow", [1, NE], F32, es_d2)
    cpB = sb("cpB", [128, NE], F32, es_d2)
    stage = sb("stage", [128, 64, 64], F32, es_d2)
    idxw = sb("idxw", [128, 512], I16, es_d2)
    ztile = sb("ztile", [128, 2050], F32, es_d2)
    pid_u = sb("pid_u", [8, 1], U32, es_d2)
    hbm_cp = nc.dram_tensor("hbm_cp", [1, NE], F32, kind="Internal").ap()
    cc_in = nc.dram_tensor("cc_in", [1, NE], F32, kind="Internal").ap()
    cc_out = nc.dram_tensor("cc_out", [8, NE], F32, kind="Internal", addr_space=("Shared" if cc_shared else "Local")).ap()

    s_z = sem("sz")
    s_zev = sem("zev")
    s_pmm = sem("pmm")
    s_ppe = sem("ppe")
    s_cnt = sem("cnt")
    s_xtr = sem("xtr")
    s_xtv = sem("xtv")
    s_xnw = sem("xnw")
    s_psw = sem("psw")
    s_idr = sem("idr")
    s_cc = sem("scc")
    s_ccd = sem("ccd")
    s_cpd = sem("cpd")
    s_keep = sem("keep")
    s_stg = sem("stg")
    s_sca = sem("sca")
    s_rz = sem("srz")
    s_gx = sem("gxr")
    s_stv = sem("stv")
    s_iev = sem("iev")

    regCC = nc.gpsimd.alloc_register()
    regT = nc.gpsimd.alloc_register()
    reg1k = nc.gpsimd.alloc_register()

    def phase_D2():
        I("gpsimd", lambda e: e.reg_mov(regCC, CC))
        I("gpsimd", lambda e: e.reg_mov(regT, T))
        I("gpsimd", lambda e: e.reg_mov(reg1k, 1024))
        for t, hd in ((0, wgd), (1, wud)):
            I("gpsimd", lambda ee, t=t, hd=hd: ee.dma_start(
                out=wring[t][:], in_=bass.AP(hd.tensor, 0, [[16 * DI, 128], [1, 16 * DI]])),
                waits=[tg["wo%d" % (13 + t)]], inc=swd_w[t], inck=16)
            tg["we0_%d" % t] = (swd_w[t], swd_w[t].n)
        for tt in range(8):
            for oc in range(16):
                w = [(s_xn, oc + 1)]
                nt = tt * 16 + oc
                if nt > 1:
                    w.append((s_xtv, nt - 1))
                pb = (p2, p3)[nt % 2]
                I("tensor", lambda e, tt=tt, oc=oc, pb=pb: e.transpose(
                    pb[:, 0:64].bitcast(BF16), xnb[:, oc, tt * 128:(tt + 1) * 128], ident[:]),
                    waits=w, inc=s_xtr)
                w2 = [(s_xtr, tt * 16 + oc + 1), s_v]
                if tt >= 2 and oc == 0:
                    w2.append((s_xnw, 16 * (tt - 1)))
                I("vector", lambda e, tt=tt, oc=oc, pb=pb: e.tensor_copy(
                    xnt[tt % 2][:, oc * 128:(oc + 1) * 128], pb[:, 0:64].bitcast(BF16)),
                    waits=w2, inc=s_xtv)
            I("sync", lambda e, tt=tt: e.dma_start(
                out=bass.AP(hbm_xn.tensor, tt * 128 * D, [[D, 128], [1, D]]),
                in_=xnt[tt % 2][:]),
                waits=[(s_xtv, (tt + 1) * 16), (s_xnw, 16 * tt)], inc=s_xnw, inck=16)

        I("gpsimd", lambda e: e.iota(iotaE[:], [[1, NE]], base=0, channel_multiplier=0, allow_small_or_imprecise_dtypes=True), inc=s_g)
        I("gpsimd", lambda e: e.iota(iota8[:], [[0, 1]], base=0, channel_multiplier=1, allow_small_or_imprecise_dtypes=True), inc=s_g)
        I("vector", lambda e: e.memset(ones32[:], 1.0), waits=[s_v], inc=s_v)
        I("vector", lambda e: e.memset(onesT[:], 1.0), waits=[s_v], inc=s_v)
        I("vector", lambda e: e.memset(stage[:], 0.0), waits=[s_v], inc=s_stg)
        I("vector", lambda e: e.memset(ztile[:], 0.0), waits=[s_v], inc=s_stg)
        I("vector", lambda e: e.memset(epsF[:], 1e-5), waits=[s_v], inc=s_v)
        I("vector", lambda e: e.tensor_copy(identF[:], ident[:]), waits=[s_v], inc=s_v)

        for tau in range(8):
            tsl = slice(tau * 128, (tau + 1) * 128)
            for hc in range(16):
                w = [(s_xn, hc + 1), (s_gw, 16)]
                if hc == 0 and tau > 0:
                    w.append((s_zev, tau))
                I("tensor", lambda e, hc=hc, tsl=tsl: e.matmul(
                    p6[:, 0:NE], xnb[:, hc, tsl], gwb[:, hc, :],
                    start=(hc == 0), stop=(hc == 15)),
                    waits=w, inc=(s_z if hc == 15 else None))
            I("vector", lambda e, tau=tau: e.tensor_copy(zf[:, tau, :], p6[:, 0:NE]),
                waits=[(s_z, tau + 1), s_v], inc=s_zev)
            I("vector", lambda e, tau=tau: e.tensor_reduce(ra[:, 0:1], zf[:, tau, :], mybir.AxisListType.X, ALU.max), waits=[s_v], inc=s_v)
            I("vector", lambda e: e.tensor_scalar(ra[:, 1:2], ra[:, 0:1], -1.0, None, ALU.mult), waits=[s_v], inc=s_v)
            I("scalar", lambda e, tau=tau: e.activation(rb[:], zf[:, tau, :], AF.Exp, bias=ra[:, 1:2], accum_out=ra[:, 2:3]), waits=[s_v, s_sc], inc=s_sc)
            I("vector", lambda e: e.reciprocal(ra[:, 3:4], ra[:, 2:3]), waits=[s_sc, s_v], inc=s_v)
            I("vector", lambda e: e.tensor_scalar(rb[:], rb[:], ra[:, 3:4], None, ALU.mult), waits=[s_v], inc=s_v)
            I("vector", lambda e: e.max(out=ra[:, 8:16], in_=rb[:]), waits=[s_v], inc=s_v)
            I("vector", lambda e: e.match_replace(out=rc[:], in_to_replace=ra[:, 8:16], in_values=rb[:], imm_value=0.0), waits=[s_v], inc=s_v)
            I("vector", lambda e, tau=tau: e.tensor_tensor(cwu[:, tau, :], rb[:], rc[:], ALU.subtract), waits=[s_v], inc=s_v)
            I("vector", lambda e, tau=tau: e.tensor_reduce(wsumr[:, tau:tau + 1], ra[:, 8:16], mybir.AxisListType.X, ALU.add), waits=[s_v], inc=s_v)
            I("vector", lambda e, tau=tau: e.tensor_scalar(ra[:, 16:48], cwu[:, tau, :], 0.0, None, ALU.is_gt), waits=[s_v], inc=s_v)
            I("vector", lambda e, tau=tau: e.tensor_copy(indb[:, tau, :], ra[:, 16:48]), waits=[s_v], inc=s_v)
            I("vector", lambda e: e.tensor_tensor_scan(rc[:], ones32[:], ra[:, 16:48], 0.0, ALU.mult, ALU.add), waits=[s_v], inc=s_v)
            I("vector", lambda e, tau=tau: e.tensor_tensor(rc[:], rc[:], ra[:, 16:48], ALU.mult), waits=[s_v], inc=s_v)
            I("vector", lambda e, tau=tau: e.tensor_scalar(rkm[:, tau, :], rc[:], -1.0, None, ALU.add), waits=[s_v], inc=s_v)
            for tp in range(tau + 1):
                w = [(s_v, B.sems["sv"].n)]
                if tp == 0 and tau > 0:
                    w.append((s_ppe, tau))
                I("tensor", lambda e, tau=tau, tp=tp: e.matmul(
                    p7[:, 0:NE],
                    (triu if tp == tau else onesT)[:],
                    indb[:, tp, :],
                    start=(tp == 0), stop=(tp == tau)),
                    waits=w, inc=(s_pmm if tp == tau else None))
            I("vector", lambda e, tau=tau: e.tensor_copy(pposF[:, tau, :], p7[:, 0:NE]),
                waits=[(s_pmm, tau + 1), s_v], inc=s_ppe)
            I("tensor", lambda e, tau=tau: e.matmul(
                p5[0:1, 0:NE], onescol_b[:], indb[:, tau, :],
                start=(tau == 0), stop=(tau == 7)),
                waits=[(s_v, B.sems["sv"].n)], inc=(s_cnt if tau == 7 else None))
            I("vector", lambda e, tau=tau: e.scalar_tensor_tensor(ra[:, 16:48], iotaE[:], float(CC), pposF[:, tau, :], ALU.mult, ALU.add), waits=[(s_ppe, tau + 1), (s_g, 6), s_v], inc=s_v)
            I("vector", lambda e, tau=tau: e.tensor_scalar(rc[:], pposF[:, tau, :], float(CC), None, ALU.is_lt), waits=[s_v], inc=s_v)
            I("vector", lambda e: e.tensor_scalar(ra[:, 16:48], ra[:, 16:48], float(-NSLOT), None, ALU.add), waits=[s_v], inc=s_v)
            I("vector", lambda e: e.tensor_tensor(ra[:, 16:48], ra[:, 16:48], rc[:], ALU.mult), waits=[s_v], inc=s_v)
            I("vector", lambda e: e.tensor_scalar(ra[:, 16:48], ra[:, 16:48], float(NSLOT), None, ALU.add), waits=[s_v], inc=s_v)
            for k in range(K_TOP):
                w = [s_v]
                if k == 0:
                    w.append((s_stv, tau))
                I("vector", lambda e, tau=tau, k=k: e.tensor_scalar(rc[:], rkm[:, tau, :], float(k), None, ALU.is_equal), waits=w, inc=s_v)
                I("vector", lambda e: e.tensor_tensor(rb[:], ra[:, 16:48], rc[:], ALU.mult), waits=[s_v], inc=s_v)
                I("vector", lambda e, tau=tau, k=k: e.tensor_reduce(ra[:, 48 + k:49 + k], rb[:], mybir.AxisListType.X, ALU.add), waits=[s_v], inc=s_v)
            I("tensor", lambda e, tau=tau: e.transpose(p4[0:8, 0:128], ra[:, 48:56], identF[:]),
                waits=[(s_v, B.sems["sv"].n)], inc=s_stv)
            I("vector", lambda e, tau=tau: e.tensor_copy(slotT[:, tau * 128:(tau + 1) * 128], p4[0:8, 0:128]),
                waits=[(s_stv, tau + 1), s_v], inc=s_v)
        I("sync", lambda e: e.dma_start(out=bass.AP(hbm_ps.tensor, 0, [[1024, 8], [1, 1024]]), in_=slotT[:]),
            waits=[(s_v, B.sems["sv"].n)], inc=s_psw, inck=16)
        I("sync", lambda e: e.dma_start(out=bass.AP(hbm_yi.tensor, 0, [[128, 8], [1024, 8], [1, 128]]), in_=slotT[:]),
            waits=[(s_v, B.sems["sv"].n)], inc=s_psw, inck=16)
        I("vector", lambda e: e.memset(idxw[:], 0), waits=[s_v], inc=s_iev)
        I("vector", lambda e: e.memset(yidxw[:], 0), waits=[s_v], inc=s_iev)
        for gi, (srcT, dstT) in enumerate(((hbm_ps, idxw), (hbm_yi, yidxw))):
            for blk in range(4):
                w = [(s_psw, 32), (s_stv, s_stv.n)]
                if gi == 0 and blk == 0:
                    w.append((s_iev, 2))
                I("sync", lambda e, srcT=srcT, blk=blk: e.dma_start(
                    out=flb[:], in_=bass.AP(srcT.tensor, blk * 2048, [[16, 128], [1, 16]])),
                    waits=w, inc=s_idr, inck=16)
                I("tensor", lambda e: e.transpose(p4[0:16, 0:128], flb[:], identF[:]),
                    waits=[(s_idr, s_idr.n), (s_iev, s_iev.n)], inc=s_stv)
                I("vector", lambda e, dstT=dstT, blk=blk: e.tensor_copy(
                    dstT[0:16, blk * 128:(blk + 1) * 128], p4[0:16, 0:128]),
                    waits=[(s_stv, s_stv.n), s_v], inc=s_iev)
        I("sync", lambda e: e.dma_start(out=idxw[16:32, :], in_=idxw[0:16, :]),
            waits=[(s_iev, 6), (s_idr, 128)], inc=s_idr, inck=16)
        I("sync", lambda e: e.dma_start(out=yidxw[16:32, :], in_=yidxw[0:16, :]),
            waits=[(s_iev, 10), (s_idr, 144)], inc=s_idr, inck=16)
        I("sync", lambda e: e.dma_start(out=yidxw[32:64, :], in_=yidxw[0:32, :]),
            waits=[(s_idr, 160)], inc=s_idr, inck=16)

        I("vector", lambda e: e.tensor_copy(cntrow[:], p5[0:1, 0:NE]), waits=[(s_cnt, 1), s_v], inc=s_v)
        I("sync", lambda e: e.dma_start(out=cc_in[:], in_=cntrow[:]), waits=[(s_v, B.sems["sv"].n)], inc=s_ccd, inck=16)
        I("gpsimd", lambda e: e.collective_compute(
            "AllGather", ALU.bypass, replica_groups=[list(range(8))],
            ins=[cc_in], outs=[cc_out]), waits=[(s_ccd, 16)], inc=s_cc)
        I("sync", lambda e: e.dma_start(out=ccs[:], in_=cc_out[:]), waits=[(s_cc, 1)], inc=s_ccd, inck=16)
        I("sync", lambda e: e.dma_start(out=pid_u[:], in_=bass.AP(nc.partition_id_tensor, 0, [[0, 8], [1, 1]])), inc=s_cpd, inck=16)
        I("vector", lambda e: e.tensor_copy(pidf[:], pid_u[:]), waits=[(s_cpd, 16), s_v], inc=s_v)
        I("vector", lambda e: e.tensor_tensor(mask8[:], iota8[:], pidf[:], ALU.is_lt), waits=[(s_g, s_g.n), s_v], inc=s_v)
        I("tensor", lambda e: e.matmul(p6[0:1, 0:NE], mask8[:], ccs[:], start=True, stop=True),
            waits=[(s_v, B.sems["sv"].n), (s_ccd, 32), (s_zev, 8)], inc=s_z)
        I("vector", lambda e: e.tensor_copy(cprow[:], p6[0:1, 0:NE]), waits=[(s_z, 9), s_v], inc=s_v)
        I("sync", lambda e: e.dma_start(out=hbm_cp[:], in_=cprow[:]), waits=[(s_v, B.sems["sv"].n)], inc=s_cpd, inck=16)
        I("sync", lambda e: e.dma_start(out=cpB[:], in_=bass.AP(hbm_cp.tensor, 0, [[0, 128], [1, NE]])), waits=[(s_cpd, 32)], inc=s_cpd, inck=16)

        for tau in range(8):
            I("vector", lambda e, tau=tau: e.tensor_tensor(ra[:, 16:48], pposF[:, tau, :], cpB[:], ALU.add), waits=[(s_cpd, 48), s_v], inc=s_v)
            I("vector", lambda e: e.tensor_scalar(ra[:, 16:48], ra[:, 16:48], float(CAP), None, ALU.is_lt), waits=[s_v], inc=s_v)
            I("vector", lambda e, tau=tau: e.tensor_scalar(rc[:], pposF[:, tau, :], float(CC), None, ALU.is_lt), waits=[s_v], inc=s_v)
            I("vector", lambda e: e.tensor_tensor(ra[:, 16:48], ra[:, 16:48], rc[:], ALU.mult), waits=[s_v], inc=s_v)
            I("vector", lambda e, tau=tau: e.tensor_tensor(ra[:, 16:48], ra[:, 16:48], cwu[:, tau, :], ALU.mult), waits=[s_v], inc=s_v)
            I("vector", lambda e, tau=tau: e.reciprocal(ra[:, 0:1], wsumr[:, tau:tau + 1]), waits=[s_v], inc=s_v)
            I("vector", lambda e: e.tensor_scalar(ra[:, 16:48], ra[:, 16:48], ra[:, 0:1], None, ALU.mult), waits=[s_v], inc=s_v)
            for k in range(K_TOP):
                I("vector", lambda e, tau=tau, k=k: e.tensor_scalar(rc[:], rkm[:, tau, :], float(k), None, ALU.is_equal), waits=[s_v], inc=s_v)
                I("vector", lambda e: e.tensor_tensor(rb[:], ra[:, 16:48], rc[:], ALU.mult), waits=[s_v], inc=s_v)
                I("vector", lambda e, tau=tau, k=k: e.tensor_reduce(cwsel[:, tau, k:k + 1], rb[:], mybir.AxisListType.X, ALU.add), waits=[s_v], inc=(s_keep if k == 7 else s_v))

        for tau in range(8):
            I("gpsimd", lambda e, tau=tau: e.iota(
                bass.AP(stage, tau * 64, [[4096, 128], [512, 8], [1, 1]]),
                [[0, 8], [0, 1]], base=tau * 128, channel_multiplier=1,
                allow_small_or_imprecise_dtypes=True),
                waits=[(s_stg, 2)], inc=s_g)
        for zi in range(4):
            I("sync", lambda e, zi=zi: e.dma_start(
                out=bass.AP(hbm_roster.tensor, zi * 128 * 2050, [[2050, 128], [1, 2050]]),
                in_=ztile[:]),
                waits=[(s_stg, 2)], inc=s_rz, inck=16)
        I("gpsimd", lambda e: e.load_library(library_config.mlp), waits=[(s_g, s_g.n)])
        for ch in range(8):
            w = [(s_rz, 64), (s_idr, 144), (s_iev, 6), (s_g, s_g.n)]
            w.append(tg["wkv2"] if ch == 0 else tg["sca%d" % (ch - 1)])
            B.G(lambda e, h, ch=ch: e.dma_scatter_add(
                hbm_roster[:], stage[:, 8 * ch:8 * (ch + 1), :], idxw[:, 64 * ch:64 * (ch + 1)],
                1024, reg1k, 64, prepare_only=True, sem=h),
                swd_m, waits=w)
            tg["sca%d" % ch] = (swd_m, swd_m.n)
        I("vector", lambda e: e.memset(gidxf[:], 0.0), waits=[s_v], inc=s_gx)
        I("vector", lambda e: e.memset(gidx[:], 0), waits=[s_v], inc=s_gx)

    phase_D2()
    B.flush_block()
    es_d2.close()
    if "route" in debug:
        dbg["d_roster"] = nc.dram_tensor("d_roster", [NSLOT_PAD, 2], F32, kind="ExternalOutput").ap()
        s_dr = sem("dr")
        I("gpsimd", lambda e: e.dma_start(out=dbg["d_roster"][0:8200], in_=bass.AP(hbm_roster.tensor, 0, [[64, 8200], [1, 2]])), waits=[tg["sca7"]], inc=s_dr, inck=16)
        I("gpsimd", lambda e: e.dma_start(out=dbg["d_roster"][8200:16400], in_=bass.AP(hbm_roster.tensor, 8200 * 64, [[64, 8200], [1, 2]])), waits=[(s_dr, 16)], inc=s_dr, inck=16)
        I("gpsimd", lambda e: e.nop(), waits=[s_dr])
        B.flush_block()

    es_e = ExitStack()
    xg = [sb("xg0", [128, 16, CC], BF16, es_e), sb("xg1", [128, 16, CC], BF16, es_e)]
    sg = sb("sg", [128, 6, CC], BF16, es_e)
    au = sb("au", [128, 6, CC], BF16, es_e)
    ystage = [bass.AP(attnb, 0, [[16384, 128], [2048, 4], [1, 2048]]),
              bass.AP(attnb, 8192, [[16384, 128], [2048, 4], [1, 2048]])]


    s_gxc = sem("gxc")
    s_gu = sem("sgu")
    s_si = sem("ssi")
    s_am = sem("sam")
    s_dm = sem("sdm")
    s_ye = sem("sye")
    s_yw = sem("syw")

    def phase_E():
        I("sync", lambda ee: ee.dma_start(
            out=gidxf[0:16, :],
            in_=bass.AP(hbm_roster.tensor, 0, [[64, 16], [1024, 32 * NE]]),
            allow_slow_non_contiguous=True),
            waits=[tg["sca7"], (s_gx, 2)], inc=s_gx, inck=16)
        I("sync", lambda ee: ee.dma_start(
            out=gidxf[16:32, :],
            in_=bass.AP(hbm_roster.tensor, 0, [[64, 16], [1024, 32 * NE]]),
            allow_slow_non_contiguous=True),
            inc=s_gx, inck=16)
        I("vector", lambda ee: ee.tensor_copy(gidx[0:32, :], gidxf[0:32, :]),
            waits=[(s_gx, 2 + 32), s_v], inc=s_gxc)
        I("sync", lambda ee: ee.dma_start(out=gidx[32:64, :], in_=gidx[0:32, :]),
            waits=[(s_gxc, 1)], inc=s_gx, inck=16)
        for e in range(NE):
            par = e % 2
            w = [(s_gxc, 1), (s_gx, 2 + 32 + 16), (s_xnw, 128)]
            if e >= 2:
                w.append((s_gu, (e - 1) * 12))
                w.append(tg["xg%d" % (e - 2)])
            else:
                w.append(tg["xc2%d" % (2 + par)])
            B.G(lambda ee, h, e=e, par=par: ee.dma_gather(
                xg[par][:], hbm_xn[:], gidx[:, 32 * e:32 * (e + 1)], CC, regCC, D,
                transpose=True, prepare_only=True, sem=h, queue_num=1),
                swd_y[par], waits=w, queue_num=1)
            tg["xg%d" % e] = (swd_y[par], swd_y[par].n)
            for t, (hd, inner) in enumerate(((wgd, DI), (wud, DI), (wdd, D))):
                if e == 0 and t < 2:
                    continue
                b = t
                w = []
                if e >= 1:
                    if t == 0:
                        w = [(s_gu, (e - 1) * 12 + 11)]
                    elif t == 1:
                        w = [(s_gu, (e - 1) * 12 + 12)]
                    else:
                        w = [(s_dm, e * 16)]
                    w.append(tg["we%d_%d" % (e - 1, t)])
                else:
                    w.append(tg["wo%d" % (13 + t)])
                if t < 2:
                    ap = bass.AP(hd.tensor, e * D * DI, [[16 * DI, 128], [1, 16 * DI]])
                else:
                    ap = bass.AP(hd.tensor, e * DI * D, [[6 * D, 128], [1, 6 * D]])
                I("gpsimd", lambda ee, b=b, ap=ap: ee.dma_start(out=wring[b][:], in_=ap),
                    waits=w, inc=swd_w[b], inck=16)
                tg["we%d_%d" % (e, t)] = (swd_w[b], swd_w[b].n)
            wgu = [tg["we%d_0" % e], tg["we%d_1" % e], tg["we%d_2" % e]]
            for ic in range(6):
                pg = (p0, p1)[ic % 2]
                pu = (p2, p3)[ic % 2]
                for hc in range(16):
                    w = []
                    if hc == 0:
                        w = [wgu[0], tg["xg%d" % e]]
                        w.append((s_si, e * 6 + ic - 1) if ic >= 2 else (s_si, e * 6))
                    I("tensor", lambda ee, e=e, ic=ic, hc=hc, pg=pg, par=par: ee.matmul(
                        pg[:, 0:CC], wring[0][:, hc * DI + ic * 128: hc * DI + ic * 128 + 128],
                        xg[par][:, hc, :], start=(hc == 0), stop=(hc == 15)),
                        waits=w, inc=(s_gu if hc == 15 else None))
                for hc in range(16):
                    w = []
                    if hc == 0:
                        w = [wgu[1]]
                        w.append((s_am, e * 6 + ic - 1) if ic >= 2 else (s_am, e * 6))
                    I("tensor", lambda ee, e=e, ic=ic, hc=hc, pu=pu, par=par: ee.matmul(
                        pu[:, 0:CC], wring[1][:, hc * DI + ic * 128: hc * DI + ic * 128 + 128],
                        xg[par][:, hc, :], start=(hc == 0), stop=(hc == 15)),
                        waits=w, inc=(s_gu if hc == 15 else None))
                I("scalar", lambda ee, e=e, ic=ic, pg=pg: ee.activation(
                    sg[:, ic, :], pg[:, 0:CC], AF.Silu),
                    waits=[(s_gu, e * 12 + 2 * ic + 1), (s_am, max(0, (e - 1) * 6 + ic + 1)), s_sc], inc=s_si)
                I("vector", lambda ee, e=e, ic=ic, pu=pu: ee.tensor_tensor(
                    au[:, ic, :], sg[:, ic, :], pu[:, 0:CC], ALU.mult),
                    waits=[(s_si, e * 6 + ic + 1), (s_gu, e * 12 + 2 * ic + 2), (s_dm, e * 16), s_v], inc=s_am)
            for sc in range(4):
                for icc in range(6):
                    for fc in range(4):
                        w = []
                        if icc == 0 and fc == 0:
                            w = [wgu[2], (s_ye, e * 16 + sc * 4)]
                        if fc == 0:
                            w.append((s_am, e * 6 + icc + 1))
                        pq = (p4, p5, p6, p7)[fc]
                        I("tensor", lambda ee, e=e, sc=sc, icc=icc, fc=fc, pq=pq: ee.matmul(
                            pq[:], au[:, icc, sc * 128:(sc + 1) * 128],
                            wring[2][:, icc * D + fc * 512: icc * D + fc * 512 + 512],
                            start=(icc == 0), stop=(icc == 5)),
                            waits=w, inc=(s_dm if icc == 5 else None))
                for fc in range(4):
                    pq = (p4, p5, p6, p7)[fc]
                    w = [(s_dm, e * 16 + sc * 4 + fc + 1)]
                    if sc == 0 and fc == 0:
                        w.append((s_yw, 16 * max(0, e - 1)))
                    I("scalar", lambda ee, e=e, sc=sc, fc=fc, pq=pq: ee.activation(
                        ystage[e % 2][:, sc, fc * 512:(fc + 1) * 512], pq[:], AF.Identity),
                        waits=w + [s_sc], inc=s_ye)
            I("sync", lambda ee, e=e: ee.dma_start(
                out=bass.AP(hbm_y.tensor, e * CC * D, [[D, 128], [128 * D, 4], [1, D]]),
                in_=ystage[e % 2]),
                waits=[(s_ye, (e + 1) * 16), (s_yw, 16 * e)], inc=s_yw, inck=16)

    phase_E()
    B.flush_block()
    es_e.close()
    es_w.close()

    es_f = ExitStack()
    gb = [sb("gb%d" % i, [128, K_TOP, D], BF16, es_f) for i in range(3)]
    accA = sb("accA", [128, D], F32, es_f)
    accB = sb("accB", [128, D], F32, es_f)
    outb = [sb("outb0", [128, D], F32, es_f), sb("outb1", [128, D], F32, es_f)]
    xna = [sb("xna%d" % i, [128, D], BF16, es_f) for i in range(3)]

    zsm = sb("zsm", [128, 256], BF16, es_f)
    fcol = sb("fcol", [128, 8], F32, es_f)

    s_fz = sem("fz")
    s_yir = sem("yir")
    s_xnl = [sem("xnl%d" % i) for i in range(3)]
    s_acc = sem("sacc")
    s_out = sem("sout")

    def phase_F():
        sq_mark = {}
        I("vector", lambda e: e.memset(zsm[:], 0.0), waits=[s_v], inc=s_v)
        I("sync", lambda e: e.dma_start(
            out=bass.AP(hbm_y.tensor, NSLOT * D, [[256, 128], [1, 256]]), in_=zsm[:]),
            waits=[(s_v, B.sems["sv"].n)], inc=s_fz, inck=16)
        for tau in range(8):
            par = tau % 3
            w = [(s_yw, 16 * NE), (s_fz, 16), (s_idr, 176)]
            if tau >= 2:
                w.append(tg["yg%d" % (tau - 2)])
            if tau >= 3:
                w.append((s_acc, tau - 2))
            if tau < 2:
                w.append(tg["xg30"])
                w.append(tg["xg31"])
            B.G(lambda e, h, tau=tau, par=par: e.dma_gather(
                gb[par][:], hbm_y[:], yidxw[:, tau * 64:(tau + 1) * 64], T, regT, D,
                transpose=False, prepare_only=True, sem=h, queue_num=1),
                swd_y[tau % 2], waits=w, queue_num=1)
            tg["yg%d" % tau] = (swd_y[tau % 2], swd_y[tau % 2].n)
            w2 = [(s_xnw, 128)]
            if tau >= 3:
                w2.append((s_acc, tau - 2))
            I("sync", lambda e, tau=tau, par=par: e.dma_start(
                out=xna[par][:],
                in_=bass.AP(hbm_xn.tensor, tau * 128 * D, [[D, 128], [1, D]])),
                waits=w2, inc=s_xnl[par], inck=16)
            for j in range(K_TOP):
                src = accA if j % 2 == 0 else accB
                dst = accB if j % 2 == 0 else accA
                w3 = [s_v]
                if j == 0:
                    w3 += [tg["yg%d" % tau], (s_xnl[par], 16 * (tau // 3 + 1)), (s_keep, 8)]
                    if tau >= 1:
                        w3.append((s_sc, sq_mark[tau - 1]))
                I("vector", lambda e, tau=tau, j=j, par=par, src=src, dst=dst: e.scalar_tensor_tensor(
                    dst[:], gb[par][:, j, :], cwsel[:, tau, j:j + 1],
                    (xna[par][:] if j == 0 else src[:]),
                    ALU.mult, ALU.add),
                    waits=w3, inc=(s_acc if j == 7 else s_v))
            I("vector", lambda e, tau=tau: e.tensor_reduce(fcol[:, 0:1], accA[:], mybir.AxisListType.X, ALU.add), waits=[s_v, (s_acc, tau + 1)], inc=s_v)
            I("vector", lambda e: e.tensor_scalar(fcol[:, 0:1], fcol[:, 0:1], 1.0 / D, None, ALU.mult), waits=[s_v], inc=s_v)
            I("scalar", lambda e: e.activation(accB[:], accA[:], AF.Square, accum_out=fcol[:, 1:2]), waits=[(s_v, B.sems["sv"].n), s_sc], inc=s_sc)
            sq_mark[tau] = B.sems["ssc"].n
            I("vector", lambda e: e.tensor_scalar(fcol[:, 1:2], fcol[:, 1:2], 1.0 / D, None, ALU.mult), waits=[s_sc, s_v], inc=s_v)
            I("vector", lambda e: e.scalar_tensor_tensor(fcol[:, 2:3], fcol[:, 0:1], 1.0, fcol[:, 0:1], ALU.mult, ALU.mult), waits=[s_v], inc=s_v)
            I("vector", lambda e: e.tensor_tensor(fcol[:, 1:2], fcol[:, 1:2], fcol[:, 2:3], ALU.subtract), waits=[s_v], inc=s_v)
            I("scalar", lambda e: e.activation(fcol[:, 3:4], fcol[:, 1:2], AF.Sqrt, bias=epsF[:]), waits=[s_v, s_sc], inc=s_sc)
            I("vector", lambda e: e.reciprocal(fcol[:, 4:5], fcol[:, 3:4]), waits=[s_sc, s_v], inc=s_v)
            ob = outb[tau % 2]
            w4 = [s_v]
            if tau >= 2:
                w4.append((s_out, 16 * (tau - 1)))
            I("vector", lambda e, tau=tau, ob=ob: e.tensor_scalar(ob[:], accA[:], fcol[:, 0:1], None, ALU.subtract), waits=w4, inc=s_v)
            I("vector", lambda e, ob=ob: e.tensor_scalar(ob[:], ob[:], fcol[:, 4:5], None, ALU.mult), waits=[s_v], inc=s_v)
            I("sync", lambda e, tau=tau, ob=ob: e.dma_start(
                out=bass.AP(outd.tensor, tau * 128 * D, [[D, 128], [1, D]]),
                in_=ob[:]),
                waits=[(s_v, B.sems["sv"].n), (s_out, 16 * tau)], inc=s_out, inck=16)
        I("sync", lambda e: e.nop(), waits=[(s_out, 128)])

    phase_F()

    if "tbl" in debug:
        d_tbl = nc.dram_tensor("d_tbl", [4, 128, S], F32, kind="ExternalOutput").ap()
        dbg["d_tbl"] = d_tbl
        s_dbg = sem("dbg")
        I("sync", lambda e: e.dma_start(out=d_tbl[0], in_=cosK[:]), waits=[B.sems["ssc"], B.sems["sv"]], inc=s_dbg, inck=16)
        I("sync", lambda e: e.dma_start(out=d_tbl[1], in_=sinK[:]), inc=s_dbg, inck=16)
        I("sync", lambda e: e.dma_start(out=bass.AP(d_tbl.tensor, 2 * 128 * S, [[S, 128], [1, T]]), in_=cosQ[:]), inc=s_dbg, inck=16)
        I("sync", lambda e: e.dma_start(out=bass.AP(d_tbl.tensor, 3 * 128 * S, [[S, 128], [1, T]]), in_=sinQ[:]), inc=s_dbg, inck=16)
        I("sync", lambda e: e.nop(), waits=[s_dbg])
        I("vector", lambda e: e.tensor_copy(scr[:, 2:3], ident[:, 0:1]), waits=[B.sems["sv"]], inc=B.sems["sv"])

    B.flush_block()
    es_f.close()
    es_moe.close()
    return nc, es, dbg


from concourse.bass_utils import run_bass_kernel_spmd
from concourse.library_overlay import lower_extended_insts

_PQ = perm_q_rows()
_PK = perm_k_rows()
_PA = perm_attn_rows()


def kernel(L0_attn_x, L0_attn_W_q, L0_attn_b_q, L0_attn_W_kv, L0_attn_b_kv,
           L0_attn_W_out, L0_attn_b_out, L0_ffn_gate_w, L0_ffn_w_gate,
           L0_ffn_w_up, L0_ffn_w_down):
    x = np.ascontiguousarray(np.asarray(L0_attn_x, dtype=np.float32))
    Wq = np.asarray(L0_attn_W_q, dtype=np.float32)[_PQ]
    bq = np.asarray(L0_attn_b_q, dtype=np.float32)[_PQ]
    Wkv = np.asarray(L0_attn_W_kv, dtype=np.float32).copy()
    Wkv[:512] = np.asarray(L0_attn_W_kv, dtype=np.float32)[:512][_PK]
    bkv = np.asarray(L0_attn_b_kv, dtype=np.float32).copy()
    bkv[:512] = np.asarray(L0_attn_b_kv, dtype=np.float32)[:512][_PK]
    woT = np.ascontiguousarray(np.asarray(L0_attn_W_out, dtype=np.float32).T[_PA])
    wqT = np.ascontiguousarray(Wq.T)
    wkvT = np.ascontiguousarray(Wkv.T)
    bq_t = np.ascontiguousarray(bq.reshape(16, 128).T)
    bkv_t = np.ascontiguousarray(bkv.reshape(8, 128).T)
    bo_t = np.ascontiguousarray(np.asarray(L0_attn_b_out, dtype=np.float32).reshape(16, 128).T)
    gw = np.ascontiguousarray(np.asarray(L0_ffn_gate_w, dtype=np.float32))
    wg = np.ascontiguousarray(np.asarray(L0_ffn_w_gate, dtype=np.float32)
                              .reshape(32, 16, 128, 768).transpose(0, 2, 1, 3).reshape(32, 128, 16 * 768))
    wu = np.ascontiguousarray(np.asarray(L0_ffn_w_up, dtype=np.float32)
                              .reshape(32, 16, 128, 768).transpose(0, 2, 1, 3).reshape(32, 128, 16 * 768))
    wd = np.ascontiguousarray(np.asarray(L0_ffn_w_down, dtype=np.float32)
                              .reshape(32, 6, 128, 2048).transpose(0, 2, 1, 3).reshape(32, 128, 6 * 2048))

    xTs = [np.ascontiguousarray(x[b].T) for b in range(4)]
    in_maps = []
    for c in range(8):
        b, h = c // 2, c % 2
        in_maps.append({
            "xT": xTs[b],
            "xqT": np.ascontiguousarray(xTs[b][:, h * T:(h + 1) * T]),
            "wq": wqT, "wkv": wkvT, "wo": woT,
            "bq": bq_t, "bkv": bkv_t, "bo": bo_t,
            "gw": gw, "wg": wg, "wu": wu, "wd": wd,
            "pos0": np.array([[float(h * T)]], np.float32),
        })

    nc, es, dbg = build()
    with es:
        lower_extended_insts(nc)
        res = run_bass_kernel_spmd(nc, in_maps, core_ids=list(range(8)))
    out = np.empty((4, S, D), np.float32)
    for c in range(8):
        b, h = c // 2, c % 2
        out[b, h * T:(h + 1) * T, :] = res.results[c]["out"]
    return out


# version 17
# speedup vs baseline: 1.0584x; 1.0189x over previous
import math
import numpy as np
import concourse.bass as bass
import concourse.mybir as mybir
from concourse import library_config

F32 = mybir.dt.float32
BF16 = mybir.dt.bfloat16
I16 = mybir.dt.int16
I32 = mybir.dt.int32
U32 = mybir.dt.uint32
AF = mybir.ActivationFunctionType
ALU = mybir.AluOpType

D = 2048
S = 2048
T = 1024
NH = 32
NKV = 8
DH = 64
NE = 32
DI = 768
CC = 512
NSLOT = NE * CC
NSLOT_PAD = NSLOT + 16
CAP = 4096
K_TOP = 8

TWO_PI_F32 = float(np.float32(2.0 * np.pi))
C1_2PI = float(np.floor(2.0 * np.pi * 4096.0) / 4096.0)
C2_2PI = float(np.float32(2.0 * np.pi - C1_2PI))
PI_F32 = float(np.float32(np.pi))
HALF_PI_F32 = float(np.float32(np.pi / 2))
LN1E4_D32 = math.log(10000.0) / 32.0
C1_FREQ = float(np.float32(-LN1E4_D32))
C2_FREQ = float(np.float32(-LN1E4_D32 - C1_FREQ))
INV2PI = float(np.float32(1.0 / TWO_PI_F32))
PI_DOWN = float(np.nextafter(np.float32(np.pi), np.float32(0.0)))


def qhead_of(tile, j):
    return 16 * (tile // 4) + 4 * j + (tile % 4)


def perm_q_rows():
    perm = np.zeros(D, dtype=np.int64)
    for tile in range(8):
        for j in range(4):
            qh = qhead_of(tile, j)
            for i in range(32):
                perm[tile * 128 + j * 32 + i] = qh * 64 + 2 * i
                perm[1024 + tile * 128 + j * 32 + i] = qh * 64 + 2 * i + 1
    return perm


def perm_k_rows():
    perm = np.zeros(512, dtype=np.int64)
    for k in range(8):
        tile = k // 4
        base = 32 * (k % 4)
        for i in range(32):
            perm[tile * 128 + base + i] = k * 64 + 2 * i
            perm[256 + tile * 128 + base + i] = k * 64 + 2 * i + 1
    return perm


def perm_attn_rows():
    perm = np.zeros(D, dtype=np.int64)
    for tau in range(8):
        for pair in range(2):
            blk = tau * 2 + pair
            for half in range(2):
                qh = qhead_of(tau, 2 * pair + half)
                for dd in range(64):
                    perm[blk * 128 + half * 64 + dd] = qh * 64 + dd
    return perm


class Sem:
    __slots__ = ("h", "n", "name")

    def __init__(self, h, name):
        self.h = h
        self.n = 0
        self.name = name


class Builder:
    ENGS = ("sync", "tensor", "vector", "scalar", "gpsimd")

    def __init__(self, nc, es):
        self.nc = nc
        self.es = es
        self.q = {e: [] for e in self.ENGS}
        self.sems = {}
        self.track = {}

    def sem(self, name):
        if name not in self.sems:
            self.sems[name] = Sem(self.es.enter_context(self.nc.semaphore(name)), name)
        return self.sems[name]

    def I(self, eng, make, waits=(), inc=None, inck=1):
        wcap = []
        if eng in ("vector", "scalar") and eng in self.track:
            ts, tn = self.track[eng]
            wcap.append((ts.h, tn))
        for w in waits:
            if isinstance(w, tuple):
                s, v = w
                wcap.append((s.h, s.n if v is None else v))
            else:
                wcap.append((w.h, w.n))
        icap = None
        if inc is not None:
            inc.n += inck
            icap = (inc.h, inck)
            if eng in ("vector", "scalar"):
                self.track[eng] = (inc, inc.n)

        def f(e, wcap=tuple(wcap), icap=icap, make=make):
            for h, v in wcap:
                if v > 0:
                    e.wait_ge(h, v)
            try:
                ins = make(e)
            except Exception:
                import traceback
                traceback.print_exc()
                raise
            if icap is not None:
                ins.then_inc(icap[0], icap[1])
        self.q[eng].append(f)

    def raw(self, eng, f):
        self.q[eng].append(f)

    def G(self, make_prep, sem_done, waits=(), inck=16, queue_num=0):
        prep = self.sem("swdge_prep")
        wcap = []
        for w in waits:
            if isinstance(w, tuple):
                s, v = w
                wcap.append((s.h, s.n if v is None else v))
            else:
                wcap.append((w.h, w.n))
        sem_done.n += inck
        prep.n += 1
        pn = prep.n

        def f(e, wcap=tuple(wcap), make_prep=make_prep, h=sem_done.h, ph=prep.h, pn=pn, qn=queue_num):
            for hh, v in wcap:
                if v > 0:
                    e.wait_ge(hh, v)
            make_prep(e, h).then_inc(ph, 1)
            e.wait_ge(ph, pn)
            e.trigger_dma(count=1, queue_num=qn)
        self.q["gpsimd"].append(f)

    def flush_block(self):
        nc = self.nc
        q, self.q = self.q, {e: [] for e in self.ENGS}
        with nc.Block() as block:
            for name in self.ENGS:
                if q[name]:
                    def body(e, fns=tuple(q[name])):
                        for f in fns:
                            f(e)
                    getattr(block, name)(body)


def build(debug=(), cc_shared=True):
    from contextlib import ExitStack
    nc = bass.Bass("TRN2", target_bir_lowering=False, debug=False, num_swdge_queues=2)
    es = ExitStack()
    B = Builder(nc, es)
    I, sem = B.I, B.sem

    xT = nc.dram_tensor("xT", [D, S], F32, kind="ExternalInput").ap()
    xqT = nc.dram_tensor("xqT", [D, T], F32, kind="ExternalInput").ap()
    wq = nc.dram_tensor("wq", [D, D], F32, kind="ExternalInput").ap()
    wkv = nc.dram_tensor("wkv", [D, 1024], F32, kind="ExternalInput").ap()
    wo = nc.dram_tensor("wo", [D, D], F32, kind="ExternalInput").ap()
    bqd = nc.dram_tensor("bq", [128, 16], F32, kind="ExternalInput").ap()
    bkvd = nc.dram_tensor("bkv", [128, 8], F32, kind="ExternalInput").ap()
    bod = nc.dram_tensor("bo", [128, 16], F32, kind="ExternalInput").ap()
    gwd = nc.dram_tensor("gw", [D, NE], F32, kind="ExternalInput").ap()
    wgd = nc.dram_tensor("wg", [NE, 128, 16 * DI], F32, kind="ExternalInput").ap()
    wud = nc.dram_tensor("wu", [NE, 128, 16 * DI], F32, kind="ExternalInput").ap()
    wdd = nc.dram_tensor("wd", [NE, 128, 6 * D], F32, kind="ExternalInput").ap()
    pos0d = nc.dram_tensor("pos0", [1, 1], F32, kind="ExternalInput").ap()
    outd = nc.dram_tensor("out", [T, D], F32, kind="ExternalOutput").ap()

    hbm_xn = nc.dram_tensor("hbm_xn", [T, D], BF16, kind="Internal").ap()
    hbm_ps = nc.dram_tensor("hbm_ps", [1, T * K_TOP], F32, kind="Internal").ap()
    hbm_yi = nc.dram_tensor("hbm_yi", [1, T * K_TOP], F32, kind="Internal").ap()
    hbm_ab = nc.dram_tensor("hbm_ab", [2, T], F32, kind="Internal").ap()
    hbm_roster = nc.dram_tensor("hbm_roster", [NSLOT_PAD, 64], F32, kind="Internal").ap()
    hbm_y = nc.dram_tensor("hbm_y", [NSLOT_PAD, D], BF16, kind="Internal").ap()

    dbg = {}
    if "qkv" in debug:
        dbg["d_q"] = nc.dram_tensor("d_q", [16, 128, T], F32, kind="ExternalOutput").ap()
        dbg["d_k"] = nc.dram_tensor("d_k", [4, 128, S], F32, kind="ExternalOutput").ap()
        dbg["d_v"] = nc.dram_tensor("d_v", [16, 128, 520], F32, kind="ExternalOutput").ap()

    def sb(name, shape, dtype, stack=None):
        return (stack or es).enter_context(nc.sbuf_tensor(name, shape, dtype))
    es_attn = ExitStack()
    es_a = ExitStack()
    es_b = ExitStack()
    psa = lambda name, shape, dtype=F32: es.enter_context(nc.psum_tensor(name, shape, dtype))

    ident = sb("ident", [128, 128], BF16)
    triu = sb("triu", [128, 128], BF16)
    onescol_b = sb("onescol_b", [128, 1], BF16)
    onescol_f = sb("onescol_f", [128, 1], F32)
    onesrow_b = sb("onesrow_b", [1, 128], BF16)
    onesrow_f = sb("onesrow_f", [1, 128], F32)
    bq_t = sb("bq_t", [128, 16], F32)
    bkv_t = sb("bkv_t", [128, 8], F32)
    bo_t = sb("bo_t", [128, 16], F32)
    gwb = sb("gwb", [128, 16, NE], BF16)
    scr = sb("scr", [128, 8], F32)

    p03 = psa("p03", [128, 4, 512])
    p47 = psa("p47", [128, 4, 512])
    p0 = p03[:, 0, :]
    p1 = p03[:, 1, :]
    p2 = p03[:, 2, :]
    p3 = p03[:, 3, :]
    p4 = p47[:, 0, :]
    p5 = p47[:, 1, :]
    p6 = p47[:, 2, :]
    p7 = p47[:, 3, :]
    pbank = [p0, p1, p2, p3, p4, p5, p6, p7]

    attnb = sb("attnb", [128, 16, T], BF16)
    qbf = sb("qbf", [128, 16, T], BF16, es_attn)
    kbf = sb("kbf", [128, 4, S], BF16, es_attn)
    vkm = sb("vkm", [128, 16, 520], BF16, es_attn)
    cosK = sb("cosK", [128, S], F32, es_b)
    sinK = sb("sinK", [128, S], F32, es_b)
    cosQ = sb("cosQ", [128, T], F32, es_b)
    sinQ = sb("sinQ", [128, T], F32, es_b)

    s_ld0 = sem("ld0")
    swd_x = [sem("swdx0"), sem("swdx1")]
    swd_w = [sem("swdw0"), sem("swdw1"), sem("swdw2")]
    swd_m = sem("swdm")
    swd_y = [sem("swdy0"), sem("swdy1")]
    tg = {}
    s_gw = swd_m
    s_v = sem("sv")
    s_sc = sem("ssc")
    s_g = sem("sg")
    s_t = sem("st_t")

    def phase_A():
        I("sync", lambda e: e.dma_start(out=bq_t[:], in_=bqd[:]), inc=s_ld0, inck=16)
        I("sync", lambda e: e.dma_start(out=bkv_t[:], in_=bkvd[:]), inc=s_ld0, inck=16)
        I("sync", lambda e: e.dma_start(out=bo_t[:], in_=bod[:]), inc=s_ld0, inck=16)
        I("sync", lambda e: e.dma_start(out=scr[:, 0:1], in_=bass.AP(pos0d.tensor, 0, [[0, 128], [1, 1]])), inc=s_ld0, inck=16)

        I("gpsimd", lambda e: e.dma_start(out=gwb[:], in_=bass.AP(gwd.tensor, 0, [[NE, 128], [128 * NE, 16], [1, NE]])), inc=swd_m, inck=16)
        tg["gw"] = (swd_m, swd_m.n)

        io_r = sb("io_r", [128, 128], F32, es_a)
        io_c = sb("io_c", [128, 128], F32, es_a)
        tbl = sb("tbl", [128, S], F32, es_a)
        tA = sb("tA", [128, S], F32, es_a)
        tB = sb("tB", [128, S], F32, es_a)
        tC = sb("tC", [128, S], F32, es_a)
        iT = sb("iT", [128, S], I32, es_a)
        freqc = sb("freqc", [128, 1], F32, es_a)
        f1 = sb("f1", [128, 1], F32, es_a)
        pmod = sb("pmod", [128, 1], F32, es_a)
        thK = sb("thK", [128, S], F32, es_a)
        thQ = sb("thQ", [128, T], F32, es_a)

        I("gpsimd", lambda e: e.iota(scr[:, 1:2], [[0, 1]], base=0, channel_multiplier=1, allow_small_or_imprecise_dtypes=True), inc=s_g)
        I("gpsimd", lambda e: e.iota(io_r[:], [[0, 128]], base=0, channel_multiplier=1, allow_small_or_imprecise_dtypes=True), inc=s_g)
        I("gpsimd", lambda e: e.iota(io_c[:], [[1, 128]], base=0, channel_multiplier=0, allow_small_or_imprecise_dtypes=True), inc=s_g)
        I("gpsimd", lambda e: e.iota(tbl[:], [[1, S]], base=0, channel_multiplier=0, allow_small_or_imprecise_dtypes=True), inc=s_g)

        for tile, val in ((onescol_b, 1.0), (onescol_f, 1.0), (onesrow_b, 1.0), (onesrow_f, 1.0)):
            I("vector", lambda e, t=tile, v=val: e.memset(t[:], v), inc=s_v)
        I("vector", lambda e: e.tensor_tensor(ident[:], io_r[:], io_c[:], ALU.is_equal), waits=[(s_g, 3), s_v], inc=s_v)
        I("vector", lambda e: e.tensor_tensor(triu[:], io_r[:], io_c[:], ALU.is_lt), waits=[s_v], inc=s_v)
        I("vector", lambda e: e.tensor_scalar(f1[:], scr[:, 1:2], 1.0 / 32.0, None, ALU.mult), waits=[(s_g, 1), s_v], inc=s_v)
        I("vector", lambda e: e.tensor_copy(iT[:, 0:1], f1[:]), waits=[s_v], inc=s_v)
        I("vector", lambda e: e.tensor_copy(pmod[:], iT[:, 0:1]), waits=[s_v], inc=s_v)
        I("vector", lambda e: e.tensor_tensor(freqc[:], pmod[:], f1[:], ALU.is_gt), waits=[s_v], inc=s_v)
        I("vector", lambda e: e.tensor_tensor(f1[:], freqc[:], pmod[:], ALU.subtract), waits=[s_v], inc=s_v)
        I("vector", lambda e: e.scalar_tensor_tensor(pmod[:], f1[:], 32.0, scr[:, 1:2], ALU.mult, ALU.add), waits=[s_v], inc=s_v)
        I("scalar", lambda e: e.activation(f1[:], pmod[:], AF.Exp, scale=C1_FREQ), waits=[s_v], inc=s_sc)
        I("scalar", lambda e: e.activation(pmod[:], pmod[:], AF.Exp, scale=C2_FREQ), waits=[s_sc], inc=s_sc)
        I("vector", lambda e: e.tensor_tensor(freqc[:], f1[:], pmod[:], ALU.mult), waits=[s_sc, s_v], inc=s_v)
        I("vector", lambda e: e.tensor_scalar(thK[:], tbl[:], freqc[:], None, ALU.mult), waits=[(s_g, 4), s_v], inc=s_v)
        I("vector", lambda e: e.tensor_scalar(thQ[:], tbl[:, 0:T], scr[:, 0:1], None, ALU.add), waits=[(s_ld0, 64), s_v], inc=s_v)
        I("vector", lambda e: e.tensor_scalar(thQ[:], thQ[:], freqc[:], None, ALU.mult), waits=[s_v], inc=s_v)

        def table(theta, shift, ncols, dest):
            a = tA[:, 0:ncols]
            b = tB[:, 0:ncols]
            c = tC[:, 0:ncols]
            it = iT[:, 0:ncols]
            I("vector", lambda e: e.tensor_scalar(a, theta, INV2PI, shift * INV2PI + 0.5, ALU.mult, ALU.add), waits=[s_v, s_sc], inc=s_v)
            I("vector", lambda e: e.tensor_copy(it, a), waits=[s_v], inc=s_v)
            I("vector", lambda e: e.tensor_copy(b, it), waits=[s_v], inc=s_v)
            I("vector", lambda e: e.tensor_tensor(c, b, a, ALU.is_gt), waits=[s_v], inc=s_v)
            I("vector", lambda e: e.tensor_tensor(a, c, b, ALU.subtract), waits=[s_v], inc=s_v)
            I("vector", lambda e: e.scalar_tensor_tensor(b, a, C1_2PI, theta, ALU.mult, ALU.add), waits=[s_v], inc=s_v)
            I("vector", lambda e: e.scalar_tensor_tensor(c, a, C2_2PI, b, ALU.mult, ALU.add), waits=[s_v], inc=s_v)
            I("vector", lambda e: e.tensor_scalar(a, c, shift, PI_DOWN, ALU.add, ALU.min), waits=[s_v], inc=s_v)
            I("vector", lambda e: e.tensor_scalar(a, a, -PI_DOWN, None, ALU.max), waits=[s_v], inc=s_v)
            I("scalar", lambda e: e.activation(dest, a, AF.Sin), waits=[s_v, s_sc], inc=s_sc)

        table(thK[:], 0.0, S, sinK[:])
        table(thK[:], HALF_PI_F32, S, cosK[:])
        table(thQ[:], 0.0, T, sinQ[:])
        table(thQ[:], HALF_PI_F32, T, cosQ[:])

    phase_A()
    B.flush_block()
    es_a.close()

    xbf = [sb("xbf0", [128, 16, 512], BF16, es_b), sb("xbf1", [128, 16, 512], BF16, es_b)]
    wkvb = sb("wkvb", [128, 16, 512], BF16, es_b)
    wqs = [sb("wqs%d" % i, [128, 16, 128], BF16, es_b) for i in range(3)]
    kf32 = sb("kf32", [128, 4, 512], F32, es_b)
    qf32 = kf32
    vfe = sb("vfe", [128, 512], BF16, es_b)
    rta = sb("rta", [128, 512], F32, es_b)
    rtb = sb("rtb", [128, 512], F32, es_b)


    s_kvmm = sem("kvmm")
    s_kevac = sem("kevac")
    s_rope = sem("rope")
    s_tr = sem("tr")
    s_trv = sem("trv")
    s_qmm = sem("qmm")
    s_qevac = sem("qevac")

    def phase_B():
        for qtc in range(2):
            I("gpsimd", lambda e, qtc=qtc: e.dma_start(
                out=xbf[qtc][:],
                in_=bass.AP(xqT.tensor, qtc * 512, [[T, 128], [128 * T, 16], [1, 512]])),
                inc=swd_x[qtc], inck=16)
            tg["xq%d" % qtc] = (swd_x[qtc], swd_x[qtc].n)
        for i2 in range(16):
            pr, half = i2 // 2, i2 % 2
            oc = pr + 8 * half
            w = []
            if i2 >= 3:
                I("gpsimd", lambda e, i2=i2: e.nop(), waits=[(s_qmm, 4 * ((i2 - 3) // 2 + 1))])
                w = [tg["wq%d" % (i2 - 3)]]
            I("gpsimd", lambda e, oc=oc, b=i2 % 3: e.dma_start(
                out=wqs[b][:],
                in_=bass.AP(wq.tensor, oc * 128, [[D, 128], [128 * D, 16], [1, 128]])),
                waits=w, inc=swd_w[i2 % 3], inck=16)
            tg["wq%d" % i2] = (swd_w[i2 % 3], swd_w[i2 % 3].n)
        I("gpsimd", lambda e: e.dma_start(out=wkvb[:], in_=bass.AP(wkv.tensor, 0, [[1024, 128], [128 * 1024, 16], [1, 512]])), waits=[tg["gw"]], inc=swd_m, inck=16)
        tg["wkv"] = (swd_m, swd_m.n)
        for tc in range(4):
            if tc >= 2:
                I("gpsimd", lambda e: e.nop(), waits=[(s_kvmm, (tc - 1) * 4)])
            else:
                I("gpsimd", lambda e: e.nop(), waits=[(s_qmm, 32)])
            I("gpsimd", lambda e, tc=tc: e.dma_start(
                out=xbf[tc % 2][:],
                in_=bass.AP(xT.tensor, tc * 512, [[S, 128], [128 * S, 16], [1, 512]])),
                waits=[tg["xq%d" % (tc % 2)] if tc < 2 else tg["xc%d" % (tc - 2)]],
                inc=swd_x[tc % 2], inck=16)
            tg["xc%d" % tc] = (swd_x[tc % 2], swd_x[tc % 2].n)
        I("gpsimd", lambda e: e.nop(), waits=[(s_kvmm, 16)])
        I("gpsimd", lambda e: e.dma_start(out=wkvb[:], in_=bass.AP(wkv.tensor, 512, [[1024, 128], [128 * 1024, 16], [1, 512]])), waits=[tg["wkv"]], inc=swd_m, inck=16)
        tg["wkv2"] = (swd_m, swd_m.n)
        for tc in range(4):
            if tc >= 2:
                I("gpsimd", lambda e: e.nop(), waits=[(s_kvmm, 16 + (tc - 1) * 4)])
            I("gpsimd", lambda e, tc=tc: e.dma_start(
                out=xbf[tc % 2][:],
                in_=bass.AP(xT.tensor, tc * 512, [[S, 128], [128 * S, 16], [1, 512]])),
                waits=[tg["xc%d" % (tc + 2)] if tc < 2 else tg["xc2%d" % (tc - 2)]],
                inc=swd_x[tc % 2], inck=16)
            tg["xc2%d" % tc] = (swd_x[tc % 2], swd_x[tc % 2].n)

        qps = [p0, p1, p2, p3]
        for pr in range(8):
            for half in range(2):
                for qtc in range(2):
                    pt = qps[half * 2 + qtc]
                    for hc in range(16):
                        w = []
                        if hc == 0:
                            i2 = 2 * pr + half
                            w = [tg["xq%d" % qtc], tg["wq%d" % i2]]
                            if pr > 0:
                                w.append((s_qevac, (pr - 1) * 4 + half * 2 + qtc + 1))
                        I("tensor", lambda e, pr=pr, half=half, qtc=qtc, hc=hc, pt=pt: e.matmul(
                            pt[:], wqs[(2 * pr + half) % 3][:, hc, :], xbf[qtc][:, hc, :],
                            start=(hc == 0), stop=(hc == 15)),
                            waits=w, inc=(s_qmm if hc == 15 else None))
            for half in range(2):
                for qtc in range(2):
                    pt = qps[half * 2 + qtc]
                    w = [(s_qmm, pr * 4 + half * 2 + qtc + 1)]
                    if pr > 0:
                        w.append((s_rope, pr * 2))
                    I("scalar", lambda e, pr=pr, half=half, qtc=qtc, pt=pt: e.activation(
                        qf32[:, half * 2 + qtc, :], pt[:], AF.Identity,
                        bias=bq_t[:, pr + 8 * half:pr + 8 * half + 1]),
                        waits=w + [s_sc, (s_ld0, 64)], inc=s_qevac)
            for qtc in range(2):
                sl = slice(qtc * 512, (qtc + 1) * 512)
                x1 = qf32[:, qtc, :]
                x2 = qf32[:, 2 + qtc, :]
                w = [(s_qevac, pr * 4 + 4)]
                I("vector", lambda e, x1=x1, sl=sl: e.tensor_tensor(rta[:], x1, cosQ[:, sl], ALU.mult), waits=w + [s_v], inc=s_v)
                I("vector", lambda e, x2=x2, sl=sl: e.tensor_tensor(rtb[:], x2, sinQ[:, sl], ALU.mult), waits=[s_v], inc=s_v)
                I("vector", lambda e, pr=pr, sl=sl: e.tensor_tensor(qbf[:, pr, sl], rta[:], rtb[:], ALU.subtract), waits=[s_v], inc=s_v)
                I("vector", lambda e, x1=x1, sl=sl: e.tensor_tensor(rta[:], x1, sinQ[:, sl], ALU.mult), waits=[s_v], inc=s_v)
                I("vector", lambda e, x2=x2, sl=sl: e.tensor_tensor(rtb[:], x2, cosQ[:, sl], ALU.mult), waits=[s_v], inc=s_v)
                I("vector", lambda e, pr=pr, sl=sl: e.tensor_tensor(qbf[:, pr + 8, sl], rta[:], rtb[:], ALU.add), waits=[s_v], inc=s_rope)

        for tc in range(4):
            sl = slice(tc * 512, (tc + 1) * 512)
            for oc in range(4):
                for hc in range(16):
                    w = []
                    if hc == 0:
                        w = [tg["xc%d" % tc], tg["wkv"]]
                        if tc > 0:
                            w.append((s_kevac, tc * 4))
                        else:
                            w.append((s_qevac, 32))
                    I("tensor", lambda e, oc=oc, hc=hc, tc=tc: e.matmul(
                        pbank[oc], wkvb[:, hc, oc * 128:(oc + 1) * 128], xbf[tc % 2][:, hc, :],
                        start=(hc == 0), stop=(hc == 15)),
                        waits=w, inc=(s_kvmm if hc == 15 else None))
            for oc in range(4):
                w = [(s_kvmm, tc * 4 + oc + 1), (s_ld0, 64), (s_rope, 16 + tc * 2)]
                I("scalar", lambda e, oc=oc: e.activation(
                    kf32[:, oc, :], pbank[oc], AF.Identity, bias=bkv_t[:, oc:oc + 1]),
                    waits=w + [s_sc], inc=s_kevac)
            for a, b2 in ((0, 2), (1, 3)):
                w = [(s_kevac, tc * 4 + 4)]
                I("vector", lambda e, a=a, sl=sl: e.tensor_tensor(rta[:], kf32[:, a, :], cosK[:, sl], ALU.mult), waits=w + [s_v], inc=s_v)
                I("vector", lambda e, b2=b2, sl=sl: e.tensor_tensor(rtb[:], kf32[:, b2, :], sinK[:, sl], ALU.mult), waits=[s_v], inc=s_v)
                I("vector", lambda e, a=a, sl=sl: e.tensor_tensor(kbf[:, a, sl], rta[:], rtb[:], ALU.subtract), waits=[s_v], inc=s_v)
                I("vector", lambda e, a=a, sl=sl: e.tensor_tensor(rta[:], kf32[:, a, :], sinK[:, sl], ALU.mult), waits=[s_v], inc=s_v)
                I("vector", lambda e, b2=b2, sl=sl: e.tensor_tensor(rtb[:], kf32[:, b2, :], cosK[:, sl], ALU.mult), waits=[s_v], inc=s_v)
                I("vector", lambda e, b2=b2, sl=sl: e.tensor_tensor(kbf[:, b2, sl], rta[:], rtb[:], ALU.add), waits=[s_v], inc=s_rope)
        for tc in range(4):
            for ocv in range(4):
                for hc in range(16):
                    w = []
                    if hc == 0:
                        w = [tg["xc2%d" % tc], tg["wkv2"]]
                        if tc == 0:
                            w.append((s_kevac, 16))
                        else:
                            w += [(s_kevac, 16 + tc * 4), (s_trv, tc * 16)]
                    I("tensor", lambda e, ocv=ocv, hc=hc, tc=tc: e.matmul(
                        pbank[ocv], wkvb[:, hc, ocv * 128:(ocv + 1) * 128], xbf[tc % 2][:, hc, :],
                        start=(hc == 0), stop=(hc == 15)),
                        waits=w, inc=(s_kvmm if hc == 15 else None))
                w = [(s_kvmm, 16 + tc * 4 + ocv + 1), (s_ld0, 64), (s_tr, tc * 16 + ocv * 4)]
                I("scalar", lambda e, ocv=ocv: e.activation(
                    vfe[:], pbank[ocv], AF.Identity, bias=bkv_t[:, 4 + ocv:5 + ocv]),
                    waits=w + [s_sc], inc=s_kevac)
                for kts in range(4):
                    kt = tc * 4 + kts
                    nt = tc * 16 + ocv * 4 + kts
                    w = [(s_kevac, 16 + tc * 4 + ocv + 1)]
                    if nt > 0:
                        w.append((s_trv, nt))
                    I("tensor", lambda e, kts=kts: e.transpose(
                        p0[:, 0:64].bitcast(BF16), vfe[:, kts * 128:(kts + 1) * 128], ident[:]),
                        waits=w, inc=s_tr)
                    h0 = 2 * ocv
                    I("vector", lambda e, kt=kt, h0=h0: e.tensor_copy(
                        vkm[:, kt, h0 * 65:h0 * 65 + 64], p0[:, 0:32].bitcast(BF16)),
                        waits=[(s_tr, tc * 16 + ocv * 4 + kts + 1), s_v], inc=s_v)
                    I("vector", lambda e, kt=kt, h0=h0: e.tensor_copy(
                        vkm[:, kt, (h0 + 1) * 65:(h0 + 1) * 65 + 64], p0[:, 32:64].bitcast(BF16)),
                        waits=[s_v], inc=s_trv)
        for h in range(8):
            I("vector", lambda e, h=h: e.memset(vkm[:, :, h * 65 + 64], 1.0), waits=[s_v], inc=s_v)

    phase_B()
    B.flush_block()
    es_b.close()

    if "qkv" in debug:
        s_dq = sem("dq")
        for i in range(16):
            I("gpsimd", lambda e, i=i: e.dma_start(out=dbg["d_q"][i], in_=qbf[:, i, :]), waits=[(s_rope, s_rope.n)], inc=s_dq, inck=16)
        for i in range(4):
            I("gpsimd", lambda e, i=i: e.dma_start(out=dbg["d_k"][i], in_=kbf[:, i, :]), inc=s_dq, inck=16)
        for i in range(16):
            I("gpsimd", lambda e, i=i: e.dma_start(out=dbg["d_v"][i], in_=vkm[:, i, :]), waits=[(s_trv, s_trv.n), (s_v, s_v.n)], inc=s_dq, inck=16)
        I("gpsimd", lambda e: e.nop(), waits=[s_dq])

    es_c = ExitStack()
    pbuf = sb("pbuf", [128, 2, 2048], BF16, es_c)
    seb = [sb("seb0", [128, 512], F32, es_c), sb("seb1", [128, 512], F32, es_c)]
    serow = sb("serow", [128, 512], F32, es_c)
    ones4 = sb("ones4", [128, 64], F32, es_c)
    sacc = sb("sacc", [128, 2048], F32, es_c)
    kbf2 = sb("kbf2", [128, 4, S], BF16, es_c)
    qbf2 = sb("qbf2", [128, 16, T], BF16, es_c)
    io_c2 = sb("io_c2", [128, 128], F32, es_c)
    selm = [sb("selm%d" % i, [128, 128], BF16, es_c) for i in range(4)]
    self32 = sb("self32", [128, 128], F32, es_c)
    rcol = sb("rcol", [128, 6], F32, es_c)
    icol = sb("icol", [128, 1], I32, es_c)

    s_smm = sem("smm")
    s_exp = sem("exp")
    s_acc = sem("acp")
    s_sum = sem("sum")
    s_pv = sem("pv")
    s_rec = sem("rec")
    s_sebc = sem("sebc")
    s_bc = sem("bc")
    s_aev = sem("aev")
    psum_bc = p7

    s_kq2 = sem("kq2")
    s_km = sem("km")

    def phase_C():
        I("vector", lambda e: e.memset(ones4[:], 1.0), inc=s_v)
        I("vector", lambda e: e.memset(p6[:], 0.0), waits=[s_v], inc=s_v)
        I("gpsimd", lambda e: e.iota(io_c2[:], [[1, 128]], base=0, channel_multiplier=0, allow_small_or_imprecise_dtypes=True), inc=s_g)
        for si, (h, jp) in enumerate(((0, 0), (0, 1), (1, 0), (1, 1))):
            c = [rcol[:, k:k + 1] for k in range(6)]
            I("vector", lambda e, jp=jp: e.tensor_scalar(c[0], scr[:, 1:2], float(64 - 64 * jp), 1.0 / 32.0, ALU.add, ALU.mult), waits=[(s_g, s_g.n), s_v], inc=s_v)
            I("vector", lambda e: e.tensor_copy(icol[:], c[0]), waits=[s_v], inc=s_v)
            I("vector", lambda e: e.tensor_copy(c[1], icol[:]), waits=[s_v], inc=s_v)
            I("vector", lambda e: e.tensor_tensor(c[2], c[1], c[0], ALU.is_gt), waits=[s_v], inc=s_v)
            I("vector", lambda e: e.tensor_tensor(c[1], c[1], c[2], ALU.subtract), waits=[s_v], inc=s_v)
            I("vector", lambda e: e.tensor_scalar(c[2], c[1], 1.5, None, ALU.is_gt), waits=[s_v], inc=s_v)
            I("vector", lambda e: e.tensor_scalar(c[3], c[1], 3.5, None, ALU.is_lt), waits=[s_v], inc=s_v)
            I("vector", lambda e: e.tensor_tensor(c[2], c[2], c[3], ALU.mult), waits=[s_v], inc=s_v)
            I("vector", lambda e, jp=jp, h=h: e.tensor_scalar(c[4], scr[:, 1:2], float(64 - 64 * jp + 32 * h - 128), None, ALU.add), waits=[s_v], inc=s_v)
            I("vector", lambda e: e.scalar_tensor_tensor(c[4], c[1], 32.0, c[4], ALU.mult, ALU.add), waits=[s_v], inc=s_v)
            I("vector", lambda e: e.tensor_tensor(c[4], c[4], c[2], ALU.mult), waits=[s_v], inc=s_v)
            I("vector", lambda e: e.tensor_tensor(c[4], c[4], c[2], ALU.add), waits=[s_v], inc=s_v)
            I("vector", lambda e: e.tensor_scalar(c[4], c[4], -1.0, None, ALU.add), waits=[s_v], inc=s_v)
            I("vector", lambda e: e.tensor_scalar(self32[:], io_c2[:], c[4], None, ALU.is_equal), waits=[s_v], inc=s_v)
            I("vector", lambda e, si=si: e.tensor_copy(selm[si][:], self32[:]), waits=[s_v], inc=s_v)
        nk2 = 0
        for g in range(2):
            for jp in range(2):
                for ns in range(4):
                    ksl2 = slice(ns * 512, (ns + 1) * 512)
                    pbk = pbank[nk2 % 4]
                    w = [(s_v, B.sems["sv"].n)]
                    if nk2 >= 4:
                        w.append((s_kq2, nk2 - 3))
                    I("tensor", lambda e, g=g, jp=jp, ksl2=ksl2, pbk=pbk: e.matmul(
                        pbk, selm[jp][:], kbf[:, g, ksl2], start=True, stop=False), waits=w)
                    I("tensor", lambda e, g=g, jp=jp, ksl2=ksl2, pbk=pbk: e.matmul(
                        pbk, selm[2 + jp][:], kbf[:, 2 + g, ksl2], start=False, stop=True), inc=s_km)
                    I("scalar", lambda e, g=g, jp=jp, ksl2=ksl2, pbk=pbk, nk2=nk2: e.activation(
                        kbf2[:, g * 2 + jp, ksl2], pbk, AF.Identity),
                        waits=[(s_km, nk2 + 1), s_sc], inc=s_kq2)
                    nk2 += 1
        for tau8 in range(8):
            for jp in range(2):
                for ns in range(2):
                    qsl2 = slice(ns * 512, (ns + 1) * 512)
                    pbk = pbank[nk2 % 4]
                    w = []
                    if nk2 >= 4:
                        w = [(s_kq2, nk2 - 3)]
                    I("tensor", lambda e, tau8=tau8, jp=jp, qsl2=qsl2, pbk=pbk: e.matmul(
                        pbk, selm[jp][:], qbf[:, tau8, qsl2], start=True, stop=False), waits=w)
                    I("tensor", lambda e, tau8=tau8, jp=jp, qsl2=qsl2, pbk=pbk: e.matmul(
                        pbk, selm[2 + jp][:], qbf[:, 8 + tau8, qsl2], start=False, stop=True), inc=s_km)
                    I("scalar", lambda e, tau8=tau8, jp=jp, qsl2=qsl2, pbk=pbk, nk2=nk2: e.activation(
                        qbf2[:, tau8 * 2 + jp, qsl2], pbk, AF.Identity),
                        waits=[(s_km, nk2 + 1), s_sc], inc=s_kq2)
                    nk2 += 1
        NKQ2 = nk2
        for u in range(16):
            tau, qc = u // 2, u % 2
            qsl = slice(qc * 512, (qc + 1) * 512)
            for kc in range(17):
                if kc < 16:
                    ksl = slice(kc * 128, (kc + 1) * 128)
                    for j in range(4):
                        jm, jp = j % 2, j // 2
                        w = []
                        if j in (0, 2):
                            if kc > 0:
                                w = [(s_exp, u * 32 + (kc - 1) * 2 + j // 2 + 1)]
                            elif j == 0 and u > 0:
                                w = [(s_exp, u * 32)]
                            elif j == 0:
                                w = [(s_kq2, NKQ2)]
                        I("tensor", lambda e, tau=tau, j=j, jm=jm, jp=jp, ksl=ksl, qsl=qsl: e.matmul(
                            pbank[j], kbf2[64 * jm:64 * jm + 64, (tau // 4) * 2 + jp, ksl],
                            qbf2[64 * jm:64 * jm + 64, tau * 2 + jp, qsl],
                            start=True, stop=True, tile_position=(64 * jm, 0)), waits=w, inc=s_smm)
                    for P in range(2):
                        w = [(s_smm, u * 64 + kc * 4 + 2 * (P + 1))]
                        if kc >= 1:
                            w.append((s_acc, u * 32 + (kc - 1) * 2))
                        elif u > 0:
                            w.append((s_acc, u * 32))
                        I("scalar", lambda e, P=P, kc=kc: e.activation(
                            pbuf[:, kc % 2, P * 1024:(P + 1) * 1024],
                            p03[:, 2 * P:2 * P + 2, :], AF.Exp, scale=0.125),
                            waits=w, inc=s_exp)
                    for P in range(2):
                        w = [(s_exp, u * 32 + kc * 2 + P + 1), s_v]
                        if kc == 0 and P == 0:
                            w.append((s_sum, u * 4))
                        if kc == 0:
                            I("vector", lambda e, P=P: e.tensor_copy(
                                sacc[:, P * 1024:(P + 1) * 1024], pbuf[:, 0, P * 1024:(P + 1) * 1024]),
                                waits=w, inc=s_acc)
                        else:
                            I("vector", lambda e, P=P, kc=kc: e.tensor_tensor(
                                sacc[:, P * 1024:(P + 1) * 1024], sacc[:, P * 1024:(P + 1) * 1024],
                                pbuf[:, kc % 2, P * 1024:(P + 1) * 1024], ALU.add),
                                waits=w, inc=s_acc)
                if kc >= 1:
                    kp = kc - 1
                    for pair in range(2):
                        for pi in range(2):
                            j = pair * 2 + pi
                            kv = 4 * (tau // 4) + j
                            w = []
                            if kp == 15 and pi == 0:
                                w.append((s_exp, u * 32 + kp * 2 + pair + 1))
                            if kp == 0 and pi == 0 and pair == 0:
                                w.append((s_aev, max(0, u * 2)))
                            I("tensor", lambda e, pair=pair, pi=pi, j=j, kv=kv, kp=kp: e.matmul(
                                pbank[4 + pair][pi * 64:(pi + 1) * 64, :],
                                vkm[:, kp, kv * 65:kv * 65 + 64],
                                pbuf[:, kp % 2, j * 512:(j + 1) * 512],
                                start=(kp == 0), stop=(kp == 15), tile_position=(0, pi * 64),
                                skip_group_check=True),
                                waits=w,
                                inc=(s_pv if kp == 15 else None))
            for j in range(4):
                w = [(s_acc, (u + 1) * 32)]
                if j == 0:
                    w.append((s_rec, u))
                I("tensor", lambda e, j=j: e.matmul(
                    p6[32 * j:32 * j + 1, :],
                    onescol_f[:], sacc[:, j * 512:(j + 1) * 512],
                    start=True, stop=True, tile_position=(0, 32 * j),
                    skip_group_check=True),
                    waits=w, inc=s_sum)
            w = [(s_sum, (u + 1) * 4), s_v]
            if u > 0:
                w.append((s_bc, u * 2))
            I("vector", lambda e: e.tensor_copy(serow[:], p6[:]), waits=w, inc=s_rec)
            for pair in range(2):
                for pi in range(2):
                    j = pair * 2 + pi
                    w = [(s_rec, u + 1)]
                    if pi == 0 and pair == 0:
                        w.append((s_sebc, u * 2))
                    elif pi == 0:
                        w.append((s_sebc, u * 2 + 1))
                    I("tensor", lambda e, j=j, pi=pi: e.matmul(
                        psum_bc[pi * 64:(pi + 1) * 64, :],
                        ones4[32 * j:32 * j + 1, 0:64],
                        serow[32 * j:32 * j + 1, :],
                        start=True, stop=True, tile_position=(32 * j, pi * 64),
                        skip_group_check=True),
                        waits=w, inc=(s_bc if pi == 1 else None))
                w = [(s_bc, u * 2 + pair + 1)]
                if u > 0:
                    w.append((s_aev, (u - 1) * 2 + pair + 1))
                I("vector", lambda e, pair=pair: e.reciprocal_approx_fast(seb[pair][:], psum_bc[:]),
                    waits=w + [s_v], inc=s_sebc)
                w = [(s_pv, u * 4 + pair * 2 + 2)]
                I("vector", lambda e, tau=tau, pair=pair, qsl=qsl: e.tensor_tensor(
                    attnb[:, tau * 2 + pair, qsl], pbank[4 + pair], seb[pair][:], ALU.mult),
                    waits=w + [s_v], inc=s_aev)

    phase_C()
    B.flush_block()

    if "attn" in debug:
        d_at = nc.dram_tensor("d_at", [16, 128, T], F32, kind="ExternalOutput").ap()
        dbg["d_at"] = d_at
        s_da = sem("da")
        for i in range(16):
            I("gpsimd", lambda e, i=i: e.dma_start(out=d_at[i], in_=attnb[:, i, :]), waits=[(s_aev, 32)], inc=s_da, inck=16)
        I("gpsimd", lambda e: e.nop(), waits=[s_da])
        B.flush_block()
    es_c.close()
    es_attn.close()

    es_d = ExitStack()
    wos = [sb("wos%d" % i, [128, 16, 128], BF16, es_d) for i in range(3)]
    r_t = sb("r_t", [128, 16, T], F32, es_d)
    xr = [sb("xr0", [128, T], F32, es_d), sb("xr1", [128, T], F32, es_d)]
    rsq = [sb("rsq0", [128, 512], F32, es_d), sb("rsq1", [128, 512], F32, es_d)]
    lnrow = sb("lnrow", [1, 4 * T], F32, es_d)
    abB = sb("abB", [128, 2 * T], F32, es_d)
    xnb = attnb
    epsr = sb("epsr", [1, 1], F32, es_d)

    s_xrl = [sem("xrl%d" % i) for i in range(2)]
    s_omm = sem("omm")
    s_resid = sem("resid")
    s_sq = sem("sq")
    s_sqm = sem("sqm")
    s_lnm = sem("lnm")
    s_ab = sem("ab")
    s_xn = sem("xn")

    def phase_D1():
        for oc in range(16):
            w = [tg["wq%d" % (13 + oc % 3)]] if oc < 3 else [tg["wo%d" % (oc - 3)]]
            if oc >= 3:
                w.append((s_omm, (oc - 2) * 2))
            I("gpsimd", lambda e, oc=oc: e.dma_start(
                out=wos[oc % 3][:],
                in_=bass.AP(wo.tensor, oc * 128, [[D, 128], [128 * D, 16], [1, 128]])),
                waits=w, inc=swd_w[oc % 3], inck=16)
            tg["wo%d" % oc] = (swd_w[oc % 3], swd_w[oc % 3].n)
            I("sync", lambda e, oc=oc: e.dma_start(
                out=xr[oc % 2][:],
                in_=bass.AP(xqT.tensor, oc * 128 * T, [[T, 128], [1, T]])),
                waits=([(s_resid, (oc - 1) * 2)] if oc >= 2 else []),
                inc=s_xrl[oc % 2], inck=16)
        for oc in range(16):
            for tc in range(2):
                tsl = slice(tc * 512, (tc + 1) * 512)
                pq = (p0, p1)[tc]
                for fc in range(16):
                    w = []
                    if fc == 0:
                        w = [tg["wo%d" % oc]]
                        if oc > 0:
                            w.append((s_resid, (oc - 1) * 2 + tc + 1))
                    I("tensor", lambda e, oc=oc, fc=fc, tsl=tsl, pq=pq: e.matmul(
                        pq[:], wos[oc % 3][:, fc, :], attnb[:, fc, tsl],
                        start=(fc == 0), stop=(fc == 15)),
                        waits=w, inc=(s_omm if fc == 15 else None))
                I("vector", lambda e, oc=oc, tc=tc, tsl=tsl, pq=pq: e.scalar_tensor_tensor(
                    r_t[:, oc, tsl], pq[:], bo_t[:, oc:oc + 1], xr[oc % 2][:, tsl],
                    ALU.add, ALU.add),
                    waits=[(s_omm, oc * 2 + tc + 1), (s_xrl[oc % 2], 16 * (oc // 2 + 1)), s_v],
                    inc=s_resid)
                I("scalar", lambda e, oc=oc, tc=tc, tsl=tsl: e.activation(
                    rsq[tc][:], r_t[:, oc, tsl], AF.Square),
                    waits=[(s_resid, oc * 2 + tc + 1), (s_sqm, max(0, (oc - 1) * 2 + tc + 1)), s_sc],
                    inc=s_sq)
                pm = (p2, p3)[tc]
                psq = (p4, p5)[tc]
                I("tensor", lambda e, oc=oc, tsl=tsl, pm=pm: e.matmul(
                    pm[0:1, :], onescol_f[:], r_t[:, oc, tsl],
                    start=(oc == 0), stop=(oc == 15)),
                    waits=[(s_resid, oc * 2 + tc + 1)],
                    inc=(s_lnm if oc == 15 else None))
                I("tensor", lambda e, oc=oc, tc=tc, psq=psq: e.matmul(
                    psq[0:1, :], onescol_f[:], rsq[tc][:],
                    start=(oc == 0), stop=(oc == 15)),
                    waits=[(s_sq, oc * 2 + tc + 1)],
                    inc=s_sqm)
        mrow = lnrow[0:1, 0:T]
        sqrow = lnrow[0:1, T:2 * T]
        arow = lnrow[0:1, 2 * T:3 * T]
        brow = lnrow[0:1, 3 * T:4 * T]
        I("vector", lambda e: e.memset(epsr[:], 1e-5), waits=[s_v], inc=s_v)
        I("vector", lambda e: e.tensor_copy(lnrow[0:1, 0:512], p2[0:1, :]), waits=[(s_lnm, 2), (s_sqm, 32), s_v], inc=s_v)
        I("vector", lambda e: e.tensor_copy(lnrow[0:1, 512:1024], p3[0:1, :]), waits=[s_v], inc=s_v)
        I("vector", lambda e: e.tensor_copy(lnrow[0:1, T:T + 512], p4[0:1, :]), waits=[s_v], inc=s_v)
        I("vector", lambda e: e.tensor_copy(lnrow[0:1, T + 512:2 * T], p5[0:1, :]), waits=[s_v], inc=s_v)
        I("vector", lambda e: e.tensor_scalar(mrow, mrow, 1.0 / D, None, ALU.mult), waits=[s_v], inc=s_v)
        I("vector", lambda e: e.tensor_scalar(sqrow, sqrow, 1.0 / D, None, ALU.mult), waits=[s_v], inc=s_v)
        I("vector", lambda e: e.scalar_tensor_tensor(arow, mrow, 1.0, mrow, ALU.mult, ALU.mult), waits=[s_v], inc=s_v)
        I("vector", lambda e: e.tensor_tensor(sqrow, sqrow, arow, ALU.subtract), waits=[s_v], inc=s_v)
        I("scalar", lambda e: e.activation(arow, sqrow, AF.Sqrt, bias=epsr[0:1, 0:1]), waits=[s_v, s_sc], inc=s_sc)
        I("vector", lambda e: e.reciprocal_approx_fast(arow, arow), waits=[s_sc, s_v], inc=s_v)
        I("vector", lambda e: e.scalar_tensor_tensor(brow, mrow, -1.0, arow, ALU.mult, ALU.mult), waits=[s_v], inc=s_v)
        I("vector", lambda e: e.nop(), waits=[s_v], inc=s_ab)
        I("sync", lambda e: e.dma_start(out=hbm_ab[0], in_=arow), waits=[s_ab], inc=s_ab, inck=16)
        I("sync", lambda e: e.dma_start(out=hbm_ab[1], in_=brow), inc=s_ab, inck=16)
        I("sync", lambda e: e.nop(), waits=[s_ab])
        I("sync", lambda e: e.dma_start(out=abB[:], in_=bass.AP(hbm_ab.tensor, 0, [[0, 128], [1, 2 * T]])), inc=s_ab, inck=16)
        for oc in range(16):
            I("vector", lambda e, oc=oc: e.tensor_tensor(xr[0][:], r_t[:, oc, :], abB[:, 0:T], ALU.mult),
                waits=[(s_ab, 49), s_v], inc=s_v)
            I("vector", lambda e, oc=oc: e.tensor_tensor(xnb[:, oc, :], xr[0][:], abB[:, T:2 * T], ALU.add),
                waits=[s_v], inc=s_xn)

    phase_D1()
    B.flush_block()
    if "xn" in debug:
        dbg["d_xn"] = nc.dram_tensor("d_xn", [16, 128, T], F32, kind="ExternalOutput").ap()
        s_dxn = sem("dxn")
        for oc in range(16):
            I("gpsimd", lambda e, oc=oc: e.dma_start(out=dbg["d_xn"][oc], in_=xnb[:, oc, :]), inc=s_dxn, inck=16)
        I("gpsimd", lambda e: e.nop(), waits=[s_dxn])
        B.flush_block()

    es_d.close()
    es_moe = ExitStack()
    gidx = sb("gidx", [128, 32 * NE], I16, es_moe)
    gidxf = sb("gidxf", [128, 32 * NE], F32, es_moe)
    cwsel = sb("cwsel", [128, 8, K_TOP], F32, es_moe)
    epsF = sb("epsF", [128, 1], F32, es_moe)
    yidxw = sb("yidxw", [128, 512], I16, es_moe)
    es_w = ExitStack()
    wring = [sb("wring%d" % i, [128, 12288], BF16, es_w) for i in range(3)]
    es_d2 = ExitStack()
    xnt = [sb("xnt0", [128, D], BF16, es_d2), sb("xnt1", [128, D], BF16, es_d2)]
    zf = sb("zf", [128, 8, NE], F32, es_d2)
    iotaE = sb("iotaE", [128, NE], F32, es_d2)
    ones32 = sb("ones32", [128, NE], F32, es_d2)
    onesT = sb("onesT", [128, 128], BF16, es_d2)
    indb = sb("indb", [128, 8, NE], BF16, es_d2)
    rkm = sb("rkm", [128, 8, NE], F32, es_d2)
    cwu = sb("cwu", [128, 8, NE], F32, es_d2)
    wsumr = sb("wsumr", [128, 8], F32, es_d2)
    pposF = sb("pposF", [128, 8, NE], F32, es_d2)
    ra = sb("ra", [128, 64], F32, es_d2)
    rb = sb("rb", [128, NE], F32, es_d2)
    rc = sb("rc", [128, NE], F32, es_d2)
    identF = sb("identF", [128, 128], F32, es_d2)
    slotT = sb("slotT", [8, 1024], F32, es_d2)
    flb = sb("flb", [128, 16], F32, es_d2)
    cntrow = sb("cntrow", [1, NE], F32, es_d2)
    ccs = sb("ccs", [8, NE], F32, es_d2)
    pidf = sb("pidf", [8, 1], F32, es_d2)
    iota8 = sb("iota8", [8, 1], F32, es_d2)
    mask8 = sb("mask8", [8, 1], F32, es_d2)
    cprow = sb("cprow", [1, NE], F32, es_d2)
    cpB = sb("cpB", [128, NE], F32, es_d2)
    stage = sb("stage", [128, 64, 64], F32, es_d2)
    idxw = sb("idxw", [128, 512], I16, es_d2)
    ztile = sb("ztile", [128, 2050], F32, es_d2)
    pid_u = sb("pid_u", [8, 1], U32, es_d2)
    hbm_cp = nc.dram_tensor("hbm_cp", [1, NE], F32, kind="Internal").ap()
    cc_in = nc.dram_tensor("cc_in", [1, NE], F32, kind="Internal").ap()
    cc_out = nc.dram_tensor("cc_out", [8, NE], F32, kind="Internal", addr_space=("Shared" if cc_shared else "Local")).ap()

    s_z = sem("sz")
    s_zev = sem("zev")
    s_pmm = sem("pmm")
    s_ppe = sem("ppe")
    s_cnt = sem("cnt")
    s_xtr = sem("xtr")
    s_xtv = sem("xtv")
    s_xnw = sem("xnw")
    s_psw = sem("psw")
    s_idr = sem("idr")
    s_cc = sem("scc")
    s_ccd = sem("ccd")
    s_cpd = sem("cpd")
    s_keep = sem("keep")
    s_stg = sem("stg")
    s_sca = sem("sca")
    s_rz = sem("srz")
    s_gx = sem("gxr")
    s_stv = sem("stv")
    s_iev = sem("iev")

    regCC = nc.gpsimd.alloc_register()
    regT = nc.gpsimd.alloc_register()
    reg1k = nc.gpsimd.alloc_register()

    def phase_D2():
        I("gpsimd", lambda e: e.reg_mov(regCC, CC))
        I("gpsimd", lambda e: e.reg_mov(regT, T))
        I("gpsimd", lambda e: e.reg_mov(reg1k, 1024))
        for t, hd in ((0, wgd), (1, wud)):
            I("gpsimd", lambda ee, t=t, hd=hd: ee.dma_start(
                out=wring[t][:], in_=bass.AP(hd.tensor, 0, [[16 * DI, 128], [1, 16 * DI]])),
                waits=[tg["wo%d" % (13 + t)]], inc=swd_w[t], inck=16)
            tg["we0_%d" % t] = (swd_w[t], swd_w[t].n)
        for tt in range(8):
            for oc in range(16):
                w = [(s_xn, oc + 1)]
                nt = tt * 16 + oc
                if nt > 1:
                    w.append((s_xtv, nt - 1))
                pb = (p2, p3)[nt % 2]
                I("tensor", lambda e, tt=tt, oc=oc, pb=pb: e.transpose(
                    pb[:, 0:64].bitcast(BF16), xnb[:, oc, tt * 128:(tt + 1) * 128], ident[:]),
                    waits=w, inc=s_xtr)
                w2 = [(s_xtr, tt * 16 + oc + 1), s_v]
                if tt >= 2 and oc == 0:
                    w2.append((s_xnw, 16 * (tt - 1)))
                I("vector", lambda e, tt=tt, oc=oc, pb=pb: e.tensor_copy(
                    xnt[tt % 2][:, oc * 128:(oc + 1) * 128], pb[:, 0:64].bitcast(BF16)),
                    waits=w2, inc=s_xtv)
            I("sync", lambda e, tt=tt: e.dma_start(
                out=bass.AP(hbm_xn.tensor, tt * 128 * D, [[D, 128], [1, D]]),
                in_=xnt[tt % 2][:]),
                waits=[(s_xtv, (tt + 1) * 16), (s_xnw, 16 * tt)], inc=s_xnw, inck=16)

        I("gpsimd", lambda e: e.iota(iotaE[:], [[1, NE]], base=0, channel_multiplier=0, allow_small_or_imprecise_dtypes=True), inc=s_g)
        I("gpsimd", lambda e: e.iota(iota8[:], [[0, 1]], base=0, channel_multiplier=1, allow_small_or_imprecise_dtypes=True), inc=s_g)
        I("vector", lambda e: e.memset(ones32[:], 1.0), waits=[s_v], inc=s_v)
        I("vector", lambda e: e.memset(onesT[:], 1.0), waits=[s_v], inc=s_v)
        I("vector", lambda e: e.memset(stage[:], 0.0), waits=[s_v], inc=s_stg)
        I("vector", lambda e: e.memset(ztile[:], 0.0), waits=[s_v], inc=s_stg)
        I("vector", lambda e: e.memset(epsF[:], 1e-5), waits=[s_v], inc=s_v)
        I("vector", lambda e: e.tensor_copy(identF[:], ident[:]), waits=[s_v], inc=s_v)

        for tau in range(8):
            tsl = slice(tau * 128, (tau + 1) * 128)
            for hc in range(16):
                w = [(s_xn, hc + 1), (s_gw, 16)]
                if hc == 0 and tau > 0:
                    w.append((s_zev, tau))
                I("tensor", lambda e, hc=hc, tsl=tsl: e.matmul(
                    p6[:, 0:NE], xnb[:, hc, tsl], gwb[:, hc, :],
                    start=(hc == 0), stop=(hc == 15)),
                    waits=w, inc=(s_z if hc == 15 else None))
            I("vector", lambda e, tau=tau: e.tensor_copy(zf[:, tau, :], p6[:, 0:NE]),
                waits=[(s_z, tau + 1), s_v], inc=s_zev)
            I("vector", lambda e, tau=tau: e.tensor_reduce(ra[:, 0:1], zf[:, tau, :], mybir.AxisListType.X, ALU.max), waits=[s_v], inc=s_v)
            I("vector", lambda e: e.tensor_scalar(ra[:, 1:2], ra[:, 0:1], -1.0, None, ALU.mult), waits=[s_v], inc=s_v)
            I("scalar", lambda e, tau=tau: e.activation(rb[:], zf[:, tau, :], AF.Exp, bias=ra[:, 1:2], accum_out=ra[:, 2:3]), waits=[s_v, s_sc], inc=s_sc)
            I("vector", lambda e: e.reciprocal(ra[:, 3:4], ra[:, 2:3]), waits=[s_sc, s_v], inc=s_v)
            I("vector", lambda e: e.tensor_scalar(rb[:], rb[:], ra[:, 3:4], None, ALU.mult), waits=[s_v], inc=s_v)
            I("vector", lambda e: e.max(out=ra[:, 8:16], in_=rb[:]), waits=[s_v], inc=s_v)
            I("vector", lambda e: e.match_replace(out=rc[:], in_to_replace=ra[:, 8:16], in_values=rb[:], imm_value=0.0), waits=[s_v], inc=s_v)
            I("vector", lambda e, tau=tau: e.tensor_tensor(cwu[:, tau, :], rb[:], rc[:], ALU.subtract), waits=[s_v], inc=s_v)
            I("vector", lambda e, tau=tau: e.tensor_reduce(wsumr[:, tau:tau + 1], ra[:, 8:16], mybir.AxisListType.X, ALU.add), waits=[s_v], inc=s_v)
            I("vector", lambda e, tau=tau: e.tensor_scalar(ra[:, 16:48], cwu[:, tau, :], 0.0, None, ALU.is_gt), waits=[s_v], inc=s_v)
            I("vector", lambda e, tau=tau: e.tensor_copy(indb[:, tau, :], ra[:, 16:48]), waits=[s_v], inc=s_v)
            I("vector", lambda e: e.tensor_tensor_scan(rc[:], ones32[:], ra[:, 16:48], 0.0, ALU.mult, ALU.add), waits=[s_v], inc=s_v)
            I("vector", lambda e, tau=tau: e.tensor_tensor(rc[:], rc[:], ra[:, 16:48], ALU.mult), waits=[s_v], inc=s_v)
            I("vector", lambda e, tau=tau: e.tensor_scalar(rkm[:, tau, :], rc[:], -1.0, None, ALU.add), waits=[s_v], inc=s_v)
            for tp in range(tau + 1):
                w = [(s_v, B.sems["sv"].n)]
                if tp == 0 and tau > 0:
                    w.append((s_ppe, tau))
                I("tensor", lambda e, tau=tau, tp=tp: e.matmul(
                    p7[:, 0:NE],
                    (triu if tp == tau else onesT)[:],
                    indb[:, tp, :],
                    start=(tp == 0), stop=(tp == tau)),
                    waits=w, inc=(s_pmm if tp == tau else None))
            I("vector", lambda e, tau=tau: e.tensor_copy(pposF[:, tau, :], p7[:, 0:NE]),
                waits=[(s_pmm, tau + 1), s_v], inc=s_ppe)
            I("tensor", lambda e, tau=tau: e.matmul(
                p5[0:1, 0:NE], onescol_b[:], indb[:, tau, :],
                start=(tau == 0), stop=(tau == 7)),
                waits=[(s_v, B.sems["sv"].n)], inc=(s_cnt if tau == 7 else None))
            I("vector", lambda e, tau=tau: e.scalar_tensor_tensor(ra[:, 16:48], iotaE[:], float(CC), pposF[:, tau, :], ALU.mult, ALU.add), waits=[(s_ppe, tau + 1), (s_g, 6), s_v], inc=s_v)
            I("vector", lambda e, tau=tau: e.tensor_scalar(rc[:], pposF[:, tau, :], float(CC), None, ALU.is_lt), waits=[s_v], inc=s_v)
            I("vector", lambda e: e.tensor_scalar(ra[:, 16:48], ra[:, 16:48], float(-NSLOT), None, ALU.add), waits=[s_v], inc=s_v)
            I("vector", lambda e: e.tensor_tensor(ra[:, 16:48], ra[:, 16:48], rc[:], ALU.mult), waits=[s_v], inc=s_v)
            I("vector", lambda e: e.tensor_scalar(ra[:, 16:48], ra[:, 16:48], float(NSLOT), None, ALU.add), waits=[s_v], inc=s_v)
            for k in range(K_TOP):
                w = [s_v]
                if k == 0:
                    w.append((s_stv, tau))
                I("vector", lambda e, tau=tau, k=k: e.tensor_scalar(rc[:], rkm[:, tau, :], float(k), None, ALU.is_equal), waits=w, inc=s_v)
                I("vector", lambda e: e.tensor_tensor(rb[:], ra[:, 16:48], rc[:], ALU.mult), waits=[s_v], inc=s_v)
                I("vector", lambda e, tau=tau, k=k: e.tensor_reduce(ra[:, 48 + k:49 + k], rb[:], mybir.AxisListType.X, ALU.add), waits=[s_v], inc=s_v)
            I("tensor", lambda e, tau=tau: e.transpose(p4[0:8, 0:128], ra[:, 48:56], identF[:]),
                waits=[(s_v, B.sems["sv"].n)], inc=s_stv)
            I("vector", lambda e, tau=tau: e.tensor_copy(slotT[:, tau * 128:(tau + 1) * 128], p4[0:8, 0:128]),
                waits=[(s_stv, tau + 1), s_v], inc=s_v)
        I("sync", lambda e: e.dma_start(out=bass.AP(hbm_ps.tensor, 0, [[1024, 8], [1, 1024]]), in_=slotT[:]),
            waits=[(s_v, B.sems["sv"].n)], inc=s_psw, inck=16)
        I("sync", lambda e: e.dma_start(out=bass.AP(hbm_yi.tensor, 0, [[128, 8], [1024, 8], [1, 128]]), in_=slotT[:]),
            waits=[(s_v, B.sems["sv"].n)], inc=s_psw, inck=16)
        I("vector", lambda e: e.memset(idxw[:], 0), waits=[s_v], inc=s_iev)
        I("vector", lambda e: e.memset(yidxw[:], 0), waits=[s_v], inc=s_iev)
        for gi, (srcT, dstT) in enumerate(((hbm_ps, idxw), (hbm_yi, yidxw))):
            for blk in range(4):
                w = [(s_psw, 32), (s_stv, s_stv.n)]
                if gi == 0 and blk == 0:
                    w.append((s_iev, 2))
                I("sync", lambda e, srcT=srcT, blk=blk: e.dma_start(
                    out=flb[:], in_=bass.AP(srcT.tensor, blk * 2048, [[16, 128], [1, 16]])),
                    waits=w, inc=s_idr, inck=16)
                I("tensor", lambda e: e.transpose(p4[0:16, 0:128], flb[:], identF[:]),
                    waits=[(s_idr, s_idr.n), (s_iev, s_iev.n)], inc=s_stv)
                I("vector", lambda e, dstT=dstT, blk=blk: e.tensor_copy(
                    dstT[0:16, blk * 128:(blk + 1) * 128], p4[0:16, 0:128]),
                    waits=[(s_stv, s_stv.n), s_v], inc=s_iev)
        I("sync", lambda e: e.dma_start(out=idxw[16:32, :], in_=idxw[0:16, :]),
            waits=[(s_iev, 6), (s_idr, 128)], inc=s_idr, inck=16)
        I("sync", lambda e: e.dma_start(out=yidxw[16:32, :], in_=yidxw[0:16, :]),
            waits=[(s_iev, 10), (s_idr, 144)], inc=s_idr, inck=16)
        I("sync", lambda e: e.dma_start(out=yidxw[32:64, :], in_=yidxw[0:32, :]),
            waits=[(s_idr, 160)], inc=s_idr, inck=16)

        I("vector", lambda e: e.tensor_copy(cntrow[:], p5[0:1, 0:NE]), waits=[(s_cnt, 1), s_v], inc=s_v)
        I("sync", lambda e: e.dma_start(out=cc_in[:], in_=cntrow[:]), waits=[(s_v, B.sems["sv"].n)], inc=s_ccd, inck=16)
        I("gpsimd", lambda e: e.collective_compute(
            "AllGather", ALU.bypass, replica_groups=[list(range(8))],
            ins=[cc_in], outs=[cc_out]), waits=[(s_ccd, 16)], inc=s_cc)
        I("sync", lambda e: e.dma_start(out=ccs[:], in_=cc_out[:]), waits=[(s_cc, 1)], inc=s_ccd, inck=16)
        I("sync", lambda e: e.dma_start(out=pid_u[:], in_=bass.AP(nc.partition_id_tensor, 0, [[0, 8], [1, 1]])), inc=s_cpd, inck=16)
        I("vector", lambda e: e.tensor_copy(pidf[:], pid_u[:]), waits=[(s_cpd, 16), s_v], inc=s_v)
        I("vector", lambda e: e.tensor_tensor(mask8[:], iota8[:], pidf[:], ALU.is_lt), waits=[(s_g, s_g.n), s_v], inc=s_v)
        I("tensor", lambda e: e.matmul(p6[0:1, 0:NE], mask8[:], ccs[:], start=True, stop=True),
            waits=[(s_v, B.sems["sv"].n), (s_ccd, 32), (s_zev, 8)], inc=s_z)
        I("vector", lambda e: e.tensor_copy(cprow[:], p6[0:1, 0:NE]), waits=[(s_z, 9), s_v], inc=s_v)
        I("sync", lambda e: e.dma_start(out=hbm_cp[:], in_=cprow[:]), waits=[(s_v, B.sems["sv"].n)], inc=s_cpd, inck=16)
        I("sync", lambda e: e.dma_start(out=cpB[:], in_=bass.AP(hbm_cp.tensor, 0, [[0, 128], [1, NE]])), waits=[(s_cpd, 32)], inc=s_cpd, inck=16)

        for tau in range(8):
            I("vector", lambda e, tau=tau: e.tensor_tensor(ra[:, 16:48], pposF[:, tau, :], cpB[:], ALU.add), waits=[(s_cpd, 48), s_v], inc=s_v)
            I("vector", lambda e: e.tensor_scalar(ra[:, 16:48], ra[:, 16:48], float(CAP), None, ALU.is_lt), waits=[s_v], inc=s_v)
            I("vector", lambda e, tau=tau: e.tensor_scalar(rc[:], pposF[:, tau, :], float(CC), None, ALU.is_lt), waits=[s_v], inc=s_v)
            I("vector", lambda e: e.tensor_tensor(ra[:, 16:48], ra[:, 16:48], rc[:], ALU.mult), waits=[s_v], inc=s_v)
            I("vector", lambda e, tau=tau: e.tensor_tensor(ra[:, 16:48], ra[:, 16:48], cwu[:, tau, :], ALU.mult), waits=[s_v], inc=s_v)
            I("vector", lambda e, tau=tau: e.reciprocal(ra[:, 0:1], wsumr[:, tau:tau + 1]), waits=[s_v], inc=s_v)
            I("vector", lambda e: e.tensor_scalar(ra[:, 16:48], ra[:, 16:48], ra[:, 0:1], None, ALU.mult), waits=[s_v], inc=s_v)
            for k in range(K_TOP):
                I("vector", lambda e, tau=tau, k=k: e.tensor_scalar(rc[:], rkm[:, tau, :], float(k), None, ALU.is_equal), waits=[s_v], inc=s_v)
                I("vector", lambda e: e.tensor_tensor(rb[:], ra[:, 16:48], rc[:], ALU.mult), waits=[s_v], inc=s_v)
                I("vector", lambda e, tau=tau, k=k: e.tensor_reduce(cwsel[:, tau, k:k + 1], rb[:], mybir.AxisListType.X, ALU.add), waits=[s_v], inc=(s_keep if k == 7 else s_v))

        for tau in range(8):
            I("gpsimd", lambda e, tau=tau: e.iota(
                bass.AP(stage, tau * 64, [[4096, 128], [512, 8], [1, 1]]),
                [[0, 8], [0, 1]], base=tau * 128, channel_multiplier=1,
                allow_small_or_imprecise_dtypes=True),
                waits=[(s_stg, 2)], inc=s_g)
        for zi in range(4):
            I("sync", lambda e, zi=zi: e.dma_start(
                out=bass.AP(hbm_roster.tensor, zi * 128 * 2050, [[2050, 128], [1, 2050]]),
                in_=ztile[:]),
                waits=[(s_stg, 2)], inc=s_rz, inck=16)
        I("gpsimd", lambda e: e.load_library(library_config.mlp), waits=[(s_g, s_g.n)])
        for ch in range(8):
            w = [(s_rz, 64), (s_idr, 144), (s_iev, 6), (s_g, s_g.n)]
            w.append(tg["wkv2"] if ch == 0 else tg["sca%d" % (ch - 1)])
            B.G(lambda e, h, ch=ch: e.dma_scatter_add(
                hbm_roster[:], stage[:, 8 * ch:8 * (ch + 1), :], idxw[:, 64 * ch:64 * (ch + 1)],
                1024, reg1k, 64, prepare_only=True, sem=h),
                swd_m, waits=w)
            tg["sca%d" % ch] = (swd_m, swd_m.n)
        I("vector", lambda e: e.memset(gidxf[:], 0.0), waits=[s_v], inc=s_gx)
        I("vector", lambda e: e.memset(gidx[:], 0), waits=[s_v], inc=s_gx)

    phase_D2()
    B.flush_block()
    es_d2.close()
    if "route" in debug:
        dbg["d_roster"] = nc.dram_tensor("d_roster", [NSLOT_PAD, 2], F32, kind="ExternalOutput").ap()
        s_dr = sem("dr")
        I("gpsimd", lambda e: e.dma_start(out=dbg["d_roster"][0:8200], in_=bass.AP(hbm_roster.tensor, 0, [[64, 8200], [1, 2]])), waits=[tg["sca7"]], inc=s_dr, inck=16)
        I("gpsimd", lambda e: e.dma_start(out=dbg["d_roster"][8200:16400], in_=bass.AP(hbm_roster.tensor, 8200 * 64, [[64, 8200], [1, 2]])), waits=[(s_dr, 16)], inc=s_dr, inck=16)
        I("gpsimd", lambda e: e.nop(), waits=[s_dr])
        B.flush_block()

    es_e = ExitStack()
    xg = [sb("xg%d" % i, [128, 16, CC], BF16, es_e) for i in range(3)]
    sg = sb("sg", [128, 6, CC], BF16, es_e)
    au = sb("au", [128, 6, CC], BF16, es_e)
    ystage = [bass.AP(attnb, 0, [[16384, 128], [2048, 4], [1, 2048]]),
              bass.AP(attnb, 8192, [[16384, 128], [2048, 4], [1, 2048]])]


    s_gxc = sem("gxc")
    s_gu = sem("sgu")
    s_si = sem("ssi")
    s_am = sem("sam")
    s_dm = sem("sdm")
    s_ye = sem("sye")
    s_yw = sem("syw")

    def phase_E():
        I("sync", lambda ee: ee.dma_start(
            out=gidxf[0:16, :],
            in_=bass.AP(hbm_roster.tensor, 0, [[64, 16], [1024, 32 * NE]]),
            allow_slow_non_contiguous=True),
            waits=[tg["sca7"], (s_gx, 2)], inc=s_gx, inck=16)
        I("sync", lambda ee: ee.dma_start(
            out=gidxf[16:32, :],
            in_=bass.AP(hbm_roster.tensor, 0, [[64, 16], [1024, 32 * NE]]),
            allow_slow_non_contiguous=True),
            inc=s_gx, inck=16)
        I("vector", lambda ee: ee.tensor_copy(gidx[0:32, :], gidxf[0:32, :]),
            waits=[(s_gx, 2 + 32), s_v], inc=s_gxc)
        I("sync", lambda ee: ee.dma_start(out=gidx[32:64, :], in_=gidx[0:32, :]),
            waits=[(s_gxc, 1)], inc=s_gx, inck=16)
        for e in range(NE):
            par = e % 2
            pg3 = e % 3
            w = [(s_gxc, 1), (s_gx, 2 + 32 + 16), (s_xnw, 128)]
            if e >= 2:
                w.append(tg["xg%d" % (e - 2)])
            if e >= 3:
                w.append((s_gu, (e - 2) * 12))
            if e < 3:
                w.append(tg["xc22"])
                w.append(tg["xc23"])
            B.G(lambda ee, h, e=e, pg3=pg3: ee.dma_gather(
                xg[pg3][:], hbm_xn[:], gidx[:, 32 * e:32 * (e + 1)], CC, regCC, D,
                transpose=True, prepare_only=True, sem=h, queue_num=1),
                swd_y[par], waits=w, queue_num=1)
            tg["xg%d" % e] = (swd_y[par], swd_y[par].n)
            for t, (hd, inner) in enumerate(((wgd, DI), (wud, DI), (wdd, D))):
                if e == 0 and t < 2:
                    continue
                b = t
                w = []
                if e >= 1:
                    if t == 0:
                        w = [(s_gu, (e - 1) * 12 + 11)]
                    elif t == 1:
                        w = [(s_gu, (e - 1) * 12 + 12)]
                    else:
                        w = [(s_dm, e * 16)]
                    w.append(tg["we%d_%d" % (e - 1, t)])
                else:
                    w.append(tg["wo%d" % (13 + t)])
                if t < 2:
                    ap = bass.AP(hd.tensor, e * D * DI, [[16 * DI, 128], [1, 16 * DI]])
                else:
                    ap = bass.AP(hd.tensor, e * DI * D, [[6 * D, 128], [1, 6 * D]])
                I("gpsimd", lambda ee, b=b, ap=ap: ee.dma_start(out=wring[b][:], in_=ap),
                    waits=w, inc=swd_w[b], inck=16)
                tg["we%d_%d" % (e, t)] = (swd_w[b], swd_w[b].n)
            wgu = [tg["we%d_0" % e], tg["we%d_1" % e], tg["we%d_2" % e]]
            for ic in range(6):
                pg = (p0, p1)[ic % 2]
                pu = (p2, p3)[ic % 2]
                for hc in range(16):
                    w = []
                    if hc == 0:
                        w = [wgu[0], tg["xg%d" % e]]
                        w.append((s_si, e * 6 + ic - 1) if ic >= 2 else (s_si, e * 6))
                    I("tensor", lambda ee, e=e, ic=ic, hc=hc, pg=pg, pg3=e % 3: ee.matmul(
                        pg[:, 0:CC], wring[0][:, hc * DI + ic * 128: hc * DI + ic * 128 + 128],
                        xg[pg3][:, hc, :], start=(hc == 0), stop=(hc == 15)),
                        waits=w, inc=(s_gu if hc == 15 else None))
                for hc in range(16):
                    w = []
                    if hc == 0:
                        w = [wgu[1]]
                        w.append((s_am, e * 6 + ic - 1) if ic >= 2 else (s_am, e * 6))
                    I("tensor", lambda ee, e=e, ic=ic, hc=hc, pu=pu, pg3=e % 3: ee.matmul(
                        pu[:, 0:CC], wring[1][:, hc * DI + ic * 128: hc * DI + ic * 128 + 128],
                        xg[pg3][:, hc, :], start=(hc == 0), stop=(hc == 15)),
                        waits=w, inc=(s_gu if hc == 15 else None))
                I("scalar", lambda ee, e=e, ic=ic, pg=pg: ee.activation(
                    sg[:, ic, :], pg[:, 0:CC], AF.Silu),
                    waits=[(s_gu, e * 12 + 2 * ic + 1), (s_am, max(0, (e - 1) * 6 + ic + 1)), s_sc], inc=s_si)
                I("vector", lambda ee, e=e, ic=ic, pu=pu: ee.tensor_tensor(
                    au[:, ic, :], sg[:, ic, :], pu[:, 0:CC], ALU.mult),
                    waits=[(s_si, e * 6 + ic + 1), (s_gu, e * 12 + 2 * ic + 2), (s_dm, e * 16), s_v], inc=s_am)
            for sc in range(4):
                for icc in range(6):
                    for fc in range(4):
                        w = []
                        if icc == 0 and fc == 0:
                            w = [wgu[2], (s_ye, e * 16 + sc * 4)]
                        if fc == 0:
                            w.append((s_am, e * 6 + icc + 1))
                        pq = (p4, p5, p6, p7)[fc]
                        I("tensor", lambda ee, e=e, sc=sc, icc=icc, fc=fc, pq=pq: ee.matmul(
                            pq[:], au[:, icc, sc * 128:(sc + 1) * 128],
                            wring[2][:, icc * D + fc * 512: icc * D + fc * 512 + 512],
                            start=(icc == 0), stop=(icc == 5)),
                            waits=w, inc=(s_dm if icc == 5 else None))
                for fc in range(4):
                    pq = (p4, p5, p6, p7)[fc]
                    w = [(s_dm, e * 16 + sc * 4 + fc + 1)]
                    if sc == 0 and fc == 0:
                        w.append((s_yw, 16 * max(0, e - 1)))
                    I("scalar", lambda ee, e=e, sc=sc, fc=fc, pq=pq: ee.activation(
                        ystage[e % 2][:, sc, fc * 512:(fc + 1) * 512], pq[:], AF.Identity),
                        waits=w + [s_sc], inc=s_ye)
            I("sync", lambda ee, e=e: ee.dma_start(
                out=bass.AP(hbm_y.tensor, e * CC * D, [[D, 128], [128 * D, 4], [1, D]]),
                in_=ystage[e % 2]),
                waits=[(s_ye, (e + 1) * 16), (s_yw, 16 * e)], inc=s_yw, inck=16)

    phase_E()
    B.flush_block()
    es_e.close()
    es_w.close()

    es_f = ExitStack()
    gb = [sb("gb%d" % i, [128, K_TOP, D], BF16, es_f) for i in range(3)]
    accA = sb("accA", [128, D], F32, es_f)
    accB = sb("accB", [128, D], F32, es_f)
    outb = [sb("outb0", [128, D], F32, es_f), sb("outb1", [128, D], F32, es_f)]
    xna = [sb("xna%d" % i, [128, D], BF16, es_f) for i in range(3)]

    zsm = sb("zsm", [128, 256], BF16, es_f)
    fcol = sb("fcol", [128, 8], F32, es_f)

    s_fz = sem("fz")
    s_yir = sem("yir")
    s_xnl = [sem("xnl%d" % i) for i in range(3)]
    s_acc = sem("sacc")
    s_out = sem("sout")

    def phase_F():
        sq_mark = {}
        I("vector", lambda e: e.memset(zsm[:], 0.0), waits=[s_v], inc=s_v)
        I("sync", lambda e: e.dma_start(
            out=bass.AP(hbm_y.tensor, NSLOT * D, [[256, 128], [1, 256]]), in_=zsm[:]),
            waits=[(s_v, B.sems["sv"].n)], inc=s_fz, inck=16)
        for tau in range(8):
            par = tau % 3
            w = [(s_yw, 16 * NE), (s_fz, 16), (s_idr, 176)]
            if tau >= 2:
                w.append(tg["yg%d" % (tau - 2)])
            if tau >= 3:
                w.append((s_acc, tau - 2))
            if tau < 2:
                w.append(tg["xg30"])
                w.append(tg["xg31"])
            B.G(lambda e, h, tau=tau, par=par: e.dma_gather(
                gb[par][:], hbm_y[:], yidxw[:, tau * 64:(tau + 1) * 64], T, regT, D,
                transpose=False, prepare_only=True, sem=h, queue_num=1),
                swd_y[tau % 2], waits=w, queue_num=1)
            tg["yg%d" % tau] = (swd_y[tau % 2], swd_y[tau % 2].n)
            w2 = [(s_xnw, 128)]
            if tau >= 3:
                w2.append((s_acc, tau - 2))
            I("sync", lambda e, tau=tau, par=par: e.dma_start(
                out=xna[par][:],
                in_=bass.AP(hbm_xn.tensor, tau * 128 * D, [[D, 128], [1, D]])),
                waits=w2, inc=s_xnl[par], inck=16)
            for j in range(K_TOP):
                src = accA if j % 2 == 0 else accB
                dst = accB if j % 2 == 0 else accA
                w3 = [s_v]
                if j == 0:
                    w3 += [tg["yg%d" % tau], (s_xnl[par], 16 * (tau // 3 + 1)), (s_keep, 8)]
                    if tau >= 1:
                        w3.append((s_sc, sq_mark[tau - 1]))
                I("vector", lambda e, tau=tau, j=j, par=par, src=src, dst=dst: e.scalar_tensor_tensor(
                    dst[:], gb[par][:, j, :], cwsel[:, tau, j:j + 1],
                    (xna[par][:] if j == 0 else src[:]),
                    ALU.mult, ALU.add),
                    waits=w3, inc=(s_acc if j == 7 else s_v))
            I("vector", lambda e, tau=tau: e.tensor_reduce(fcol[:, 0:1], accA[:], mybir.AxisListType.X, ALU.add), waits=[s_v, (s_acc, tau + 1)], inc=s_v)
            I("vector", lambda e: e.tensor_scalar(fcol[:, 0:1], fcol[:, 0:1], 1.0 / D, None, ALU.mult), waits=[s_v], inc=s_v)
            I("scalar", lambda e: e.activation(accB[:], accA[:], AF.Square, accum_out=fcol[:, 1:2]), waits=[(s_v, B.sems["sv"].n), s_sc], inc=s_sc)
            sq_mark[tau] = B.sems["ssc"].n
            I("vector", lambda e: e.tensor_scalar(fcol[:, 1:2], fcol[:, 1:2], 1.0 / D, None, ALU.mult), waits=[s_sc, s_v], inc=s_v)
            I("vector", lambda e: e.scalar_tensor_tensor(fcol[:, 2:3], fcol[:, 0:1], 1.0, fcol[:, 0:1], ALU.mult, ALU.mult), waits=[s_v], inc=s_v)
            I("vector", lambda e: e.tensor_tensor(fcol[:, 1:2], fcol[:, 1:2], fcol[:, 2:3], ALU.subtract), waits=[s_v], inc=s_v)
            I("scalar", lambda e: e.activation(fcol[:, 3:4], fcol[:, 1:2], AF.Sqrt, bias=epsF[:]), waits=[s_v, s_sc], inc=s_sc)
            I("vector", lambda e: e.reciprocal(fcol[:, 4:5], fcol[:, 3:4]), waits=[s_sc, s_v], inc=s_v)
            ob = outb[tau % 2]
            w4 = [s_v]
            if tau >= 2:
                w4.append((s_out, 16 * (tau - 1)))
            I("vector", lambda e, tau=tau, ob=ob: e.tensor_scalar(ob[:], accA[:], fcol[:, 0:1], None, ALU.subtract), waits=w4, inc=s_v)
            I("vector", lambda e, ob=ob: e.tensor_scalar(ob[:], ob[:], fcol[:, 4:5], None, ALU.mult), waits=[s_v], inc=s_v)
            I("sync", lambda e, tau=tau, ob=ob: e.dma_start(
                out=bass.AP(outd.tensor, tau * 128 * D, [[D, 128], [1, D]]),
                in_=ob[:]),
                waits=[(s_v, B.sems["sv"].n), (s_out, 16 * tau)], inc=s_out, inck=16)
        I("sync", lambda e: e.nop(), waits=[(s_out, 128)])

    phase_F()

    if "tbl" in debug:
        d_tbl = nc.dram_tensor("d_tbl", [4, 128, S], F32, kind="ExternalOutput").ap()
        dbg["d_tbl"] = d_tbl
        s_dbg = sem("dbg")
        I("sync", lambda e: e.dma_start(out=d_tbl[0], in_=cosK[:]), waits=[B.sems["ssc"], B.sems["sv"]], inc=s_dbg, inck=16)
        I("sync", lambda e: e.dma_start(out=d_tbl[1], in_=sinK[:]), inc=s_dbg, inck=16)
        I("sync", lambda e: e.dma_start(out=bass.AP(d_tbl.tensor, 2 * 128 * S, [[S, 128], [1, T]]), in_=cosQ[:]), inc=s_dbg, inck=16)
        I("sync", lambda e: e.dma_start(out=bass.AP(d_tbl.tensor, 3 * 128 * S, [[S, 128], [1, T]]), in_=sinQ[:]), inc=s_dbg, inck=16)
        I("sync", lambda e: e.nop(), waits=[s_dbg])
        I("vector", lambda e: e.tensor_copy(scr[:, 2:3], ident[:, 0:1]), waits=[B.sems["sv"]], inc=B.sems["sv"])

    B.flush_block()
    es_f.close()
    es_moe.close()
    return nc, es, dbg


from concourse.bass_utils import run_bass_kernel_spmd
from concourse.library_overlay import lower_extended_insts

_PQ = perm_q_rows()
_PK = perm_k_rows()
_PA = perm_attn_rows()


def kernel(L0_attn_x, L0_attn_W_q, L0_attn_b_q, L0_attn_W_kv, L0_attn_b_kv,
           L0_attn_W_out, L0_attn_b_out, L0_ffn_gate_w, L0_ffn_w_gate,
           L0_ffn_w_up, L0_ffn_w_down):
    x = np.ascontiguousarray(np.asarray(L0_attn_x, dtype=np.float32))
    Wq = np.asarray(L0_attn_W_q, dtype=np.float32)[_PQ]
    bq = np.asarray(L0_attn_b_q, dtype=np.float32)[_PQ]
    Wkv = np.asarray(L0_attn_W_kv, dtype=np.float32).copy()
    Wkv[:512] = np.asarray(L0_attn_W_kv, dtype=np.float32)[:512][_PK]
    bkv = np.asarray(L0_attn_b_kv, dtype=np.float32).copy()
    bkv[:512] = np.asarray(L0_attn_b_kv, dtype=np.float32)[:512][_PK]
    woT = np.ascontiguousarray(np.asarray(L0_attn_W_out, dtype=np.float32).T[_PA])
    wqT = np.ascontiguousarray(Wq.T)
    wkvT = np.ascontiguousarray(Wkv.T)
    bq_t = np.ascontiguousarray(bq.reshape(16, 128).T)
    bkv_t = np.ascontiguousarray(bkv.reshape(8, 128).T)
    bo_t = np.ascontiguousarray(np.asarray(L0_attn_b_out, dtype=np.float32).reshape(16, 128).T)
    gw = np.ascontiguousarray(np.asarray(L0_ffn_gate_w, dtype=np.float32))
    wg = np.ascontiguousarray(np.asarray(L0_ffn_w_gate, dtype=np.float32)
                              .reshape(32, 16, 128, 768).transpose(0, 2, 1, 3).reshape(32, 128, 16 * 768))
    wu = np.ascontiguousarray(np.asarray(L0_ffn_w_up, dtype=np.float32)
                              .reshape(32, 16, 128, 768).transpose(0, 2, 1, 3).reshape(32, 128, 16 * 768))
    wd = np.ascontiguousarray(np.asarray(L0_ffn_w_down, dtype=np.float32)
                              .reshape(32, 6, 128, 2048).transpose(0, 2, 1, 3).reshape(32, 128, 6 * 2048))

    xTs = [np.ascontiguousarray(x[b].T) for b in range(4)]
    in_maps = []
    for c in range(8):
        b, h = c // 2, c % 2
        in_maps.append({
            "xT": xTs[b],
            "xqT": np.ascontiguousarray(xTs[b][:, h * T:(h + 1) * T]),
            "wq": wqT, "wkv": wkvT, "wo": woT,
            "bq": bq_t, "bkv": bkv_t, "bo": bo_t,
            "gw": gw, "wg": wg, "wu": wu, "wd": wd,
            "pos0": np.array([[float(h * T)]], np.float32),
        })

    nc, es, dbg = build()
    with es:
        lower_extended_insts(nc)
        res = run_bass_kernel_spmd(nc, in_maps, core_ids=list(range(8)))
    out = np.empty((4, S, D), np.float32)
    for c in range(8):
        b, h = c // 2, c % 2
        out[b, h * T:(h + 1) * T, :] = res.results[c]["out"]
    return out
